# version 02
# baseline (speedup 1.0000x reference)
from contextlib import ExitStack
import numpy as np
import ml_dtypes
import concourse.bass as bass
import concourse.mybir as mybir
from concourse.bass_utils import run_bass_kernel_spmd

F32 = mybir.dt.float32
BF16 = mybir.dt.bfloat16
AF = mybir.ActivationFunctionType
ALU = mybir.AluOpType
f = np.float32
BF = ml_dtypes.bfloat16

NCORE = 8
TOK = 16384
TPC = TOK // NCORE
D = 1024
SEQ = 8192

ENGINES = ("tensor", "vector", "scalar", "gpsimd", "sync")
DMA_Q = {"sync": "dma_sync", "gpsimd": "dma_gp", "scalar": "dma_act", "vector": "dma_dve"}


class Prog:

    def __init__(self):
        self.vins = []
        self.deps = []
        self.last_w = {}
        self.readers = {}

    def _record(self, veng, issue_eng, emitter, reads, writes, partial=False):
        vidx = len(self.vins)
        dep = set()
        for k in reads:
            for w in self.last_w.get(k, ()):
                dep.add(w)
        for k in writes:
            for w in self.last_w.get(k, ()):
                dep.add(w)
            for r in self.readers.get(k, ()):
                dep.add(r)
        self.vins.append((veng, issue_eng, emitter))
        self.deps.append(dep)
        for k in reads:
            self.readers.setdefault(k, []).append(vidx)
        for k in writes:
            if partial:
                self.last_w.setdefault(k, []).append(vidx)
            else:
                self.last_w[k] = [vidx]
            self.readers[k] = []
        return vidx

    def op(self, eng, emitter, reads=(), writes=()):
        return self._record(eng, eng, emitter, reads, writes)

    def dma(self, queue, out_ap, in_ap, reads=(), writes=(), partial=False, **kw):
        return self._record(
            DMA_Q[queue], queue,
            (lambda e: e.dma_start(out=out_ap, in_=in_ap, **kw)),
            reads, writes, partial=partial)

    SERIAL = False

    def emit(self, nc):
        n = len(self.vins)
        if Prog.SERIAL:
            for i in range(1, n):
                self.deps[i].add(i - 1)
        targets = set()
        for i in range(n):
            si = self.vins[i][0]
            for d in self.deps[i]:
                if self.vins[d][0] != si:
                    targets.add(d)
        for i in range(n):
            if self.vins[i][0].startswith("dma_"):
                targets.add(i)
        rank = {}
        counts = {}
        for i in range(n):
            if i in targets:
                ve = self.vins[i][0]
                counts[ve] = counts.get(ve, 0) + 1
                rank[i] = counts[ve]

        sem_cms = {ve: nc.semaphore(f"s_{ve}") for ve in counts}
        sems = {ve: cm.__enter__() for ve, cm in sem_cms.items()}

        per_eng = {e: [] for e in ENGINES}
        for i in range(n):
            per_eng[self.vins[i][1]].append(i)

        def emit_engine(e, eng_name):
            done = {}
            for i in per_eng[eng_name]:
                ve = self.vins[i][0]
                need = {}
                for d in self.deps[i]:
                    dve = self.vins[d][0]
                    if dve == ve:
                        continue
                    v = rank[d] * (16 if dve.startswith("dma_") else 1)
                    need[dve] = max(need.get(dve, 0), v)
                for dve in sorted(need):
                    v = need[dve]
                    if done.get(dve, 0) < v:
                        e.wait_ge(sems[dve], v)
                        done[dve] = v
                ins = self.vins[i][2](e)
                if i in targets:
                    ins.then_inc(sems[ve], 16 if ve.startswith("dma_") else 1)
                    if ve.startswith("dma_"):
                        e.wait_ge(sems[ve], rank[i] * 16)
                        done[ve] = max(done.get(ve, 0), rank[i] * 16)
            if eng_name == "sync":
                for ve, c in sorted(counts.items()):
                    if ve.startswith("dma_"):
                        e.wait_ge(sems[ve], c * 16)

        block_cm = nc.Block()
        block = block_cm.__enter__()

        @block.sync
        def _(e):
            emit_engine(e, "sync")

        @block.tensor
        def _(e):
            emit_engine(e, "tensor")

        @block.vector
        def _(e):
            emit_engine(e, "vector")

        @block.scalar
        def _(e):
            emit_engine(e, "scalar")

        @block.gpsimd
        def _(e):
            emit_engine(e, "gpsimd")

        block_cm.__exit__(None, None, None)
        for cm in sem_cms.values():
            cm.__exit__(None, None, None)


def rearr(ap, pat, **kw):
    return ap.rearrange(pat, **kw)



def emit_weight_load(p, w_dram, w_stage, w_bf16, kchunks, dout, key,
                     queue="gpsimd", stage_key="wstage", stage_cols=4096):
    per = max(1, stage_cols // dout)
    for k0 in range(0, kchunks, per):
        k1 = min(k0 + per, kchunks)
        nk = k1 - k0
        p.dma(queue,
              rearr(w_stage[:, 0:nk * dout], "p (k d) -> p k d", k=nk),
              rearr(w_dram[k0 * 128:k1 * 128, :], "(k p) d -> p k d", p=128),
              reads=(), writes=(stage_key,))
        p.op("vector", lambda e, k0=k0, nk=nk: e.tensor_copy(
            w_bf16[:, k0 * dout:(k0 + nk) * dout], w_stage[:, 0:nk * dout]),
            reads=(stage_key,), writes=(key,))


def emit_ln(p, x3, KC, T, ones, onesrow, sq, stats, bcast, psS, psS2, tag, out_keys,
            gamma=None, beta=None, gb_key="gb", out3=None, out_keys2=None, eps=1e-5):
    D_ = KC * 128
    segs = [(a, min(a + 512, T)) for a in range(0, T, 512)]
    for k in range(KC):
        for (a, b) in segs:
            p.op("tensor", lambda e, k=k, a=a, b=b: e.matmul(
                psS[0:1, a:b], ones[:, 0:1], x3[:, k, a:b], start=(k == 0), stop=(k == KC - 1)),
                reads=("ones", out_keys[k]), writes=(tag + "psS", "psS"))
    p.op("vector", lambda e: e.tensor_scalar(stats[0:1, 0:T], psS[0:1, 0:T],
                                             1.0 / D_, None, ALU.mult),
         reads=(tag + "psS", "psS", "statsbuf"), writes=(tag + "mean", "statsbuf"))
    for (a, b) in segs:
        p.op("tensor", lambda e, a=a, b=b: e.matmul(psS[:, a:b], onesrow[0:1, :], stats[0:1, a:b],
                                                    start=True, stop=True),
             reads=("ones", tag + "mean"), writes=(tag + "psS", "psS"))
    p.op("vector", lambda e: e.tensor_copy(bcast[:, 0:T], psS[:, 0:T]),
         reads=(tag + "psS", "psS", "bcastbuf"), writes=(tag + "bmean", "bcastbuf"))
    for k in range(KC):
        p.op("vector", lambda e, k=k: e.tensor_tensor(
            x3[:, k, :], x3[:, k, :], bcast[:, 0:T], ALU.subtract),
            reads=(out_keys[k], tag + "bmean"), writes=(out_keys[k],))
    for k in range(KC):
        p.op("scalar", lambda e, k=k: e.activation(sq[:, 0:T], x3[:, k, :], AF.Square),
             reads=(out_keys[k], "sqbuf"), writes=(tag + "sq", "sqbuf"))
        for (a, b) in segs:
            p.op("tensor", lambda e, k=k, a=a, b=b: e.matmul(
                psS2[0:1, a:b], ones[:, 0:1], sq[:, a:b], start=(k == 0), stop=(k == KC - 1)),
                reads=("ones", tag + "sq"), writes=(tag + "psS2", "psS2"))
    p.op("scalar", lambda e: e.activation(stats[0:1, T:2 * T], psS2[0:1, 0:T],
                                          AF.Ln, bias=f(eps), scale=f(1.0 / D_)),
         reads=(tag + "psS2", "psS2", "statsbuf"), writes=(tag + "lnv", "statsbuf"))
    p.op("scalar", lambda e: e.activation(stats[0:1, 2 * T:3 * T], stats[0:1, T:2 * T],
                                          AF.Exp, scale=f(-0.5)),
         reads=(tag + "lnv", "statsbuf"), writes=(tag + "rstd", "statsbuf"))
    for (a, b) in segs:
        p.op("tensor", lambda e, a=a, b=b: e.matmul(psS2[:, a:b], onesrow[0:1, :],
                                                    stats[0:1, 2 * T + a:2 * T + b],
                                                    start=True, stop=True),
             reads=("ones", tag + "rstd"), writes=(tag + "psS2", "psS2"))
    p.op("vector", lambda e: e.tensor_copy(bcast[:, T:2 * T], psS2[:, 0:T]),
         reads=(tag + "psS2", "psS2", "bcastbuf"), writes=(tag + "brstd", "bcastbuf"))
    for k in range(KC):
        p.op("vector", lambda e, k=k: e.tensor_tensor(
            x3[:, k, :], x3[:, k, :], bcast[:, T:2 * T], ALU.mult),
            reads=(out_keys[k], tag + "brstd"), writes=(out_keys[k],))
        if gamma is not None:
            p.op("vector", lambda e, k=k: e.tensor_scalar(
                out3[:, k, :], x3[:, k, :], gamma[:, k:k + 1], beta[:, k:k + 1],
                ALU.mult, ALU.add),
                reads=(out_keys[k], gb_key), writes=(out_keys2[k],))



def mk_l0ffn(T=TPC, TSUB=512):
    KC = D // 128
    nsub = T // TSUB
    nc = bass.Bass("TRN2", target_bir_lowering=False, debug=False)
    r1 = nc.dram_tensor("r1", [D, 1 + T], F32, kind="ExternalInput").ap()
    WrT = nc.dram_tensor("WrT", [D, D], F32, kind="ExternalInput").ap()
    WkT = nc.dram_tensor("WkT", [D, D], F32, kind="ExternalInput").ap()
    WvT = nc.dram_tensor("WvT", [D, D], F32, kind="ExternalInput").ap()
    mur = nc.dram_tensor("mur", [D, 1], F32, kind="ExternalInput").ap()
    muk = nc.dram_tensor("muk", [D, 1], F32, kind="ExternalInput").ap()
    r2 = nc.dram_tensor("r2", [D, T], F32, kind="ExternalOutput").ap()

    st = ExitStack()
    xr1 = st.enter_context(nc.sbuf_tensor("xr1", [128, KC * (1 + T)], F32))
    wst = st.enter_context(nc.sbuf_tensor("wst", [128, 4096], F32))
    wr = st.enter_context(nc.sbuf_tensor("wr", [128, KC * D], BF16))
    wk = st.enter_context(nc.sbuf_tensor("wk", [128, KC * D], BF16))
    wv = st.enter_context(nc.sbuf_tensor("wv", [128, KC * D], BF16))
    smu = st.enter_context(nc.sbuf_tensor("smu", [128, 2 * KC], F32))
    xrb = st.enter_context(nc.sbuf_tensor("xrb", [128, KC * TSUB], BF16))
    xkb = st.enter_context(nc.sbuf_tensor("xkb", [128, KC * TSUB], BF16))
    dwork = st.enter_context(nc.sbuf_tensor("dwork", [128, TSUB], F32))
    rsig = st.enter_context(nc.sbuf_tensor("rsig", [128, KC * TSUB], F32))
    kact = st.enter_context(nc.sbuf_tensor("kact", [128, KC * TSUB], BF16))
    rout = st.enter_context(nc.sbuf_tensor("rout", [128, KC * TSUB], F32))
    ones = st.enter_context(nc.sbuf_tensor("ones", [128, 1], F32))
    onesrow = st.enter_context(nc.sbuf_tensor("onesrow", [1, 128], F32))
    sq = st.enter_context(nc.sbuf_tensor("sq", [128, TSUB], F32))
    stats = st.enter_context(nc.sbuf_tensor("stats", [1, 4 * TSUB], F32))
    bcast = st.enter_context(nc.sbuf_tensor("bcast", [128, 2 * TSUB], F32))
    psA = st.enter_context(nc.psum_tensor("psA", [128, 512], F32))
    psB = st.enter_context(nc.psum_tensor("psB", [128, 512], F32))
    psC = st.enter_context(nc.psum_tensor("psC", [128, 512], F32))
    psS = st.enter_context(nc.psum_tensor("psS", [128, 512], F32))
    psS2 = st.enter_context(nc.psum_tensor("psS2", [128, 512], F32))
    if True:
        p = Prog()
        x1 = rearr(xr1[:], "p (k t) -> p k t", k=KC)

        p.dma("sync", x1, rearr(r1, "(k p) t -> p k t", p=128), writes=("r1",))
        p.dma("sync", smu[:, 0:KC].unsqueeze(2),
              rearr(mur, "(k p) o -> p k o", p=128), writes=("mu",),
              allow_slow_non_contiguous=True)
        p.dma("sync", smu[:, KC:2 * KC].unsqueeze(2),
              rearr(muk, "(k p) o -> p k o", p=128), writes=("mu",),
              allow_slow_non_contiguous=True)
        emit_weight_load(p, WrT, wst, wr, KC, D, "wr")
        emit_weight_load(p, WkT, wst, wk, KC, D, "wk")
        emit_weight_load(p, WvT, wst, wv, KC, D, "wv")
        p.op("gpsimd", lambda e: e.memset(ones[:], 1.0), writes=("ones",))
        p.op("gpsimd", lambda e: e.memset(onesrow[:], 1.0), writes=("ones",))

        xrb3 = rearr(xrb[:], "p (k t) -> p k t", k=KC)
        xkb3 = rearr(xkb[:], "p (k t) -> p k t", k=KC)
        rsig3 = rearr(rsig[:], "p (k t) -> p k t", k=KC)
        kact3 = rearr(kact[:], "p (k t) -> p k t", k=KC)
        rout3 = rearr(rout[:], "p (k t) -> p k t", k=KC)

        for s in range(nsub):
            base = 1 + s * TSUB
            for k in range(KC):
                cur = x1[:, k, base:base + TSUB]
                prv = x1[:, k, base - 1:base - 1 + TSUB]
                p.op("vector", lambda e, cur=cur, prv=prv: e.tensor_tensor(
                    dwork[:], cur, prv, ALU.subtract), reads=("r1",), writes=("dw",))
                p.op("vector", lambda e, k=k, prv=prv: e.scalar_tensor_tensor(
                    xrb3[:, k, :], dwork[:], smu[:, k:k + 1], prv, ALU.mult, ALU.add),
                    reads=("dw", "mu", "r1"), writes=(f"xrb{k}",))
                p.op("vector", lambda e, k=k, prv=prv: e.scalar_tensor_tensor(
                    xkb3[:, k, :], dwork[:], smu[:, KC + k:KC + k + 1], prv, ALU.mult, ALU.add),
                    reads=("dw", "mu", "r1"), writes=(f"xkb{k}",))
            for m in range(KC):
                for k in range(KC):
                    p.op("tensor", lambda e, m=m, k=k: e.matmul(
                        psA[:, 0:TSUB], wr[:, k * D + m * 128:k * D + (m + 1) * 128],
                        xrb3[:, k, :], start=(k == 0), stop=(k == KC - 1)),
                        reads=("wr", f"xrb{k}"), writes=("psA",))
                p.op("scalar", lambda e, m=m: e.activation(
                    rsig3[:, m, :], psA[:, 0:TSUB], AF.Sigmoid),
                    reads=("psA",), writes=(f"rsig{m}",))
                for k in range(KC):
                    p.op("tensor", lambda e, m=m, k=k: e.matmul(
                        psB[:, 0:TSUB], wk[:, k * D + m * 128:k * D + (m + 1) * 128],
                        xkb3[:, k, :], start=(k == 0), stop=(k == KC - 1)),
                        reads=("wk", f"xkb{k}"), writes=("psB",))
                p.op("vector", lambda e, m=m: e.tensor_scalar(
                    sq[:], psB[:, 0:TSUB], 0.0, None, ALU.max),
                    reads=("psB", "sqbuf"), writes=("sqv", "sqbuf"))
                p.op("scalar", lambda e, m=m: e.activation(
                    kact3[:, m, :], sq[:], AF.Square),
                    reads=("sqv",), writes=(f"kact{m}",))
            for m in range(KC):
                for k in range(KC):
                    p.op("tensor", lambda e, m=m, k=k: e.matmul(
                        psC[:, 0:TSUB], wv[:, k * D + m * 128:k * D + (m + 1) * 128],
                        kact3[:, k, :], start=(k == 0), stop=(k == KC - 1)),
                        reads=("wv", f"kact{k}"), writes=("psC",))
                p.op("vector", lambda e, m=m: e.tensor_tensor(
                    rout3[:, m, :], rsig3[:, m, :], psC[:, 0:TSUB], ALU.mult),
                    reads=("psC", f"rsig{m}"), writes=(f"rout{m}",))
                p.op("vector", lambda e, m=m, base=base: e.tensor_tensor(
                    rout3[:, m, :], rout3[:, m, :], x1[:, m, base:base + TSUB], ALU.add),
                    reads=("r1", f"rout{m}"), writes=(f"rout{m}",))
            emit_ln(p, rout3, KC, TSUB, ones, onesrow, sq, stats, bcast, psS, psS2,
                    tag=f"ln{s}", out_keys=[f"rout{m}" for m in range(KC)])
            p.dma("sync", rearr(r2[:, s * TSUB:(s + 1) * TSUB], "(k p) t -> p k t", p=128),
                  rout3, reads=tuple(f"rout{m}" for m in range(KC)), writes=())
        p.emit(nc)
    st.close()
    return nc



_cache = {}


def run_l0ffn(r1T, WrT, WkT, WvT, mu_r, mu_k):
    if "l0ffn" not in _cache:
        _cache["l0ffn"] = mk_l0ffn()
    nc = _cache["l0ffn"]
    halo = np.concatenate([np.zeros((D, 1), np.float32), r1T], axis=1)
    in_maps = []
    for c in range(NCORE):
        t0 = c * TPC
        chunk = np.ascontiguousarray(halo[:, t0:t0 + TPC + 1])
        if c % 4 == 0:
            chunk[:, 0] = 0.0
        in_maps.append(dict(
            r1=chunk, WrT=WrT, WkT=WkT, WvT=WvT,
            mur=np.ascontiguousarray(mu_r.reshape(D, 1)),
            muk=np.ascontiguousarray(mu_k.reshape(D, 1))))
    res = run_bass_kernel_spmd(nc, in_maps, core_ids=list(range(NCORE)))
    return np.concatenate([r["r2"] for r in res.results], axis=1)


def kernel(**inputs):
    inp = inputs
    x = inp["L0_attn_x"]
    xT = np.ascontiguousarray(x.reshape(16384, 1024).T)
    ao = run_l0attn(xT, inp)
    r1T = run_l0post(xT, ao, inp["L0_attn_W_O"])
    r2T = run_l0ffn(r1T,
                    np.ascontiguousarray(inp["L0_ffn_W_r"].T),
                    np.ascontiguousarray(inp["L0_ffn_W_k"].T),
                    np.ascontiguousarray(inp["L0_ffn_W_v"].T),
                    inp["L0_ffn_mu_r"], inp["L0_ffn_mu_k"])
    ssm, dt, zz, Bn, Cn = run_mambapre(r2T, inp)
    y = run_scan(ssm, dt, Bn, Cn, inp["L1_attn_A_log"], inp["L1_attn_D_param"])
    r3T = run_mambapost(y, zz, r2T, inp["L1_attn_W_out"])
    r4T = run_l1ffn(r3T, inp, layer=0)
    outT = run_l1ffn(r4T, inp, layer=1, r3T=r3T)
    return np.ascontiguousarray(outT.T).reshape(2, 8192, 1024).astype(np.float32)



def mk_mambapre(T=TPC, TSUB=512):
    KC = D // 128
    MC = 2048 // 128
    nsub = T // TSUB
    nc = bass.Bass("TRN2", target_bir_lowering=False, debug=False)
    r2 = nc.dram_tensor("r2", [D, 3 + T], F32, kind="ExternalInput").ap()
    WinT = nc.dram_tensor("WinT", [D, 4096], F32, kind="ExternalInput").ap()
    gam = nc.dram_tensor("gam", [D, 1], F32, kind="ExternalInput").ap()
    bet = nc.dram_tensor("bet", [D, 1], F32, kind="ExternalInput").ap()
    cw = nc.dram_tensor("cw", [2048, 4], F32, kind="ExternalInput").ap()
    cb = nc.dram_tensor("cb", [2048, 1], F32, kind="ExternalInput").ap()
    WxT = nc.dram_tensor("WxT", [2048, 33], F32, kind="ExternalInput").ap()
    WdtT = nc.dram_tensor("WdtT", [33, 2048], F32, kind="ExternalInput").ap()
    bdt = nc.dram_tensor("bdt", [2048, 1], F32, kind="ExternalInput").ap()
    hm = nc.dram_tensor("hm", [1, 3], F32, kind="ExternalInput").ap()
    ssm_o = nc.dram_tensor("ssm_o", [2048, T], F32, kind="ExternalOutput").ap()
    dt_o = nc.dram_tensor("dt_o", [2048, T], F32, kind="ExternalOutput").ap()
    z_o = nc.dram_tensor("z_o", [2048, T], F32, kind="ExternalOutput").ap()
    B_o = nc.dram_tensor("B_o", [16, T], F32, kind="ExternalOutput").ap()
    C_o = nc.dram_tensor("C_o", [16, T], F32, kind="ExternalOutput").ap()

    st = ExitStack()
    r2c = st.enter_context(nc.sbuf_tensor("r2c", [128, KC * (TSUB + 3)], F32))
    xnb = st.enter_context(nc.sbuf_tensor("xnb", [128, KC * (TSUB + 3)], BF16))
    wst = st.enter_context(nc.sbuf_tensor("wst", [128, 4096], F32))
    wib = st.enter_context(nc.sbuf_tensor("wib", [128, 2, KC * 512], BF16))
    sgb = st.enter_context(nc.sbuf_tensor("sgb", [128, 2 * KC], F32))
    scw = st.enter_context(nc.sbuf_tensor("scw", [128, MC * 4], F32))
    scb = st.enter_context(nc.sbuf_tensor("scb", [128, MC], F32))
    sbdt = st.enter_context(nc.sbuf_tensor("sbdt", [128, MC], F32))
    swx = st.enter_context(nc.sbuf_tensor("swx", [128, MC * 33], BF16))
    swdt = st.enter_context(nc.sbuf_tensor("swdt", [33, 2048], BF16))
    shm = st.enter_context(nc.sbuf_tensor("shm", [128, 3], F32))
    xsp = st.enter_context(nc.sbuf_tensor("xsp", [128, MC * (TSUB + 3)], F32))
    ssm = st.enter_context(nc.sbuf_tensor("ssm", [128, MC * TSUB], F32))
    ssmb = st.enter_context(nc.sbuf_tensor("ssmb", [128, MC * TSUB], BF16))
    zt = st.enter_context(nc.sbuf_tensor("zt", [128, 2, TSUB], F32))
    dtt = st.enter_context(nc.sbuf_tensor("dtt", [128, 2, TSUB], F32))
    dtw = st.enter_context(nc.sbuf_tensor("dtw", [128, 3, TSUB], F32))
    xp = st.enter_context(nc.sbuf_tensor("xp", [64, TSUB], F32))
    xpb = st.enter_context(nc.sbuf_tensor("xpb", [64, TSUB], BF16))
    ones = st.enter_context(nc.sbuf_tensor("ones", [128, 1], F32))
    onesrow = st.enter_context(nc.sbuf_tensor("onesrow", [1, 128], F32))
    sq = st.enter_context(nc.sbuf_tensor("sq", [128, TSUB + 3], F32))
    stats = st.enter_context(nc.sbuf_tensor("stats", [1, 4 * (TSUB + 3)], F32))
    bcast = st.enter_context(nc.sbuf_tensor("bcast", [128, 2 * (TSUB + 3)], F32))
    psA = st.enter_context(nc.psum_tensor("psA", [128, 512], F32))
    psB = st.enter_context(nc.psum_tensor("psB", [128, 512], F32))
    psS = st.enter_context(nc.psum_tensor("psS", [128, 512], F32))
    psS2 = st.enter_context(nc.psum_tensor("psS2", [128, 512], F32))
    if True:
        p = Prog()
        r2c3 = rearr(r2c[:], "p (k t) -> p k t", k=KC)
        xnb3 = rearr(xnb[:], "p (k t) -> p k t", k=KC)
        xsp3 = rearr(xsp[:], "p (k t) -> p k t", k=MC)
        ssm3 = rearr(ssm[:], "p (k t) -> p k t", k=MC)
        ssmb3 = rearr(ssmb[:], "p (k t) -> p k t", k=MC)

        p.dma("sync", sgb[:, 0:KC].unsqueeze(2), rearr(gam, "(k p) o -> p k o", p=128),
              writes=("gb",), allow_slow_non_contiguous=True)
        p.dma("sync", sgb[:, KC:2 * KC].unsqueeze(2), rearr(bet, "(k p) o -> p k o", p=128),
              writes=("gb",), allow_slow_non_contiguous=True)
        p.dma("sync", rearr(scw[:], "p (k j) -> p k j", k=MC), rearr(cw, "(k p) j -> p k j", p=128),
              writes=("cw",))
        p.dma("sync", scb[:].unsqueeze(2), rearr(cb, "(k p) o -> p k o", p=128),
              writes=("cw",), allow_slow_non_contiguous=True)
        p.dma("sync", sbdt[:].unsqueeze(2), rearr(bdt, "(k p) o -> p k o", p=128),
              writes=("cw",), allow_slow_non_contiguous=True)
        p.dma("sync", shm[:], hm.partition_broadcast(128), writes=("hm",))
        p.op("gpsimd", lambda e: e.memset(ones[:], 1.0), writes=("ones",))
        p.op("gpsimd", lambda e: e.memset(onesrow[:], 1.0), writes=("ones",))
        emit_weight_load(p, WxT, wst, swx, MC, 33, "wx")
        p.dma("sync", wst[0:33, 0:2048], WdtT, writes=("wstage",))
        p.op("vector", lambda e: e.tensor_copy(swdt[:], wst[0:33, 0:2048]),
             reads=("wstage",), writes=("wdt",))

        for s in range(nsub):
            par = s % 2
            p.dma("sync", r2c3, rearr(r2[:, s * TSUB:s * TSUB + TSUB + 3], "(k p) t -> p k t", p=128),
                  reads=("r2c_free",), writes=("r2c",))
            emit_ln(p, r2c3[:, :, 0:3], KC, 3, ones, onesrow, sq, stats, bcast, psS, psS2,
                    tag=f"mlnh{s}", out_keys=["r2c"] * KC,
                    gamma=sgb[:, 0:KC], beta=sgb[:, KC:2 * KC], gb_key="gb",
                    out3=xnb3[:, :, 0:3], out_keys2=[f"xnb{k}" for k in range(KC)])
            emit_ln(p, r2c3[:, :, 3:3 + TSUB], KC, TSUB, ones, onesrow, sq, stats, bcast, psS, psS2,
                    tag=f"mln{s}", out_keys=["r2c"] * KC,
                    gamma=sgb[:, 0:KC], beta=sgb[:, KC:2 * KC], gb_key="gb",
                    out3=xnb3[:, :, 3:3 + TSUB], out_keys2=[f"xnb{k}" for k in range(KC)])
            p.op("vector", lambda e: e.tensor_copy(sq[:, 0:1], r2c3[:, 0, 0:1]),
                 reads=("r2c", "sqbuf") + tuple(f"xnb{k}" for k in range(KC)),
                 writes=("r2c_free", "sqbuf"))
            for mg in range(8):
                gpar = mg % 2
                p.dma("gpsimd", rearr(wst[:], "p (k d) -> p k d", k=KC),
                      rearr(WinT[:, mg * 512:(mg + 1) * 512], "(k p) d -> p k d", p=128),
                      reads=(f"wib{gpar}",), writes=("wstage",)) if s == 0 else None
                if s == 0:
                    p.op("vector", lambda e, gpar=gpar: e.tensor_copy(
                        wib[:, gpar, :], wst[:]), reads=("wstage",), writes=(f"wib{gpar}",))
                if s > 0:
                    p.dma("gpsimd", rearr(wst[:], "p (k d) -> p k d", k=KC),
                          rearr(WinT[:, mg * 512:(mg + 1) * 512], "(k p) d -> p k d", p=128),
                          reads=(f"wib{gpar}",), writes=("wstage",))
                    p.op("vector", lambda e, gpar=gpar: e.tensor_copy(
                        wib[:, gpar, :], wst[:]), reads=("wstage",), writes=(f"wib{gpar}",))
                for mi in range(4):
                    m = mg * 4 + mi
                    for k in range(KC):
                        p.op("tensor", lambda e, gpar=gpar, mi=mi, k=k: e.matmul(
                            psA[:, 0:TSUB],
                            wib[:, gpar, k * 512 + mi * 128:k * 512 + (mi + 1) * 128],
                            xnb3[:, k, 3:3 + TSUB], start=(k == 0), stop=(k == KC - 1)),
                            reads=(f"wib{gpar}",) + tuple(f"xnb{k}" for k in range(KC)),
                            writes=("psA",))
                    if m < MC:
                        if s == 0:
                            for k in range(KC):
                                p.op("tensor", lambda e, gpar=gpar, mi=mi, k=k: e.matmul(
                                    psB[:, 0:3],
                                    wib[:, gpar, k * 512 + mi * 128:k * 512 + (mi + 1) * 128],
                                    xnb3[:, k, 0:3], start=(k == 0), stop=(k == KC - 1)),
                                    reads=(f"wib{gpar}",) + tuple(f"xnb{k}" for k in range(KC)),
                                    writes=("psB",))
                            p.op("vector", lambda e, m=m: e.tensor_tensor(
                                xsp3[:, m, 0:3], psB[:, 0:3], shm[:], ALU.mult),
                                reads=("psB", "hm"), writes=(f"xsp{m}",))
                        p.op("vector", lambda e, m=m: e.tensor_copy(
                            xsp3[:, m, 3:3 + TSUB], psA[:, 0:TSUB]),
                            reads=("psA",), writes=(f"xsp{m}",))
                    else:
                        zpar = m % 2
                        p.op("vector", lambda e, zpar=zpar: e.tensor_copy(
                            zt[:, zpar, :], psA[:, 0:TSUB]),
                            reads=("psA", f"zt{zpar}"), writes=(f"zt{zpar}",))
                        p.dma("gpsimd", rearr(z_o[(m - MC) * 128:(m - MC + 1) * 128,
                                              s * TSUB:(s + 1) * TSUB], "p t -> p t"),
                              zt[:, zpar, :], reads=(f"zt{zpar}",), writes=())
            for m in range(MC):
                p.op("vector", lambda e, m=m: e.tensor_scalar(
                    sq[:, 0:TSUB], xsp3[:, m, 0:TSUB], scw[:, m * 4:m * 4 + 1], None, ALU.mult),
                    reads=(f"xsp{m}", "cw", "sqbuf"), writes=("cacc", "sqbuf"))
                for j in (1, 2, 3):
                    p.op("vector", lambda e, m=m, j=j: e.scalar_tensor_tensor(
                        sq[:, 0:TSUB], xsp3[:, m, j:j + TSUB], scw[:, m * 4 + j:m * 4 + j + 1],
                        sq[:, 0:TSUB], ALU.mult, ALU.add),
                        reads=(f"xsp{m}", "cw", "cacc", "sqbuf"), writes=("cacc", "sqbuf"))
                p.op("scalar", lambda e, m=m: e.activation(
                    ssm3[:, m, :], sq[:, 0:TSUB], AF.Silu, bias=scb[:, m:m + 1]),
                    reads=("cacc", "cw"), writes=(f"ssm{m}",))
                p.op("vector", lambda e, m=m: e.tensor_copy(ssmb3[:, m, :], ssm3[:, m, :]),
                     reads=(f"ssm{m}",), writes=(f"ssmb{m}",))
                p.dma("sync", ssm_o[m * 128:(m + 1) * 128, s * TSUB:(s + 1) * TSUB],
                      ssm3[:, m, :], reads=(f"ssm{m}",), writes=())
                p.op("vector", lambda e, m=m: e.tensor_copy(
                    xsp3[:, m, 0:3], xsp3[:, m, TSUB:TSUB + 3]),
                    reads=(f"xsp{m}",), writes=(f"xsp{m}",))
            for k in range(MC):
                p.op("tensor", lambda e, k=k: e.matmul(
                    psB[0:33, 0:TSUB], swx[:, k * 33:(k + 1) * 33], ssmb3[:, k, :],
                    start=(k == 0), stop=(k == MC - 1)),
                    reads=("wx", f"ssmb{k}"), writes=("psB",))
            p.op("vector", lambda e: e.tensor_copy(xp[0:33, :], psB[0:33, 0:TSUB]),
                 reads=("psB",), writes=("xp",))
            p.op("vector", lambda e: e.tensor_copy(xpb[0:33, :], xp[0:33, :]),
                 reads=("xp",), writes=("xpb",))
            p.dma("sync", B_o[:, s * TSUB:(s + 1) * TSUB], xp[1:17, :], reads=("xp",), writes=())
            p.dma("sync", C_o[:, s * TSUB:(s + 1) * TSUB], xp[17:33, :], reads=("xp",), writes=())
            for m in range(MC):
                p.op("tensor", lambda e, m=m: e.matmul(
                    psA[:, 0:TSUB], swdt[0:33, m * 128:(m + 1) * 128], xpb[0:33, :],
                    start=True, stop=True),
                    reads=("wdt", "xpb"), writes=("psA",))
                p.op("vector", lambda e, m=m: e.tensor_scalar(
                    dtw[:, 0, :], psA[:, 0:TSUB], sbdt[:, m:m + 1], None, ALU.add),
                    reads=("psA", "cw"), writes=("dtw0",))
                p.op("scalar", lambda e: e.activation(dtw[:, 1, :], dtw[:, 0, :], AF.Abs),
                     reads=("dtw0",), writes=("dtw1",))
                p.op("scalar", lambda e: e.activation(dtw[:, 2, :], dtw[:, 1, :], AF.Exp, scale=f(-1.0)),
                     reads=("dtw1",), writes=("dtw2",))
                p.op("scalar", lambda e: e.activation(dtw[:, 1, :], dtw[:, 2, :], AF.Ln, bias=f(1.0)),
                     reads=("dtw2",), writes=("dtw1",))
                p.op("vector", lambda e: e.tensor_scalar(dtw[:, 2, :], dtw[:, 0, :], 0.0, None, ALU.max),
                     reads=("dtw0",), writes=("dtw2",))
                dpar = m % 2
                p.op("vector", lambda e, dpar=dpar: e.scalar_tensor_tensor(
                    dtt[:, dpar, :], dtw[:, 2, :], 0.001, dtw[:, 1, :], ALU.add, ALU.add),
                    reads=("dtw2", "dtw1", f"dtt{dpar}"), writes=(f"dtt{dpar}",))
                p.dma("gpsimd", dt_o[m * 128:(m + 1) * 128, s * TSUB:(s + 1) * TSUB],
                      dtt[:, dpar, :], reads=(f"dtt{dpar}",), writes=())
        p.emit(nc)
    st.close()
    return nc


def run_mambapre(r2T, inp):
    if "mambapre" not in _cache:
        _cache["mambapre"] = mk_mambapre()
    nc = _cache["mambapre"]
    halo = np.concatenate([np.zeros((D, 3), np.float32), r2T], axis=1)
    WinT = np.ascontiguousarray(inp["L1_attn_W_in"].T)
    WxT = np.ascontiguousarray(inp["L1_attn_W_xproj"].T)
    WdtT = np.ascontiguousarray(inp["L1_attn_W_dt"].T)
    in_maps = []
    for c in range(NCORE):
        t0 = c * TPC
        chunk = np.ascontiguousarray(halo[:, t0:t0 + TPC + 3])
        m = np.ones((1, 3), np.float32)
        if c % 4 == 0:
            chunk[:, 0:3] = 0.0
            m[:] = 0.0
        in_maps.append(dict(
            r2=chunk, WinT=WinT,
            gam=np.ascontiguousarray(inp["L1_attn_ln_gamma"].reshape(D, 1)),
            bet=np.ascontiguousarray(inp["L1_attn_ln_beta"].reshape(D, 1)),
            cw=np.ascontiguousarray(inp["L1_attn_conv_w"][:, 0, :]),
            cb=np.ascontiguousarray(inp["L1_attn_conv_b"].reshape(2048, 1)),
            WxT=WxT, WdtT=WdtT,
            bdt=np.ascontiguousarray(inp["L1_attn_b_dt"].reshape(2048, 1)),
            hm=m))
    res = run_bass_kernel_spmd(nc, in_maps, core_ids=list(range(NCORE)))
    ssm = np.concatenate([r["ssm_o"] for r in res.results], axis=1)
    dt = np.concatenate([r["dt_o"] for r in res.results], axis=1)
    zz = np.concatenate([r["z_o"] for r in res.results], axis=1)
    Bn = np.concatenate([r["B_o"] for r in res.results], axis=1)
    Cn = np.concatenate([r["C_o"] for r in res.results], axis=1)
    return ssm, dt, zz, Bn, Cn



DBG_SCAN = False


def mk_scan(L=SEQ, TP=128):
    nch = 64
    nc = bass.Bass("TRN2", target_bir_lowering=False, debug=False)
    ssm = nc.dram_tensor("ssm", [256, 2 * L], F32, kind="ExternalInput").ap()
    dts = nc.dram_tensor("dts", [256, 2 * L], F32, kind="ExternalInput").ap()
    Bn = nc.dram_tensor("Bn", [16, 2 * L], F32, kind="ExternalInput").ap()
    Cn = nc.dram_tensor("Cn", [16, 2 * L], F32, kind="ExternalInput").ap()
    Alog = nc.dram_tensor("Alog", [256, 16], F32, kind="ExternalInput").ap()
    Dp = nc.dram_tensor("Dp", [256, 1], F32, kind="ExternalInput").ap()
    y_o = nc.dram_tensor("y_o", [256, 2 * L], F32, kind="ExternalOutput").ap()
    dbg = nc.dram_tensor("dbg", [128, 7 * 64 * TP], F32, kind="ExternalOutput").ap() if DBG_SCAN else None

    from concourse.ap import AP as RawAP

    def chunk_ap(t_ap, t0, inner, TPc):
        if inner == 4:
            return RawAP(t_ap.tensor, t0, [[L, 2], [4 * 2 * L, 64], [2 * L, 4], [1, TPc]])
        else:
            return RawAP(t_ap.tensor, t0, [[0, 64], [2 * L, 16], [1, TPc]])
    Alog3 = RawAP(Alog.tensor, 0, [[0, 2], [64, 64], [1, 64]])
    Dp3 = RawAP(Dp.tensor, 0, [[0, 2], [4, 64], [1, 4]])

    st = ExitStack()
    atile = st.enter_context(nc.sbuf_tensor("atile", [128, 64], F32))
    ninva = st.enter_context(nc.sbuf_tensor("ninva", [128, 64], F32))
    dtile = st.enter_context(nc.sbuf_tensor("dtile", [128, 4], F32))
    tmp64 = st.enter_context(nc.sbuf_tensor("tmp64", [128, 64], F32))
    dtc_b = [st.enter_context(nc.sbuf_tensor(f"dtc{i}", [128, 4 * TP], F32)) for i in range(2)]
    xc_b = [st.enter_context(nc.sbuf_tensor(f"xcb{i}", [128, 4 * TP], F32)) for i in range(2)]
    brep_b = [st.enter_context(nc.sbuf_tensor(f"brep{i}", [128, 16 * TP], F32)) for i in range(2)]
    crep_b = [st.enter_context(nc.sbuf_tensor(f"crep{i}", [128, 16 * TP], F32)) for i in range(2)]
    ybuf_b = [st.enter_context(nc.sbuf_tensor(f"ybuf{i}", [128, 4 * TP], F32)) for i in range(2)]
    abuf = st.enter_context(nc.sbuf_tensor("abuf", [128, 64 * TP], F32))
    ubuf = st.enter_context(nc.sbuf_tensor("ubuf", [128, 64 * TP], F32))
    obuf = st.enter_context(nc.sbuf_tensor("obuf", [128, 64 * TP], F32))
    hist = st.enter_context(nc.sbuf_tensor("hist", [128, 64 * TP], F32))
    ysm = st.enter_context(nc.sbuf_tensor("ysm", [128, 4 * TP], F32))
    ydx = st.enter_context(nc.sbuf_tensor("ydx", [128, 4 * TP], F32))
    if True:
        p = Prog()
        p.dma("sync", atile[:], Alog3, writes=("alog",))
        p.dma("sync", dtile[:], Dp3, writes=("dtile",))
        p.op("scalar", lambda e: e.activation(atile[:], atile[:], AF.Exp),
             reads=("alog",), writes=("alog",))
        p.op("vector", lambda e: e.tensor_scalar(atile[:], atile[:], -1.0, None, ALU.mult),
             reads=("alog",), writes=("atile",))
        p.op("vector", lambda e: e.memset(hist[:, (TP - 1) * 64:TP * 64], 0.0),
             writes=("hist",))
        p.op("vector", lambda e: e.memset(obuf[:], 0.0), writes=("omd",))
        p.op("vector", lambda e: e.tensor_scalar(tmp64[:], atile[:], 1e-8, None, ALU.add),
             reads=("atile",), writes=("t64",))
        p.op("vector", lambda e: e.memset(ubuf[:], 0.0), writes=("ubuf",))
        p.op("vector", lambda e: e.reciprocal(ninva[:], tmp64[:]),
             reads=("t64",), writes=("ninva",))
        p.op("vector", lambda e: e.memset(abuf[:], 0.0), writes=("aexp",))
        p.op("vector", lambda e: e.tensor_scalar(ninva[:], ninva[:], -1.0, None, ALU.mult),
             reads=("ninva",), writes=("ninva",))
        p.op("vector", lambda e: e.memset(ysm[:], 0.0), writes=("ysm",))

        a4 = rearr(abuf[:], "p (t c n) -> p t c n", t=TP, c=4)
        u4 = rearr(ubuf[:], "p (t c n) -> p t c n", t=TP, c=4)
        o4 = rearr(obuf[:], "p (t c n) -> p t c n", t=TP, c=4)
        h4 = rearr(hist[:], "p (t c n) -> p t c n", t=TP, c=4)
        A_ap = rearr(atile[:], "p (c n) -> p c n", c=4).unsqueeze(1).broadcast_to([128, TP, 4, 16])
        NI_ap = rearr(ninva[:], "p (c n) -> p c n", c=4).unsqueeze(1).broadcast_to([128, TP, 4, 16])
        D_ap = dtile[:].unsqueeze(1).broadcast_to([128, TP, 4])

        for c in range(nch):
            par = c % 2
            t0 = c * TP
            p.dma("sync", rearr(dtc_b[par][:], "p (c t) -> p c t", c=4).opt(),
                  chunk_ap(dts, t0, 4, TP), reads=(f"dtcF{par}",), writes=(f"dtc{par}",))
            p.dma("sync", rearr(xc_b[par][:], "p (c t) -> p c t", c=4).opt(),
                  chunk_ap(ssm, t0, 4, TP), reads=(f"xcF{par}",), writes=(f"xc{par}",))
            for bb in range(2):
                p.dma("sync", rearr(brep_b[par][bb * 64:(bb + 1) * 64, :], "p (n t) -> p n t", n=16),
                      chunk_ap(Bn, bb * L + t0, 16, TP), reads=(f"brF{par}",), writes=(f"br{par}",), partial=True)
                p.dma("sync", rearr(crep_b[par][bb * 64:(bb + 1) * 64, :], "p (n t) -> p n t", n=16),
                      chunk_ap(Cn, bb * L + t0, 16, TP), reads=(f"crF{par}",), writes=(f"cr{par}",), partial=True)
            dt_ap = rearr(dtc_b[par][:], "p (c t) -> p t c", c=4).unsqueeze(3).broadcast_to([128, TP, 4, 16])
            x_ap = rearr(xc_b[par][:], "p (c t) -> p t c", c=4).unsqueeze(3).broadcast_to([128, TP, 4, 16])
            xtc_ap = rearr(xc_b[par][:], "p (c t) -> p t c", c=4)
            b_ap = rearr(brep_b[par][:], "p (n t) -> p t n", n=16).unsqueeze(2).broadcast_to([128, TP, 4, 16])
            c_ap = rearr(crep_b[par][:], "p (n t) -> p t n", n=16).unsqueeze(2).broadcast_to([128, TP, 4, 16])
            p.op("vector", lambda e, dt_ap=dt_ap: e.tensor_tensor(o4, A_ap, dt_ap, ALU.mult),
                 reads=("atile", f"dtc{par}", "omd", "och"), writes=("am", f"dtcF{par}"))
            p.op("scalar", lambda e: e.activation(abuf[:], obuf[:], AF.Exp),
                 reads=("am", "aexp"), writes=("aexp",))
            p.op("vector", lambda e, b_ap=b_ap, x_ap=x_ap: e.tensor_tensor(u4, b_ap, x_ap, ALU.mult),
                 reads=(f"br{par}", f"xc{par}", "ubuf", "uok"), writes=("ubuf", f"brF{par}"))
            p.op("vector", lambda e: e.tensor_tensor(u4, u4, NI_ap, ALU.mult),
                 reads=("ubuf", "ninva"), writes=("ubuf",))
            p.op("vector", lambda e: e.tensor_scalar(obuf[:], abuf[:], -1.0, None, ALU.add),
                 reads=("aexp", "am"), writes=("omd",))
            p.op("vector", lambda e: e.tensor_tensor(ubuf[:], ubuf[:], obuf[:], ALU.mult),
                 reads=("ubuf", "omd"), writes=("ubuf", "uok"))
            for t in range(TP):
                prev = hist[:, (t - 1) * 64:t * 64] if t > 0 else hist[:, (TP - 1) * 64:TP * 64]
                rd = ("aexp", "ubuf", "uok", "hist") if t == 0 else ()
                wr = ()
                if t == TP - 1:
                    rd = ("aexp", "ubuf", "uok")
                    wr = ("hist",)
                p.op("vector", lambda e, t=t, prev=prev: e.tensor_tensor(
                    tmp64[:], abuf[:, t * 64:(t + 1) * 64], prev, ALU.mult), reads=rd)
                p.op("vector", lambda e, t=t: e.tensor_tensor(
                    tmp64[:], tmp64[:], ubuf[:, t * 64:(t + 1) * 64], ALU.add))
                p.op("vector", lambda e, t=t: e.tensor_scalar(
                    hist[:, t * 64:(t + 1) * 64], tmp64[:], 10.0, -10.0, ALU.min, ALU.max),
                    writes=wr)
            if DBG_SCAN and c == 1:
                p.dma("sync", rearr(dbg[:, 0:64 * TP], "p f -> p f"), abuf[:], reads=("aexp",))
                p.dma("sync", rearr(dbg[:, 64 * TP:2 * 64 * TP], "p f -> p f"), ubuf[:], reads=("ubuf", "uok"))
                p.dma("sync", rearr(dbg[:, 2 * 64 * TP:3 * 64 * TP], "p f -> p f"), hist[:], reads=("hist",))
                p.dma("sync", rearr(dbg[:, 3 * 64 * TP:3 * 64 * TP + 4 * TP], "p f -> p f"),
                      dtc_b[par][:], reads=(f"dtc{par}",))
                p.dma("sync", rearr(dbg[:, 4 * 64 * TP:4 * 64 * TP + 4 * TP], "p f -> p f"),
                      xc_b[par][:], reads=(f"xc{par}",))
                p.dma("sync", rearr(dbg[:, 5 * 64 * TP:5 * 64 * TP + 16 * TP], "p f -> p f"),
                      brep_b[par][:], reads=(f"br{par}",))
                p.dma("sync", rearr(dbg[:, 6 * 64 * TP:6 * 64 * TP + 16 * TP], "p f -> p f"),
                      crep_b[par][:], reads=(f"cr{par}",))
            p.op("vector", lambda e, c_ap=c_ap: e.tensor_tensor(o4, h4, c_ap, ALU.mult),
                 reads=("hist", f"cr{par}", "omd"), writes=("och", f"crF{par}"))
            p.op("vector", lambda e: e.reduce_sum(
                ysm[:].unsqueeze(2),
                rearr(obuf[:], "p (tc n) -> p tc n", n=16),
                axis=mybir.AxisListType.X),
                reads=("och",), writes=("ysm",))
            p.op("vector", lambda e, xtc_ap=xtc_ap: e.tensor_tensor(
                rearr(ydx[:], "p (t c) -> p t c", t=TP), xtc_ap, D_ap, ALU.mult),
                reads=(f"xc{par}", "dtile", "ydx"), writes=("ydx", f"xcF{par}"))
            p.op("vector", lambda e, par=par: e.tensor_tensor(
                rearr(ybuf_b[par][:], "p (c t) -> p t c", c=4), 
                rearr(ysm[:], "p (t c) -> p t c", t=TP),
                rearr(ydx[:], "p (t c) -> p t c", t=TP), ALU.add),
                reads=("ysm", "ydx", f"ybF{par}"), writes=(f"yb{par}",))
            p.dma("gpsimd", chunk_ap(y_o, t0, 4, TP),
                  rearr(ybuf_b[par][:], "p (c t) -> p c t", c=4).opt(),
                  reads=(f"yb{par}",), writes=(f"ybF{par}",))
        p.emit(nc)
    st.close()
    return nc


def run_scan(ssm, dt, Bn, Cn, A_log, D_param):
    if "scan" not in _cache:
        _cache["scan"] = mk_scan()
    nc = _cache["scan"]
    in_maps = []
    for c in range(NCORE):
        ch0 = c * 256
        in_maps.append(dict(
            ssm=np.ascontiguousarray(ssm[ch0:ch0 + 256]),
            dts=np.ascontiguousarray(dt[ch0:ch0 + 256]),
            Bn=Bn, Cn=Cn,
            Alog=np.ascontiguousarray(A_log[ch0:ch0 + 256]),
            Dp=np.ascontiguousarray(D_param[ch0:ch0 + 256].reshape(256, 1))))
    res = run_bass_kernel_spmd(nc, in_maps, core_ids=list(range(NCORE)))
    return np.concatenate([r["y_o"] for r in res.results], axis=0)



def mk_mambapost(T=TPC, TSUB=512):
    KC = D // 128
    MC = 2048 // 128
    nsub = T // TSUB
    nc = bass.Bass("TRN2", target_bir_lowering=False, debug=False)
    yv = nc.dram_tensor("yv", [2048, T], F32, kind="ExternalInput").ap()
    zv = nc.dram_tensor("zv", [2048, T], F32, kind="ExternalInput").ap()
    r2 = nc.dram_tensor("r2", [D, T], F32, kind="ExternalInput").ap()
    WoT = nc.dram_tensor("WoT", [2048, D], F32, kind="ExternalInput").ap()
    r3 = nc.dram_tensor("r3", [D, T], F32, kind="ExternalOutput").ap()

    st = ExitStack()
    r2c = st.enter_context(nc.sbuf_tensor("r2c", [128, KC * TSUB], F32))
    yc = st.enter_context(nc.sbuf_tensor("yc", [128, MC * TSUB], F32))
    zc = st.enter_context(nc.sbuf_tensor("zc", [128, 2 * TSUB], F32))
    gb = st.enter_context(nc.sbuf_tensor("gb", [128, MC * TSUB], BF16))
    wst = st.enter_context(nc.sbuf_tensor("wst", [128, 4096], F32))
    wo = st.enter_context(nc.sbuf_tensor("wo", [128, MC * D], BF16))
    rout = st.enter_context(nc.sbuf_tensor("rout", [128, KC * TSUB], F32))
    ones = st.enter_context(nc.sbuf_tensor("ones", [128, 1], F32))
    onesrow = st.enter_context(nc.sbuf_tensor("onesrow", [1, 128], F32))
    sq = st.enter_context(nc.sbuf_tensor("sq", [128, TSUB], F32))
    stats = st.enter_context(nc.sbuf_tensor("stats", [1, 4 * TSUB], F32))
    bcast = st.enter_context(nc.sbuf_tensor("bcast", [128, 2 * TSUB], F32))
    psA = st.enter_context(nc.psum_tensor("psA", [128, 512], F32))
    psS = st.enter_context(nc.psum_tensor("psS", [128, 512], F32))
    psS2 = st.enter_context(nc.psum_tensor("psS2", [128, 512], F32))
    if True:
        p = Prog()
        r2c3 = rearr(r2c[:], "p (k t) -> p k t", k=KC)
        yc3 = rearr(yc[:], "p (k t) -> p k t", k=MC)
        gb3 = rearr(gb[:], "p (k t) -> p k t", k=MC)
        rout3 = rearr(rout[:], "p (k t) -> p k t", k=KC)
        p.op("gpsimd", lambda e: e.memset(ones[:], 1.0), writes=("ones",))
        p.op("gpsimd", lambda e: e.memset(onesrow[:], 1.0), writes=("ones",))
        emit_weight_load(p, WoT, wst, wo, MC, D, "wo")
        for s in range(nsub):
            sl = slice(s * TSUB, (s + 1) * TSUB)
            p.dma("sync", r2c3, rearr(r2[:, sl], "(k p) t -> p k t", p=128),
                  reads=("r2f",), writes=("r2c",))
            p.dma("sync", yc3, rearr(yv[:, sl], "(k p) t -> p k t", p=128),
                  reads=("ycf",), writes=("ycv",))
            for m in range(MC):
                zpar = m % 2
                p.dma("sync", zc[:, zpar * TSUB:(zpar + 1) * TSUB],
                      zv[m * 128:(m + 1) * 128, sl],
                      reads=(f"szf{zpar}",), writes=(f"sz{zpar}",))
                p.op("scalar", lambda e, zpar=zpar: e.activation(
                    sq[:], zc[:, zpar * TSUB:(zpar + 1) * TSUB], AF.Silu),
                    reads=(f"sz{zpar}", "sqbuf"), writes=("sz2", "sqbuf", f"szf{zpar}"))
                p.op("vector", lambda e, m=m: e.tensor_tensor(
                    gb3[:, m, :], yc3[:, m, :], sq[:], ALU.mult),
                    reads=("ycv", "sz2"), writes=(f"gb{m}",))
            for mo in range(KC):
                for k in range(MC):
                    p.op("tensor", lambda e, mo=mo, k=k: e.matmul(
                        psA[:, 0:TSUB], wo[:, k * D + mo * 128:k * D + (mo + 1) * 128],
                        gb3[:, k, :], start=(k == 0), stop=(k == MC - 1)),
                        reads=("wo", f"gb{k}"), writes=("psA",))
                p.op("vector", lambda e, mo=mo: e.tensor_tensor(
                    rout3[:, mo, :], psA[:, 0:TSUB], r2c3[:, mo, :], ALU.add),
                    reads=("psA", "r2c"), writes=(f"rout{mo}",))
            p.op("vector", lambda e: e.tensor_copy(sq[:, 0:1], yc3[:, 0, 0:1]),
                 reads=("ycv", "r2c", "sqbuf") + tuple(f"rout{mo}" for mo in range(KC)),
                 writes=("ycf", "r2f", "sqbuf"))
            emit_ln(p, rout3, KC, TSUB, ones, onesrow, sq, stats, bcast, psS, psS2,
                    tag=f"ln{s}", out_keys=[f"rout{mo}" for mo in range(KC)])
            p.dma("sync", rearr(r3[:, sl], "(k p) t -> p k t", p=128), rout3,
                  reads=tuple(f"rout{mo}" for mo in range(KC)), writes=())
        p.emit(nc)
    st.close()
    return nc


def run_mambapost(y, z, r2T, W_out):
    if "mambapost" not in _cache:
        _cache["mambapost"] = mk_mambapost()
    nc = _cache["mambapost"]
    WoT = np.ascontiguousarray(W_out.T)
    in_maps = []
    for c in range(NCORE):
        sl = slice(c * TPC, (c + 1) * TPC)
        in_maps.append(dict(
            yv=np.ascontiguousarray(y[:, sl]), zv=np.ascontiguousarray(z[:, sl]),
            r2=np.ascontiguousarray(r2T[:, sl]), WoT=WoT))
    res = run_bass_kernel_spmd(nc, in_maps, core_ids=list(range(NCORE)))
    return np.concatenate([r["r3"] for r in res.results], axis=1)



DBG_ATTN = False


def mk_l0attn(S=SEQ):
    KC = D // 128
    nqc = S // 512
    ntb = S // 512
    nc = bass.Bass("TRN2", target_bir_lowering=False, debug=False)
    xb = nc.dram_tensor("xb", [D, S], F32, kind="ExternalInput").ap()
    WDQ = nc.dram_tensor("WDQ", [D, 256], F32, kind="ExternalInput").ap()
    WDKV = nc.dram_tensor("WDKV", [D, 256], F32, kind="ExternalInput").ap()
    WUQ = nc.dram_tensor("WUQ", [256, 256], F32, kind="ExternalInput").ap()
    WQR = nc.dram_tensor("WQR", [256, 128], F32, kind="ExternalInput").ap()
    WUK = nc.dram_tensor("WUK", [256, 256], F32, kind="ExternalInput").ap()
    WUV = nc.dram_tensor("WUV", [256, 256], F32, kind="ExternalInput").ap()
    WKR = nc.dram_tensor("WKR", [D, 32], F32, kind="ExternalInput").ap()
    rc = nc.dram_tensor("rc", [32, S], F32, kind="ExternalInput").ap()
    rs = nc.dram_tensor("rs", [32, S], F32, kind="ExternalInput").ap()
    cmask = nc.dram_tensor("cmask", [4, 128, 512], F32, kind="ExternalInput").ap()
    ao = nc.dram_tensor("ao", [256, S], BF16, kind="ExternalOutput").ap()
    dbg = nc.dram_tensor("dbg", [128, 4 * 512 + 6 * 512], F32, kind="ExternalOutput").ap() if DBG_ATTN else None

    SC = float(1.0 / np.sqrt(96.0))
    EBIAS = float(-80.0 * SC)
    PLO = float(np.exp(-160.0 * SC))

    st = ExitStack()
    xb16 = st.enter_context(nc.sbuf_tensor("xb16", [128, KC * 512], BF16))
    wdq = st.enter_context(nc.sbuf_tensor("wdq", [128, KC * 256], BF16))
    wdkv = st.enter_context(nc.sbuf_tensor("wdkv", [128, KC * 256], BF16))
    wu = st.enter_context(nc.sbuf_tensor("wu", [128, 3 * 512 + 512], BF16))
    wqr = st.enter_context(nc.sbuf_tensor("wqr", [128, 512], BF16))
    wqr2 = st.enter_context(nc.sbuf_tensor("wqr2", [128, 512], BF16))
    wkr = st.enter_context(nc.sbuf_tensor("wkr", [128, KC * 64], BF16))
    wst = st.enter_context(nc.sbuf_tensor("wst", [128, 1024], F32))
    cq = st.enter_context(nc.sbuf_tensor("cq", [128, 2 * S], BF16))
    ckv = st.enter_context(nc.sbuf_tensor("ckv", [128, 2 * S], BF16))
    krr = st.enter_context(nc.sbuf_tensor("krr", [128, S], BF16))
    rct = st.enter_context(nc.sbuf_tensor("rct", [128, S], BF16))
    rst = st.enter_context(nc.sbuf_tensor("rst", [128, S], BF16))
    kc96 = st.enter_context(nc.sbuf_tensor("kc96", [128, S], BF16))
    qc96 = st.enter_context(nc.sbuf_tensor("qc96", [128, S], BF16))
    qr96 = st.enter_context(nc.sbuf_tensor("qr96", [128, S], BF16))
    vtok = st.enter_context(nc.sbuf_tensor("vtok", [128, 64 * 65], BF16))
    probs = st.enter_context(nc.sbuf_tensor("probs", [128, 512], BF16))
    smask = st.enter_context(nc.sbuf_tensor("smask", [128, 4 * 512], BF16))
    attnf = st.enter_context(nc.sbuf_tensor("attnf", [128, 512], F32))
    sums = st.enter_context(nc.sbuf_tensor("sums", [128, 2 * 512], F32))
    onesrow = st.enter_context(nc.sbuf_tensor("onesrow", [1, 128], F32))
    aob = st.enter_context(nc.sbuf_tensor("aob", [128, 512], BF16))
    psK = st.enter_context(nc.psum_tensor("psK", [128, 512], F32))
    psQ = st.enter_context(nc.psum_tensor("psQ", [128, 512], F32))
    psR = st.enter_context(nc.psum_tensor("psR", [128, 512], F32))
    psSc = st.enter_context(nc.psum_tensor("psSc", [128, 512], F32))
    psAt = st.enter_context(nc.psum_tensor("psAt", [128, 512], F32))
    psB2 = st.enter_context(nc.psum_tensor("psB2", [128, 512], F32))
    if True:
        p = Prog()
        xstg3 = rearr(wst[:, 0:1024], "p (k t) -> p k t", k=2)
        xb3 = rearr(xb16[:], "p (k t) -> p k t", k=KC)
        cq3 = rearr(cq[:], "p (m t) -> p m t", m=2)
        ckv3 = rearr(ckv[:], "p (m t) -> p m t", m=2)
        wkr3 = rearr(wkr[:], "p (k d) -> p k d", k=KC)
        emit_weight_load(p, WDQ, wst, wdq, KC, 256, "wdq", stage_cols=1024)
        emit_weight_load(p, WDKV, wst, wdkv, KC, 256, "wdkv", stage_cols=1024)
        for i, wsrc in enumerate((WUQ, WUK, WUV)):
            p.dma("sync", rearr(wst[:, 0:512], "p (k d) -> p k d", k=2),
                  rearr(wsrc, "(k p) d -> p k d", p=128), reads=("wstage",), writes=("wstage",))
            p.op("vector", lambda e, i=i: e.tensor_copy(wu[:, i * 512:(i + 1) * 512], wst[:, 0:512]),
                 reads=("wstage",), writes=("wu",))
        p.dma("sync", rearr(wst[:, 0:256], "p (k d) -> p k d", k=2),
              rearr(WQR, "(k p) d -> p k d", p=128), reads=("wstage",), writes=("wstage",))
        p.op("vector", lambda e: e.tensor_copy(wqr[:, 0:256], wst[:, 0:256]),
             reads=("wstage",), writes=("wqr",))
        wq1 = rearr(wqr[:, 0:256], "p (k d) -> p k d", k=2)
        wq2 = rearr(wqr[:, 256:512], "p (k d) -> p k d", k=2)
        p.op("vector", lambda e: e.tensor_scalar(
            wq2[:, :, 0:128:2], wq1[:, :, 1:128:2], -1.0, None, ALU.mult),
            reads=("wqr",), writes=("wqrs",))
        p.op("vector", lambda e: e.tensor_copy(wq2[:, :, 1:128:2], wq1[:, :, 0:128:2]),
             reads=("wqr",), writes=("wqrs",))
        wq2v = rearr(wu[:, 1536:2048], "p (k h f d) -> p k h f d", k=2, h=4, f=2)
        p.op("vector", lambda e: e.tensor_copy(
            wq2v[:, :, :, 0, :], rearr(wqr[:, 0:256], "p (k h d) -> p k h d", k=2, h=4)),
            reads=("wqr",), writes=("wqr2", "wu"))
        p.op("vector", lambda e: e.tensor_copy(
            wq2v[:, :, :, 1, :], rearr(wqr[:, 256:512], "p (k h d) -> p k h d", k=2, h=4)),
            reads=("wqrs",), writes=("wqr2", "wu"))
        p.dma("sync", rearr(wst[:, 0:256], "p (k d) -> p k d", k=KC),
              rearr(WKR, "(k p) d -> p k d", p=128), reads=("wstage", "wqr", "wqrs"), writes=("wstage",))
        p.op("vector", lambda e: e.tensor_copy(
            wkr3[:, :, 0:32], rearr(wst[:, 0:256], "p (k d) -> p k d", k=KC)),
            reads=("wstage",), writes=("wkr",))
        p.op("vector", lambda e: e.tensor_scalar(
            wkr3[:, :, 32:64:2], wkr3[:, :, 1:32:2], -1.0, None, ALU.mult),
            reads=("wkr",), writes=("wkrs",))
        p.op("vector", lambda e: e.tensor_copy(wkr3[:, :, 33:64:2], wkr3[:, :, 0:32:2]),
             reads=("wkr",), writes=("wkrs",))
        for tt, dst in ((rc, rct), (rs, rst)):
            for cc in range(S // 2048):
                p.dma("sync", wst[0:32, 0:512], tt[:, cc * 2048:cc * 2048 + 512], reads=("wstage",), writes=("wstage",))
                p.op("vector", lambda e, dst=dst, cc=cc: e.tensor_copy(
                    dst[0:32, cc * 2048:cc * 2048 + 512], wst[0:32, 0:512]),
                    reads=("wstage",), writes=("rope",))
        for tt, dst in ((rc, rct), (rs, rst)):
            for cc in range(S // 512):
                if cc % 4 == 0:
                    continue
                p.dma("sync", wst[0:32, 0:512], tt[:, cc * 512:(cc + 1) * 512], reads=("wstage",), writes=("wstage",))
                p.op("vector", lambda e, dst=dst, cc=cc: e.tensor_copy(
                    dst[0:32, cc * 512:(cc + 1) * 512], wst[0:32, 0:512]),
                    reads=("wstage",), writes=("rope",))
        for vv in range(4):
            p.dma("sync", wst[:, 0:512], cmask[vv], reads=("wstage",), writes=("wstage",))
            p.op("vector", lambda e, vv=vv: e.tensor_copy(
                smask[:, vv * 512:(vv + 1) * 512], wst[:, 0:512]),
                reads=("wstage",), writes=("cmask",))
        p.op("gpsimd", lambda e: e.memset(onesrow[:], 1.0), writes=("ones",))
        for tb in range(ntb):
            for hh in range(4):
                p.dma("sync", xstg3,
                      rearr(xb[hh * 256:(hh + 1) * 256, tb * 512:(tb + 1) * 512], "(k p) t -> p k t", p=128),
                      reads=("wstage",), writes=("wstage",))
                p.op("vector", lambda e, hh=hh: e.tensor_copy(xb3[:, hh * 2:(hh + 1) * 2, :], xstg3),
                     reads=("wstage",), writes=("xs", "wstage"))
            for wbuf, wkey, ob3, okey, pst, pk in (
                    (wdq, "wdq", cq3, "cq", psK, "psK"), (wdkv, "wdkv", ckv3, "ckv", psQ, "psQ")):
                for m in range(2):
                    for k in range(KC):
                        p.op("tensor", lambda e, m=m, k=k, wbuf=wbuf, pst=pst: e.matmul(
                            pst[:, 0:512], wbuf[:, k * 256 + m * 128:k * 256 + (m + 1) * 128],
                            xb3[:, k, :], start=(k == 0), stop=(k == KC - 1)),
                            reads=(wkey, "xs"), writes=(pk,))
                    p.op("scalar", lambda e, m=m, tb=tb, ob3=ob3, pst=pst: e.copy(
                        ob3[:, m, tb * 512:(tb + 1) * 512], pst[:, 0:512]),
                        reads=(pk,), writes=(okey,))
            for half in range(2):
                for k in range(KC):
                    p.op("tensor", lambda e, k=k, half=half: e.matmul(
                        psR[0:32, 0:512], wkr3[:, k, half * 32:(half + 1) * 32], xb3[:, k, :],
                        start=(k == 0), stop=(k == KC - 1)),
                        reads=("wkr", "wkrs", "xs"), writes=("psR",))
                p.op("vector", lambda e, half=half: e.tensor_copy(
                    wst[0:32, half * 512:(half + 1) * 512], psR[0:32, 0:512]),
                    reads=("psR", "wstage"), writes=("wstage",))
            p.op("vector", lambda e, tb=tb: e.tensor_tensor(
                wst[0:32, 0:512], wst[0:32, 0:512], rct[0:32, tb * 512:(tb + 1) * 512], ALU.mult),
                reads=("wstage", "rope"), writes=("wstage",))
            p.op("vector", lambda e, tb=tb: e.tensor_tensor(
                wst[0:32, 512:1024], wst[0:32, 512:1024], rst[0:32, tb * 512:(tb + 1) * 512], ALU.mult),
                reads=("wstage", "rope"), writes=("wstage",))
            p.op("vector", lambda e, tb=tb: e.tensor_tensor(
                krr[0:32, tb * 512:(tb + 1) * 512], wst[0:32, 0:512], wst[0:32, 512:1024], ALU.add),
                reads=("wstage",), writes=("krrot",))
        for h in range(4):
            for tb in range(ntb):
                for k in range(2):
                    p.op("tensor", lambda e, h=h, tb=tb, k=k: e.matmul(
                        psK[0:64, 0:512], wu[:, 512 + k * 256 + h * 64:512 + k * 256 + (h + 1) * 64],
                        ckv3[:, k, tb * 512:(tb + 1) * 512], start=(k == 0), stop=(k == 1)),
                        reads=("wu", "ckv"), writes=("psK",))
                p.op("scalar", lambda e, tb=tb: e.copy(
                    kc96[0:64, tb * 512:(tb + 1) * 512], psK[0:64, 0:512]),
                    reads=("psK",), writes=("k96c",))
                for k in range(2):
                    p.op("tensor", lambda e, h=h, tb=tb, k=k: e.matmul(
                        psQ[0:64, 0:512], wu[:, k * 256 + h * 64:k * 256 + (h + 1) * 64],
                        cq3[:, k, tb * 512:(tb + 1) * 512], start=(k == 0), stop=(k == 1)),
                        reads=("wu", "cq"), writes=("psQ",))
                p.op("scalar", lambda e, tb=tb: e.copy(
                    qc96[0:64, tb * 512:(tb + 1) * 512], psQ[0:64, 0:512]),
                    reads=("psQ",), writes=("q96c",))
                for k in range(2):
                    p.op("tensor", lambda e, h=h, tb=tb, k=k: e.matmul(
                        psQ[0:64, 0:512],
                        wu[:, 1536 + k * 256 + h * 64:1536 + k * 256 + (h + 1) * 64],
                        cq3[:, k, tb * 512:(tb + 1) * 512], start=(k == 0), stop=(k == 1)),
                        reads=("wqr2", "cq"), writes=("psQ",))
                p.op("vector", lambda e: e.tensor_copy(
                    wst[0:32, 0:512], psQ[0:32, 0:512]),
                    reads=("psQ", "wstage"), writes=("wstage",))
                p.op("vector", lambda e: e.tensor_copy(
                    wst[0:32, 512:1024], psQ[32:64, 0:512]),
                    reads=("psQ", "wstage"), writes=("wstage",))
                if DBG_ATTN and h == 0 and tb == 0:
                    p.dma("gpsimd", dbg[0:32, 4096:4608], wst[0:32, 0:512], reads=("wstage",))
                    p.dma("gpsimd", dbg[0:32, 4608:5120], wst[0:32, 512:1024], reads=("wstage",))
                p.op("vector", lambda e, tb=tb: e.tensor_tensor(
                    wst[0:32, 0:512], wst[0:32, 0:512], rct[0:32, tb * 512:(tb + 1) * 512], ALU.mult),
                    reads=("wstage", "rope"), writes=("wstage",))
                p.op("vector", lambda e, tb=tb: e.tensor_tensor(
                    wst[0:32, 512:1024], wst[0:32, 512:1024], rst[0:32, tb * 512:(tb + 1) * 512], ALU.mult),
                    reads=("wstage", "rope"), writes=("wstage",))
                if DBG_ATTN and h == 0 and tb == 0:
                    p.dma("gpsimd", dbg[0:32, 1024:1536], wst[0:32, 0:512], reads=("wstage",))
                    p.dma("gpsimd", dbg[0:32, 1536:2048], wst[0:32, 512:1024], reads=("wstage",))
                    p.dma("gpsimd", dbg[0:32, 0:512], rct[0:32, 0:512], reads=("rope",))
                    p.dma("gpsimd", dbg[0:32, 512:1024], rst[0:32, 0:512], reads=("rope",))
                p.op("vector", lambda e, tb=tb: e.tensor_tensor(
                    qr96[0:32, tb * 512:(tb + 1) * 512], wst[0:32, 0:512], wst[0:32, 512:1024], ALU.add),
                    reads=("wstage",), writes=("q96r",))
            p.op("gpsimd", lambda e: e.memset(vtok[:], 0.0), reads=("vuse",), writes=("vtok",))
            p.op("gpsimd", lambda e: e.memset(
                RawAP2(vtok, 64, 65), 1.0), reads=("vtok",), writes=("vtok",))
            for blk in range(S // 128):
                for k in range(2):
                    p.op("tensor", lambda e, h=h, blk=blk, k=k: e.matmul(
                        psB2[:, 0:64], ckv3[:, k, blk * 128:(blk + 1) * 128],
                        wu[:, 1024 + k * 256 + h * 64:1024 + k * 256 + (h + 1) * 64],
                        start=(k == 0), stop=(k == 1)),
                        reads=("wu", "ckv"), writes=("psB2",))
                p.op("vector", lambda e, blk=blk: e.tensor_copy(
                    vtok[:, blk * 65:blk * 65 + 64], psB2[:, 0:64]),
                    reads=("psB2", "vtok"), writes=("vtok",))
            for qc in range(nqc):
                nkc = 4 * qc + 4
                for kc in range(nkc):
                    par = kc % 2
                    p.op("tensor", lambda e, qc=qc, kc=kc: e.matmul(
                        psSc[:, 0:512], kc96[0:64, kc * 128:(kc + 1) * 128],
                        qc96[0:64, qc * 512:(qc + 1) * 512], start=True, stop=False),
                        reads=("k96c", "q96c", "psScE"), writes=("psSc",))
                    p.op("tensor", lambda e, qc=qc, kc=kc: e.matmul(
                        psSc[:, 0:512], krr[0:32, kc * 128:(kc + 1) * 128],
                        qr96[0:32, qc * 512:(qc + 1) * 512], start=False, stop=True),
                        reads=("krrot", "q96r", "psScE"), writes=("psSc",))
                    p.op("scalar", lambda e, par=par: e.activation(
                        probs[:, 0:512], psSc[:, 0:512],
                        AF.Exp, bias=f(EBIAS), scale=f(SC)),
                        reads=("psSc", f"prF{par}"), writes=(f"pr{par}", "psScE"))
                    p.op("vector", lambda e, par=par: e.tensor_scalar(
                        probs[:, 0:512], probs[:, 0:512],
                        1.0, PLO, ALU.min, ALU.max),
                        reads=(f"pr{par}",), writes=(f"pr{par}",))
                    j = kc - 4 * qc
                    if j >= 0:
                        p.op("vector", lambda e, par=par, j=j: e.tensor_tensor(
                            probs[:, 0:512],
                            probs[:, 0:512],
                            rearr(smask[:], "p (j t) -> p j t", j=4)[:, j, :], ALU.mult),
                            reads=(f"pr{par}", "cmask"), writes=(f"pr{par}",))
                    p.op("tensor", lambda e, qc=qc, kc=kc, par=par: e.matmul(
                        psAt[0:65, 0:512], vtok[:, kc * 65:(kc + 1) * 65],
                        probs[:, 0:512],
                        start=(kc == 0), stop=(kc == nkc - 1)),
                        reads=("vtok", f"pr{par}"), writes=("psAt", f"prF{par}"))
                if DBG_ATTN and h == 0 and qc == 0:

                    p.dma("gpsimd", dbg[0:64, 4 * 512:5 * 512], qc96[0:64, 0:512], reads=("q96c",))
                    p.dma("gpsimd", dbg[0:64, 5 * 512:6 * 512], kc96[0:64, 0:512], reads=("k96c",))
                    p.dma("gpsimd", dbg[0:32, 6 * 512:7 * 512], qr96[0:32, 0:512], reads=("q96r",))
                    p.dma("gpsimd", dbg[0:32, 7 * 512:8 * 512], krr[0:32, 0:512], reads=("krrot",))

                p.op("vector", lambda e: e.tensor_copy(sums[0:1, 0:512], psAt[64:65, 0:512]),
                     reads=("psAt", "sumsb"), writes=("sums1", "sumsb"))
                p.op("vector", lambda e: e.reciprocal(sums[0:1, 512:1024], sums[0:1, 0:512]),
                     reads=("sums1", "sumsb"), writes=("sums2", "sumsb"))
                p.op("tensor", lambda e: e.matmul(
                    psB2[0:64, 0:512], onesrow[0:1, 0:64], sums[0:1, 512:1024],
                    start=True, stop=True),
                    reads=("ones", "sums2"), writes=("psB2",))
                p.op("scalar", lambda e: e.copy(attnf[0:64, 0:512], psAt[0:64, 0:512]),
                     reads=("psAt", "attnf"), writes=("attnf",))
                p.op("vector", lambda e: e.tensor_tensor(
                    aob[0:64, 0:512], attnf[0:64, 0:512], psB2[0:64, 0:512], ALU.mult),
                    reads=("attnf", "psB2", "aobF"), writes=("aob",))
                p.dma("sync", ao[h * 64:(h + 1) * 64, qc * 512:(qc + 1) * 512], aob[0:64, 0:512],
                      reads=("aob",), writes=("aobF",))
            p.op("vector", lambda e: e.tensor_copy(sums[0:1, 0:1], vtok[0:1, 0:1]),
                 reads=("vtok", "sumsb"), writes=("vuse", "sumsb"))
        p.emit(nc)
    st.close()
    return nc


def RawAP2(t, off, step):
    from concourse.ap import AP as _AP
    n = t[:].shape[1] // step
    return _AP(t[:].tensor, off, [[t[:].shape[1], 128], [step, n]])


def run_l0attn(xT, inp):
    if "l0attn" not in _cache:
        _cache["l0attn"] = mk_l0attn()
    nc = _cache["l0attn"]
    d = 32
    inv = 10000.0 ** (-np.arange(0, d, 2, dtype=np.float64) / d)
    ang = np.arange(SEQ, dtype=np.float64)[:, None] * inv
    cos_t = np.zeros((32, SEQ), np.float32)
    sin_t = np.zeros((32, SEQ), np.float32)
    for i in range(16):
        cos_t[2 * i] = np.cos(ang[:, i]); cos_t[2 * i + 1] = np.cos(ang[:, i])
        sin_t[2 * i] = np.sin(ang[:, i]); sin_t[2 * i + 1] = np.sin(ang[:, i])
    cm = np.zeros((4, 128, 512), np.float32)
    for j in range(4):
        for kk in range(128):
            cm[j, kk, :] = (np.arange(512) >= kk + j * 128)
    in_maps = []
    for c in range(NCORE):
        b = c // 4
        h0 = (c % 4) * 4
        in_maps.append(dict(
            xb=np.ascontiguousarray(xT[:, b * SEQ:(b + 1) * SEQ]),
            WDQ=inp["L0_attn_W_DQ"], WDKV=inp["L0_attn_W_DKV"],
            WUQ=np.ascontiguousarray(inp["L0_attn_W_UQ"][:, h0 * 64:(h0 + 4) * 64]),
            WQR=np.ascontiguousarray(inp["L0_attn_W_QR"][:, h0 * 32:(h0 + 4) * 32]),
            WUK=np.ascontiguousarray(inp["L0_attn_W_UK"][:, h0 * 64:(h0 + 4) * 64]),
            WUV=np.ascontiguousarray(inp["L0_attn_W_UV"][:, h0 * 64:(h0 + 4) * 64]),
            WKR=inp["L0_attn_W_KR"],
            rc=cos_t, rs=sin_t, cmask=cm))
    res = run_bass_kernel_spmd(nc, in_maps, core_ids=list(range(NCORE)))
    out = np.zeros((1024, TOK), dtype=BF)
    for c in range(NCORE):
        b = c // 4
        h0 = (c % 4) * 4
        out[h0 * 64:(h0 + 4) * 64, b * SEQ:(b + 1) * SEQ] = res.results[c]["ao"]
    return out



L1_NO_ATTN = False
L1_NO_FFN = False


def mk_l1ffn(final, T=TPC):
    KC = 8
    NW = 6
    NJ = 22
    FS = 256
    nc = bass.Bass("TRN2", target_bir_lowering=False, debug=False)
    rin = nc.dram_tensor("rin", [D, 1025 + T], F32, kind="ExternalInput").ap()
    wqkv = nc.dram_tensor("wqkv", [D, 1536], F32, kind="ExternalInput").ap()
    qsc = nc.dram_tensor("qsc", [128, 2], F32, kind="ExternalInput").ap()
    ksc = nc.dram_tensor("ksc", [128, 2], F32, kind="ExternalInput").ap()
    wo_d = nc.dram_tensor("wo", [512, D], F32, kind="ExternalInput").ap()
    lng = nc.dram_tensor("lng", [D, 1], F32, kind="ExternalInput").ap()
    lnb = nc.dram_tensor("lnb", [D, 1], F32, kind="ExternalInput").ap()
    w1p = nc.dram_tensor("w1p", [D, NJ * 256], F32, kind="ExternalInput").ap()
    b1p = nc.dram_tensor("b1p", [128, 2 * NJ], F32, kind="ExternalInput").ap()
    w2p = nc.dram_tensor("w2p", [NJ * 128, D], F32, kind="ExternalInput").ap()
    b2 = nc.dram_tensor("b2", [D, 1], F32, kind="ExternalInput").ap()
    tabs = nc.dram_tensor("tabs", [128, 4 * (1024 + T)], F32, kind="ExternalInput").ap()
    amask = nc.dram_tensor("amask", [128, 1408 + 1024], F32, kind="ExternalInput").ap()
    eye = nc.dram_tensor("eye", [128, 128], F32, kind="ExternalInput").ap()
    bonesd = nc.dram_tensor("bonesd", [2, 128], F32, kind="ExternalInput").ap()
    r3c = nc.dram_tensor("r3c", [D, T], F32, kind="ExternalInput").ap()
    rout = nc.dram_tensor("rout", [D, T], F32, kind="ExternalOutput").ap()

    st = ExitStack()
    x1 = st.enter_context(nc.sbuf_tensor("x1", [128, KC * 513], F32))
    x1b = st.enter_context(nc.sbuf_tensor("x1b", [128, KC * 513], BF16))
    wq = st.enter_context(nc.sbuf_tensor("wq", [128, KC * 1536], BF16))
    wrot = st.enter_context(nc.sbuf_tensor("wrot", [128, KC * 1024], BF16))
    wst = st.enter_context(nc.sbuf_tensor("wst", [128, 1024], F32))
    wob = st.enter_context(nc.sbuf_tensor("wob", [64, 8 * D], BF16))
    w1b = st.enter_context(nc.sbuf_tensor("w1b", [128, KC * 256], BF16))
    w2b = st.enter_context(nc.sbuf_tensor("w2b", [128, D], BF16))
    sqs = st.enter_context(nc.sbuf_tensor("sqs", [128, 4], F32))
    sgb = st.enter_context(nc.sbuf_tensor("sgb", [128, 2 * KC], F32))
    sb2 = st.enter_context(nc.sbuf_tensor("sb2", [128, KC], F32))
    sb1 = st.enter_context(nc.sbuf_tensor("sb1", [128, 2 * NJ], F32))
    stab = st.enter_context(nc.sbuf_tensor("stab", [128, 4 * 512], BF16))
    smask = st.enter_context(nc.sbuf_tensor("smask", [128, 1408 + 1024], BF16))
    eyeb = st.enter_context(nc.sbuf_tensor("eyeb", [128, 128], BF16))
    bones = st.enter_context(nc.sbuf_tensor("bones", [2, 128], F32))
    honesr = st.enter_context(nc.sbuf_tensor("honesr", [128, 2], F32))
    onesrow = st.enter_context(nc.sbuf_tensor("onesrow", [1, 128], F32))
    ones = st.enter_context(nc.sbuf_tensor("ones", [128, 1], F32))
    qraw = st.enter_context(nc.sbuf_tensor("qraw", [128, 512], BF16))
    qrot = st.enter_context(nc.sbuf_tensor("qrot", [128, 512], BF16))
    qu = st.enter_context(nc.sbuf_tensor("qu", [128, 2 * 4 * 512], BF16))
    ku = st.enter_context(nc.sbuf_tensor("ku", [128, 2 * 4 * 512], BF16))
    vt = st.enter_context(nc.sbuf_tensor("vt", [128, 2 * 8 * 4 * 65], BF16))
    normb = st.enter_context(nc.sbuf_tensor("normb", [2, 512], F32))
    sc = st.enter_context(nc.sbuf_tensor("sc", [128, 1024], F32))
    rmax = st.enter_context(nc.sbuf_tensor("rmax", [128, 2], F32))
    probs = st.enter_context(nc.sbuf_tensor("probs", [128, 1024], BF16))
    atn = st.enter_context(nc.sbuf_tensor("atn", [64, 8 * 128], BF16))
    attnf = st.enter_context(nc.sbuf_tensor("attnf", [128, 128], F32))
    sums = st.enter_context(nc.sbuf_tensor("sums", [128, 2 * 128], F32))
    x2 = st.enter_context(nc.sbuf_tensor("x2", [128, KC * 513], F32))
    lnbuf = st.enter_context(nc.sbuf_tensor("lnbuf", [128, KC * FS], BF16))
    hbuf = st.enter_context(nc.sbuf_tensor("hbuf", [128, NJ * FS], BF16))
    gel = st.enter_context(nc.sbuf_tensor("gel", [128, FS], F32))
    rob = st.enter_context(nc.sbuf_tensor("rob", [128, KC * 512], F32))
    sq = st.enter_context(nc.sbuf_tensor("sq", [128, 512], F32))
    stats = st.enter_context(nc.sbuf_tensor("stats", [1, 3 * 512], F32))
    bcast = st.enter_context(nc.sbuf_tensor("bcast", [128, 2 * 512], F32))
    psA = st.enter_context(nc.psum_tensor("psA", [128, 512], F32))
    psG = st.enter_context(nc.psum_tensor("psG", [128, 512], F32))
    ps0 = st.enter_context(nc.psum_tensor("ps0", [128, 512], F32))
    ps1 = st.enter_context(nc.psum_tensor("ps1", [128, 512], F32))
    ps2 = st.enter_context(nc.psum_tensor("ps2", [128, 512], F32))
    psS = st.enter_context(nc.psum_tensor("psS", [128, 512], F32))
    psS2 = st.enter_context(nc.psum_tensor("psS2", [128, 512], F32))
    psT = st.enter_context(nc.psum_tensor("psT", [128, 256], BF16))
    if True:
        p = Prog()
        x13 = rearr(x1[:], "p (k t) -> p k t", k=KC)
        x1b3 = rearr(x1b[:], "p (k t) -> p k t", k=KC)
        x23 = rearr(x2[:], "p (k t) -> p k t", k=KC)
        qu4 = rearr(qu[:], "p (w i t) -> p w i t", w=2, i=4)
        ku4 = rearr(ku[:], "p (w i t) -> p w i t", w=2, i=4)
        vt4 = rearr(vt[:], "p (w h b) -> p w h b", w=2, h=8)
        lnb3 = rearr(lnbuf[:], "p (k t) -> p k t", k=KC)
        rob3 = rearr(rob[:], "p (k t) -> p k t", k=KC)
        hb3 = rearr(hbuf[:], "p (j t) -> p j t", j=NJ)
        stab3 = rearr(stab[:], "p (a t) -> p a t", a=4)
        wq4 = rearr(wq[:], "p (k c) -> p k c", k=KC)
        wr4 = rearr(wrot[:], "p (k c) -> p k c", k=KC)
        p.op("gpsimd", lambda e: e.memset(onesrow[:], 1.0), writes=("ones",))
        p.op("gpsimd", lambda e: e.memset(ones[:], 1.0), writes=("ones",))
        p.op("gpsimd", lambda e: e.memset(honesr[:], 0.0), writes=("hones",))
        p.op("gpsimd", lambda e: e.memset(honesr[0:64, 0:1], 1.0), writes=("hones",))
        p.op("gpsimd", lambda e: e.memset(honesr[64:128, 1:2], 1.0), writes=("hones",))
        p.dma("sync", bones[:], bonesd, writes=("bones",))
        for cc in range(3):
            c0, c1 = cc * 1024, min((cc + 1) * 1024, 2432)
            p.dma("sync", wst[:, 0:c1 - c0], amask[:, c0:c1], reads=("wstage",), writes=("wstage",))
            p.op("vector", lambda e, c0=c0, c1=c1: e.tensor_copy(
                smask[:, c0:c1], wst[:, 0:c1 - c0]),
                reads=("wstage",), writes=("amask",))
        p.dma("sync", wst[:, 0:128], eye, reads=("wstage",), writes=("wstage",))
        p.op("vector", lambda e: e.tensor_copy(eyeb[:], wst[:, 0:128]), reads=("wstage",), writes=("eye", "wstage"))
        p.dma("sync", sqs[:, 0:2], qsc, writes=("sqs",))
        p.dma("sync", sqs[:, 2:4], ksc, writes=("sqs",))
        p.dma("sync", sgb[:, 0:KC].unsqueeze(2), rearr(lng, "(k p) o -> p k o", p=128),
              writes=("gb",), allow_slow_non_contiguous=True)
        p.dma("sync", sgb[:, KC:2 * KC].unsqueeze(2), rearr(lnb, "(k p) o -> p k o", p=128),
              writes=("gb",), allow_slow_non_contiguous=True)
        p.dma("sync", sb2[:].unsqueeze(2), rearr(b2, "(k p) o -> p k o", p=128),
              writes=("gb",), allow_slow_non_contiguous=True)
        p.dma("sync", sb1[:], b1p, writes=("gb",))
        for k_ in range(KC):
            for hh in range(2):
                p.dma("sync", wst[:, 0:768], wqkv[k_ * 128:(k_ + 1) * 128, hh * 768:(hh + 1) * 768],
                      reads=("wstage",), writes=("wstage",))
                p.op("vector", lambda e, k_=k_, hh=hh: e.tensor_copy(
                    wq[:, k_ * 1536 + hh * 768:k_ * 1536 + (hh + 1) * 768], wst[:, 0:768]),
                    reads=("wstage",), writes=("wq",))
        for h_ in range(8):
            p.dma("sync", wst[0:64, 0:1024], wo_d[h_ * 64:(h_ + 1) * 64, :],
                  reads=("wstage",), writes=("wstage",))
            p.op("vector", lambda e, h_=h_: e.tensor_copy(
                wob[0:64, h_ * D:(h_ + 1) * D], wst[0:64, 0:1024]),
                reads=("wstage",), writes=("wo",))
        for half in range(2):
            src_ = rearr(wq4[:, :, half * 512:(half + 1) * 512], "p k (h d) -> p k h d", h=8)
            dst_ = rearr(wr4[:, :, half * 512:(half + 1) * 512], "p k (h d) -> p k h d", h=8)
            p.op("vector", lambda e, src_=src_, dst_=dst_: e.tensor_scalar(
                dst_[:, :, :, 0:32], src_[:, :, :, 32:64], -1.0, None, ALU.mult),
                reads=("wq",), writes=("wrot",))
            p.op("vector", lambda e, src_=src_, dst_=dst_: e.tensor_copy(
                dst_[:, :, :, 32:64], src_[:, :, :, 0:32]),
                reads=("wq",), writes=("wrot",))

        qblocks = {1: [3], 2: [0, 1, 2, 3], 3: [0, 1, 2, 3], 4: [0, 1, 2, 3], 5: [0, 1, 2, 3]}

        for g in range(NW):
            gp = g % 2
            p.dma("sync", x13,
                  rearr(rin[:, g * 512:g * 512 + 513], "(k p) t -> p k t", p=128),
                  reads=("x1F",), writes=("x1",))
            p.op("vector", lambda e: e.tensor_copy(x1b3, x13),
                 reads=("x1", "x1bF"), writes=("x1b",))
            p.dma("gpsimd", stab3,
                  rearr(tabs[:, :], "p (a t) -> p a t", a=4)[:, :, g * 512:(g + 1) * 512],
                  reads=("tabB",), writes=("tab",))
            for m in range(8):
                which = 0 if m < 4 else 1
                i = m % 4
                uk = f"qu{gp}" if which == 0 else f"ku{gp}"
                xu4 = qu4 if which == 0 else ku4
                sc0 = 0 if which == 0 else 2
                ta, tb_ = (0, 1) if which == 0 else (2, 3)
                for k in range(KC):
                    sh = 0 if k < 4 else -1
                    p.op("tensor", lambda e, m=m, k=k, sh=sh, gp=gp: e.matmul(
                        psA[:, 0:512], wq4[:, k, m * 128:(m + 1) * 128],
                        x1b3[:, k, 1 + sh:513 + sh], start=(k == 0), stop=(k == KC - 1)),
                        reads=("wq", "x1b"), writes=("psA",))
                for k in range(KC):
                    sh = 0 if k < 4 else -1
                    p.op("tensor", lambda e, m=m, k=k, sh=sh, gp=gp: e.matmul(
                        psG[:, 0:512], wr4[:, k, m * 128:(m + 1) * 128],
                        x1b3[:, k, 1 + sh:513 + sh], start=(k == 0), stop=(k == KC - 1)),
                        reads=("wrot", "x1b"), writes=("psG",))
                p.op("vector", lambda e: e.tensor_copy(qraw[:], psA[:, 0:512]),
                     reads=("psA", "qraw"), writes=("qraw",))
                p.op("vector", lambda e: e.tensor_copy(qrot[:], psG[:, 0:512]),
                     reads=("psG", "qrot"), writes=("qrot",))
                p.op("scalar", lambda e: e.activation(sq[:], qraw[:], AF.Square),
                     reads=("qraw", "sqb"), writes=("sq2", "sqb"))
                p.op("tensor", lambda e: e.matmul(
                    psS2[0:2, 0:512], honesr[:], sq[:], start=True, stop=True),
                    reads=("hones", "sq2"), writes=("psS2",))
                p.op("vector", lambda e: e.tensor_scalar(
                    normb[:], psS2[0:2, 0:512], 1e-24, None, ALU.max),
                    reads=("psS2", "nb"), writes=("nb",))
                p.op("scalar", lambda e: e.activation(normb[:], normb[:], AF.Ln),
                     reads=("nb",), writes=("nb",))
                p.op("scalar", lambda e: e.activation(normb[:], normb[:], AF.Exp, scale=f(-0.5)),
                     reads=("nb",), writes=("nb",))
                p.op("tensor", lambda e: e.matmul(
                    psS2[:, 0:512], bones[:], normb[:], start=True, stop=True),
                    reads=("bones", "nb"), writes=("psS2",))
                e1 = wst[:, 0:512]
                e2 = wst[:, 512:1024]
                p.op("vector", lambda e, e1=e1: e.tensor_tensor(e1, qraw[:], psS2[:, 0:512], ALU.mult),
                     reads=("qraw", "psS2", "wstage"), writes=("wstage",))
                p.op("vector", lambda e, e2=e2: e.tensor_tensor(e2, qrot[:], psS2[:, 0:512], ALU.mult),
                     reads=("qrot", "psS2", "wstage"), writes=("wstage",))
                p.op("vector", lambda e, e1=e1, sc0=sc0: e.tensor_scalar(
                    e1, e1, sqs[:, sc0:sc0 + 1], None, ALU.mult),
                    reads=("sqs", "wstage"), writes=("wstage",))
                p.op("vector", lambda e, e2=e2, sc0=sc0: e.tensor_scalar(
                    e2, e2, sqs[:, sc0 + 1:sc0 + 2], None, ALU.mult),
                    reads=("sqs", "wstage"), writes=("wstage",))
                p.op("vector", lambda e, e1=e1, ta=ta: e.tensor_tensor(
                    e1, e1, stab3[:, ta, :], ALU.mult),
                    reads=("tab", "wstage"), writes=("wstage",))
                p.op("vector", lambda e, e2=e2, tb_=tb_: e.tensor_tensor(
                    e2, e2, stab3[:, tb_, :], ALU.mult),
                    reads=("tab", "wstage"), writes=("wstage",))
                p.op("vector", lambda e, e1=e1, e2=e2, xu4=xu4, i=i, gp=gp: e.tensor_tensor(
                    xu4[:, gp, i, :], e1, e2, ALU.add),
                    reads=("wstage", uk), writes=(uk,))
            p.op("gpsimd", lambda e, gp=gp: e.memset(vt4[:, gp, :, :], 0.0),
                 reads=(f"vtF{gp}",), writes=(f"vt{gp}",))
            for h in range(8):
                p.op("gpsimd", lambda e, gp=gp, h=h: e.memset(
                    VOnesAP(vt, gp, h), 1.0), reads=(f"vt{gp}",), writes=(f"vt{gp}",))
            for h in range(8):
                for blk in range(4):
                    for k in range(KC):
                        sh = 0 if k < 4 else -1
                        p.op("tensor", lambda e, h=h, blk=blk, k=k, sh=sh, gp=gp: e.matmul(
                            psS[:, 0:64], x1b3[:, k, 1 + sh + blk * 128:1 + sh + (blk + 1) * 128],
                            wq4[:, k, 1024 + h * 64:1024 + (h + 1) * 64],
                            start=(k == 0), stop=(k == KC - 1)),
                            reads=("wq", "x1b"), writes=("psS",))
                    p.op("vector", lambda e, gp=gp, h=h, blk=blk: e.tensor_copy(
                        vt4[:, gp, h, blk * 65:blk * 65 + 64], psS[:, 0:64]),
                        reads=("psS", f"vt{gp}"), writes=(f"vt{gp}",))
            if g in qblocks and not L1_NO_ATTN:
                for qb in qblocks[g]:
                    for h in range(8):
                        pi = h // 2
                        base = 64 * (h % 2)
                        for kw in range(2):
                            kgp = (g - 1 + kw) % 2
                            ps_t = ps0 if kw == 0 else ps1
                            pkey = "ps0" if kw == 0 else "ps1"
                            p.op("tensor", lambda e, base=base, pi=pi, qb=qb, kgp=kgp, ps_t=ps_t, gp=gp: e.matmul(
                                ps_t[:, 0:512],
                                qu4[:, gp, pi, qb * 128:(qb + 1) * 128][base:base + 64, :],
                                ku4[:, kgp, pi, :][base:base + 64, :],
                                start=True, stop=True),
                                reads=(f"qu{gp}", f"ku{(g - 1) % 2}", f"ku{gp}", pkey + "E"), writes=(pkey,))
                        off = (3 - qb) * 128
                        mvar = smask[:, off:off + 1024]
                        p.op("vector", lambda e, mvar=mvar: e.tensor_tensor(
                            sc[:, 0:512], ps0[:, 0:512], mvar[:, 0:512], ALU.mult),
                            reads=("ps0", "amask", "scb"), writes=("scb", "ps0E"))
                        p.op("vector", lambda e, mvar=mvar: e.tensor_tensor(
                            sc[:, 512:1024], ps1[:, 0:512], mvar[:, 512:1024], ALU.mult),
                            reads=("ps1", "amask", "scb"), writes=("scb", "ps1E"))
                        if g == 2:
                            p.op("vector", lambda e: e.tensor_tensor(
                                sc[:], sc[:], smask[:, 1408:2432], ALU.mult),
                                reads=("amask", "scb"), writes=("scb",))
                        p.op("vector", lambda e: e.reduce_max(
                            rmax[:, 0:1].unsqueeze(2), sc[:].unsqueeze(1), axis=mybir.AxisListType.X),
                            reads=("scb",), writes=("rmaxb",))
                        p.op("vector", lambda e: e.tensor_scalar(
                            sc[:], sc[:], rmax[:, 0:1], None, ALU.subtract),
                            reads=("scb", "rmaxb"), writes=("scb",))
                        p.op("scalar", lambda e: e.activation(
                            probs[:], sc[:], AF.Exp, scale=f(8.0)),
                            reads=("scb", "prb"), writes=("prb",))
                        p.op("vector", lambda e, mvar=mvar: e.tensor_tensor(
                            probs[:], probs[:], mvar, ALU.mult),
                            reads=("prb", "amask"), writes=("prb",))
                        if g == 2:
                            p.op("vector", lambda e: e.tensor_tensor(
                                probs[:], probs[:], smask[:, 1408:2432], ALU.mult),
                                reads=("prb", "amask"), writes=("prb",))
                        for blk in range(8):
                            p.op("tensor", lambda e, blk=blk: e.transpose(
                                psT[:, 0:128], probs[:, blk * 128:(blk + 1) * 128], eyeb[:]),
                                reads=("prb", "eye", "psTE"), writes=("psT",))
                            p.op("vector", lambda e, blk=blk: e.tensor_copy(
                                probs[:, blk * 128:(blk + 1) * 128], psT[:, 0:128]),
                                reads=("psT", "prb"), writes=("prT", "prb", "psTE"))
                        for blk in range(8):
                            kw = blk // 4
                            kgp = (g - 1 + kw) % 2
                            p.op("tensor", lambda e, blk=blk, kgp=kgp, h=h: e.matmul(
                                psS2[0:65, 0:128], vt4[:, kgp, h, (blk % 4) * 65:(blk % 4 + 1) * 65],
                                probs[:, blk * 128:(blk + 1) * 128],
                                start=(blk == 0), stop=(blk == 7)),
                                reads=(f"vt{kgp}", "prT", "prb", "psS2E"), writes=("psS2",))
                        p.op("vector", lambda e: e.tensor_copy(sums[0:1, 0:128], psS2[64:65, 0:128]),
                             reads=("psS2", "sumsb"), writes=("sums1", "sumsb"))
                        p.op("vector", lambda e: e.tensor_copy(attnf[0:64, 0:128], sc[0:64, 0:128]),
                             reads=("scb", "attnf"), writes=("attnf",))
                        p.op("vector", lambda e: e.reciprocal(sums[0:1, 128:256], sums[0:1, 0:128]),
                             reads=("sums1", "sumsb"), writes=("sums2", "sumsb"))
                        p.op("tensor", lambda e: e.matmul(
                            ps2[0:64, 0:128],
                            onesrow[0:1, 0:64], sums[0:1, 128:256], start=True, stop=True),
                            reads=("ones", "sums2", "ps2E"), writes=("ps2",))
                        p.op("scalar", lambda e: e.copy(attnf[0:64, 0:128], psS2[0:64, 0:128]),
                             reads=("psS2", "attnf"), writes=("attnf", "psS2E"))
                        p.op("vector", lambda e, h=h: e.tensor_tensor(
                            atn[0:64, h * 128:(h + 1) * 128],
                            attnf[0:64, 0:128], ps2[0:64, 0:128], ALU.mult),
                            reads=("attnf", "ps2", "atnb"), writes=("atnb", "ps2E"))
                    for mo in range(KC):
                        for h2 in range(8):
                            p.op("tensor", lambda e, mo=mo, h2=h2: e.matmul(
                                ps2[:, 0:128],
                                wob[0:64, h2 * D + mo * 128:h2 * D + (mo + 1) * 128],
                                atn[0:64, h2 * 128:(h2 + 1) * 128],
                                start=(h2 == 0), stop=(h2 == 7)),
                                reads=("wo", "atnb", "ps2E"), writes=("ps2",))
                        if g >= 2:
                            p.op("vector", lambda e, mo=mo, qb=qb, gp=gp: e.tensor_tensor(
                                x23[:, mo, 1 + qb * 128:1 + (qb + 1) * 128], ps2[:, 0:128],
                                x13[:, mo, 1 + qb * 128:1 + (qb + 1) * 128], ALU.add),
                                reads=("ps2", "x1", "x2b"), writes=("x2b", "ps2E"))
                        else:
                            p.op("vector", lambda e, mo=mo, gp=gp: e.tensor_tensor(
                                x23[:, mo, 0:1], ps2[:, 127:128],
                                x13[:, mo, 512:513], ALU.add),
                                reads=("ps2", "x1", "x2b"), writes=("x2b", "ps2E"))
            p.op("vector", lambda e: e.tensor_copy(sq[:, 0:1], x13[:, 0, 0:1]),
                 reads=("x1", "x1b", "sqb"), writes=("x1F", "x1bF", "sqb"))
            if g >= 2 and not L1_NO_FFN:
                w = g - 2
                for kk in range(KC):
                    sh = 0 if kk < 4 else -1
                    p.op("vector", lambda e, kk=kk, sh=sh: e.tensor_copy(
                        rob3[:, kk, :], x23[:, kk, 1 + sh:513 + sh]),
                        reads=("x2b", "robb"), writes=(f"rob{kk}", "robb"))
                emit_ln(p, rob3, KC, 512, ones, onesrow, sq, stats, bcast, psS, psS2,
                        tag=f"fln{g}", out_keys=[f"rob{kk}" for kk in range(KC)])
                for fs in range(2):
                    fsl = slice(fs * FS, (fs + 1) * FS)
                    for kk in range(KC):
                        p.op("vector", lambda e, kk=kk, fsl=fsl: e.tensor_scalar(
                            lnb3[:, kk, :], rob3[:, kk, fsl], sgb[:, kk:kk + 1],
                            sgb[:, KC + kk:KC + kk + 1], ALU.mult, ALU.add),
                            reads=(f"rob{kk}", "gb", f"lnbF{kk}"), writes=(f"lnb{kk}",))
                    for j in range(NJ):
                        for hh in range(2):
                            p.dma("gpsimd", rearr(wst[:, 0:1024], "p (k c) -> p k c", k=4),
                                  rearr(w1p[512 * hh:512 * (hh + 1), j * 256:(j + 1) * 256],
                                        "(k p) c -> p k c", p=128),
                                  reads=("w1b", "wstage"), writes=("wstage1", "wstage"))
                            p.op("vector", lambda e, hh=hh: e.tensor_copy(
                                w1b[:, hh * 1024:(hh + 1) * 1024], wst[:, 0:1024]),
                                reads=("wstage1",), writes=("w1b",))
                        w1b3 = rearr(w1b[:], "p (k c) -> p k c", k=KC)
                        for k in range(KC):
                            p.op("tensor", lambda e, k=k: e.matmul(
                                psA[:, 0:FS], w1b3[:, k, 0:128], lnb3[:, k, :],
                                start=(k == 0), stop=(k == KC - 1)),
                                reads=("w1b",) + tuple(f"lnb{kk}" for kk in range(KC)),
                                writes=("psA",))
                        for k in range(KC):
                            p.op("tensor", lambda e, k=k: e.matmul(
                                psG[:, 0:FS], w1b3[:, k, 128:256], lnb3[:, k, :],
                                start=(k == 0), stop=(k == KC - 1)),
                                reads=("w1b",) + tuple(f"lnb{kk}" for kk in range(KC)),
                                writes=("psG",))
                        p.op("vector", lambda e, j=j: e.tensor_scalar(
                            gel[:], psG[:, 0:FS], sb1[:, NJ + j:NJ + j + 1], None, ALU.add),
                            reads=("psG", "gb", "gelb"), writes=("gelb",))
                        p.op("scalar", lambda e: e.activation(gel[:], gel[:], AF.Gelu),
                             reads=("gelb",), writes=("gelb",))
                        p.op("vector", lambda e, j=j: e.tensor_scalar(
                            sq[:, 0:FS], psA[:, 0:FS], sb1[:, j:j + 1], None, ALU.add),
                            reads=("psA", "gb", "sqb"), writes=("sq2", "sqb"))
                        p.op("vector", lambda e, j=j: e.tensor_tensor(
                            hb3[:, j, :], sq[:, 0:FS], gel[:], ALU.mult),
                            reads=("sq2", "gelb", "hbb"), writes=(f"hb{j}", "hbb"))
                    for mg in range(2):
                        banks = (ps0, ps1, ps2, psA)
                        bkeys = ("ps0", "ps1", "ps2", "psA")
                        for j in range(NJ):
                            p.dma("gpsimd", w2b_stage(wst), w2p[j * 128:(j + 1) * 128, :],
                                  reads=("w2b", "wstage"), writes=("wstage2", "wstage"))
                            p.op("vector", lambda e: e.tensor_copy(w2b[:, 0:D], wst[:, 0:1024]),
                                 reads=("wstage2",), writes=("w2b",))
                            for mi in range(4):
                                mo = mg * 4 + mi
                                p.op("tensor", lambda e, mi=mi, mo=mo, j=j, banks=banks: e.matmul(
                                    banks[mi][:, 0:FS], w2b[:, mo * 128:(mo + 1) * 128],
                                    hb3[:, j, :], start=(j == 0), stop=(j == NJ - 1)),
                                    reads=("w2b", f"hb{j}", bkeys[mi] + "E"), writes=(bkeys[mi],))
                        for mi in range(4):
                            mo = mg * 4 + mi
                            p.op("vector", lambda e, mi=mi, mo=mo, banks=banks, fsl=fsl: e.tensor_scalar(
                                sq[:, 0:FS], banks[mi][:, 0:FS], sb2[:, mo:mo + 1], None, ALU.add),
                                reads=(bkeys[mi], "gb", "sqb"), writes=("sq2", "sqb", bkeys[mi] + "E"))
                            p.op("vector", lambda e, mi=mi, mo=mo, fsl=fsl, fs=fs: e.tensor_tensor(
                                rob3[:, mo, fsl], sq[:, 0:FS], x23[:, mo, 1 + fs * FS:1 + (fs + 1) * FS], ALU.add),
                                reads=("sq2", "x2b", f"rob{mo}", "robb2"),
                                writes=(f"robo{mo}{fs}", f"rob{mo}", "robb2"))
                    p.op("vector", lambda e: e.tensor_copy(gel[:, 0:1], lnb3[:, 0, 0:1]),
                         reads=tuple(f"lnb{kk}" for kk in range(KC)) + ("gelb",),
                         writes=tuple(f"lnbF{kk}" for kk in range(KC)) + ("gelb",))
                if final:
                    p.op("vector", lambda e: e.tensor_copy(x23[:, :, 0:1], x23[:, :, 512:513]),
                         reads=("x2b",), writes=("x2b",))
                    p.dma("sync", x23[:, :, 1:513],
                          rearr(r3c[:, w * 512:(w + 1) * 512], "(k p) t -> p k t", p=128),
                          reads=("x2b",), writes=("x2b", "r3cb"))
                    for kk in range(KC):
                        p.op("vector", lambda e, kk=kk: e.tensor_tensor(
                            rob3[:, kk, :], rob3[:, kk, :], x23[:, kk, 1:513], ALU.add),
                            reads=("r3cb", "x2b", f"robo{kk}0", f"robo{kk}1"), writes=(f"rob{kk}",))
                    emit_ln(p, rob3, KC, 512, ones, onesrow, sq, stats, bcast, psS, psS2,
                            tag=f"oln{g}", out_keys=[f"rob{kk}" for kk in range(KC)])
                    p.dma("sync", rearr(rout[:, w * 512:(w + 1) * 512], "(k p) t -> p k t", p=128),
                          rob3, reads=tuple(f"rob{kk}" for kk in range(KC)) + ("robb",),
                          writes=("robb",))
                else:
                    p.dma("sync", rearr(rout[:, w * 512:(w + 1) * 512], "(k p) t -> p k t", p=128),
                          rob3,
                          reads=tuple(f"robo{kk}{ff}" for kk in range(KC) for ff in range(2)) + ("robb",),
                          writes=("robb",))
                if not final:
                    p.op("vector", lambda e: e.tensor_copy(x23[:, :, 0:1], x23[:, :, 512:513]),
                         reads=("x2b",), writes=("x2b",))
        if L1_NO_FFN:
            for w2_ in range(4):
                p.dma("sync", rearr(rout[:, w2_ * 512:(w2_ + 1) * 512], "(k p) t -> p k t", p=128),
                      x23[:, :, 0:512], reads=("x2b",), writes=())
        p.emit(nc)
    st.close()
    return nc


def VOnesAP(vt, gp, h):
    from concourse.ap import AP as _AP
    rowlen = vt[:].shape[1]
    off = gp * 8 * 4 * 65 + h * 4 * 65 + 64
    return _AP(vt[:].tensor, off, [[rowlen, 128], [65, 4]])


def w2b_stage(wst):
    return wst[:, 0:1024]



def _l1_hosts():
    if "l1h" in _cache:
        return _cache["l1h"]
    DIM_HEAD, WINDOW, n = 64, 512, SEQ
    half = np.arange(0, DIM_HEAD, 2, dtype=np.float64)
    inv = 1.0 / 10000.0 ** (half / DIM_HEAD)
    t = np.arange(n, dtype=np.float64)
    freqs = t[:, None] * inv[None, :]
    freqs = np.concatenate([freqs, freqs], axis=-1)
    base = (half + 0.4 * DIM_HEAD) / (1.4 * DIM_HEAD)
    power = (t - n // 2) / float(WINDOW // 2)
    scale = base[None, :] ** power[:, None]
    scale = np.concatenate([scale, scale], axis=-1)
    cos, sin = np.cos(freqs), np.sin(freqs)
    tabAq = (cos * scale).T.astype(np.float32)
    tabBq = (sin * scale).T.astype(np.float32)
    tabAk = (cos / scale).T.astype(np.float32)
    tabBk = (sin / scale).T.astype(np.float32)
    am = np.zeros((128, 1408 + 1024), np.float32)
    kk_ = np.arange(128)[:, None]
    cc_ = np.arange(1408)[None, :]
    am[:, 0:1408] = ((cc_ >= 384 + kk_) & (cc_ <= 384 + kk_ + 512)).astype(np.float32)
    am[:, 1408:2432] = (np.arange(1024)[None, :] >= 512).astype(np.float32)
    _cache["l1h"] = (tabAq, tabBq, tabAk, tabBk, am)
    return _cache["l1h"]


def _perm_w1(w1l, b1l, w2l):
    NJ, HID = 22, 2730
    w1a, w1g = w1l[:, :HID], w1l[:, HID:]
    b1a, b1g = b1l[:HID], b1l[HID:]
    w1p = np.zeros((D, NJ * 256), np.float32)
    b1p = np.zeros((128, 2 * NJ), np.float32)
    w2p = np.zeros((NJ * 128, D), np.float32)
    for j in range(NJ):
        lo, hi = j * 128, min((j + 1) * 128, HID)
        nn = hi - lo
        w1p[:, j * 256:j * 256 + nn] = w1a[:, lo:hi]
        w1p[:, j * 256 + 128:j * 256 + 128 + nn] = w1g[:, lo:hi]
        b1p[0:nn, j] = b1a[lo:hi]
        b1p[0:nn, NJ + j] = b1g[lo:hi]
        w2p[j * 128:j * 128 + nn, :] = w2l[lo:hi, :]
    return w1p, b1p, w2p


def run_l1ffn(rinT, inp, layer, r3T=None):
    key = f"l1ffn{int(layer == 1)}"
    if key not in _cache:
        _cache[key] = mk_l1ffn(final=(layer == 1))
    nc = _cache[key]
    tabAq, tabBq, tabAk, tabBk, am = _l1_hosts()
    l = layer
    w1p, b1p, w2p = _perm_w1(inp["L1_ffn_w1"][l], inp["L1_ffn_b1"][l], inp["L1_ffn_w2"][l])
    qs = inp["L1_ffn_q_scale"][l]
    ks = inp["L1_ffn_k_scale"][l]
    qsc = np.stack([np.tile(qs, 2), np.tile(np.roll(qs, 32), 2)], axis=1).astype(np.float32)
    kscv = np.stack([np.tile(ks, 2), np.tile(np.roll(ks, 32), 2)], axis=1).astype(np.float32)
    halo = np.concatenate([np.zeros((D, 1025), np.float32), rinT], axis=1)
    eye = np.eye(128, dtype=np.float32)
    in_maps = []
    for c in range(NCORE):
        t0 = c * TPC
        s0 = (c % 4) * TPC
        chunk = np.ascontiguousarray(halo[:, t0:t0 + 1025 + TPC])
        if c % 4 == 0:
            chunk[:, 0:1025] = 0.0
        idx = np.arange(s0 - 1024, s0 + TPC)
        idx = np.clip(idx, 0, SEQ - 1)
        tabs = np.concatenate([np.tile(tt[:, idx], (2, 1))[None] for tt in
                               (tabAq, tabBq, tabAk, tabBk)], axis=0)
        tabs = np.ascontiguousarray(tabs.transpose(1, 0, 2).reshape(128, 4 * 3072))
        amc = am.copy()
        if c % 4 != 0:
            amc[:, 1408:2432] = 1.0
        d = dict(rin=chunk, wqkv=np.ascontiguousarray(inp["L1_ffn_wqkv"][l]),
                 qsc=qsc, ksc=kscv,
                 wo=np.ascontiguousarray(inp["L1_ffn_wo"][l]),
                 lng=np.ascontiguousarray(inp["L1_ffn_ln_g"][l].reshape(D, 1)),
                 lnb=np.ascontiguousarray(inp["L1_ffn_ln_b"][l].reshape(D, 1)),
                 w1p=w1p, b1p=b1p, w2p=w2p,
                 b2=np.ascontiguousarray(inp["L1_ffn_b2"][l].reshape(D, 1)),
                 tabs=tabs, amask=amc, eye=eye,
                 bonesd=np.concatenate([np.concatenate([np.ones((1, 64), np.float32), np.zeros((1, 64), np.float32)], 1),
                                        np.concatenate([np.zeros((1, 64), np.float32), np.ones((1, 64), np.float32)], 1)], 0),
                 r3c=np.ascontiguousarray((r3T if r3T is not None else rinT)[:, t0:t0 + TPC]))
        in_maps.append(d)
    res = run_bass_kernel_spmd(nc, in_maps, core_ids=list(range(NCORE)))
    return np.concatenate([r["rout"] for r in res.results], axis=1)



def mk_l0post(T=TPC, TSUB=512):
    KC = D // 128
    nsub = T // TSUB
    nc = bass.Bass("TRN2", target_bir_lowering=False, debug=False)
    xc_d = nc.dram_tensor("xc", [D, T], F32, kind="ExternalInput").ap()
    aoc = nc.dram_tensor("aoc", [D, T], BF16, kind="ExternalInput").ap()
    WO = nc.dram_tensor("WO", [D, D], F32, kind="ExternalInput").ap()
    r1 = nc.dram_tensor("r1", [D, T], F32, kind="ExternalOutput").ap()

    st = ExitStack()
    xb = st.enter_context(nc.sbuf_tensor("xb", [128, KC * TSUB], F32))
    ab = st.enter_context(nc.sbuf_tensor("ab", [128, KC * TSUB], BF16))
    wst = st.enter_context(nc.sbuf_tensor("wst", [128, 4096], F32))
    wo = st.enter_context(nc.sbuf_tensor("wo", [128, KC * D], BF16))
    rout = st.enter_context(nc.sbuf_tensor("rout", [128, KC * TSUB], F32))
    ones = st.enter_context(nc.sbuf_tensor("ones", [128, 1], F32))
    onesrow = st.enter_context(nc.sbuf_tensor("onesrow", [1, 128], F32))
    sq = st.enter_context(nc.sbuf_tensor("sq", [128, TSUB], F32))
    stats = st.enter_context(nc.sbuf_tensor("stats", [1, 4 * TSUB], F32))
    bcast = st.enter_context(nc.sbuf_tensor("bcast", [128, 2 * TSUB], F32))
    psA = st.enter_context(nc.psum_tensor("psA", [128, 512], F32))
    psS = st.enter_context(nc.psum_tensor("psS", [128, 512], F32))
    psS2 = st.enter_context(nc.psum_tensor("psS2", [128, 512], F32))
    if True:
        p = Prog()
        xb3 = rearr(xb[:], "p (k t) -> p k t", k=KC)
        ab3 = rearr(ab[:], "p (k t) -> p k t", k=KC)
        rout3 = rearr(rout[:], "p (k t) -> p k t", k=KC)
        p.op("gpsimd", lambda e: e.memset(ones[:], 1.0), writes=("ones",))
        p.op("gpsimd", lambda e: e.memset(onesrow[:], 1.0), writes=("ones",))
        emit_weight_load(p, WO, wst, wo, KC, D, "wo")
        for s in range(nsub):
            sl = slice(s * TSUB, (s + 1) * TSUB)
            p.dma("sync", xb3, rearr(xc_d[:, sl], "(k p) t -> p k t", p=128),
                  reads=("xbf",), writes=("xb",))
            p.dma("sync", ab3, rearr(aoc[:, sl], "(k p) t -> p k t", p=128),
                  reads=("abf",), writes=("ab",))
            for mo in range(KC):
                for k in range(KC):
                    p.op("tensor", lambda e, mo=mo, k=k: e.matmul(
                        psA[:, 0:TSUB], wo[:, k * D + mo * 128:k * D + (mo + 1) * 128],
                        ab3[:, k, :], start=(k == 0), stop=(k == KC - 1)),
                        reads=("wo", "ab"), writes=("psA",))
                p.op("vector", lambda e, mo=mo: e.tensor_tensor(
                    rout3[:, mo, :], psA[:, 0:TSUB], xb3[:, mo, :], ALU.add),
                    reads=("psA", "xb"), writes=(f"rout{mo}",))
            p.op("vector", lambda e: e.tensor_copy(sq[:, 0:1], xb3[:, 0, 0:1]),
                 reads=("xb", "ab", "sqbuf") + tuple(f"rout{mo}" for mo in range(KC)),
                 writes=("xbf", "abf", "sqbuf"))
            emit_ln(p, rout3, KC, TSUB, ones, onesrow, sq, stats, bcast, psS, psS2,
                    tag=f"ln{s}", out_keys=[f"rout{mo}" for mo in range(KC)])
            p.dma("sync", rearr(r1[:, sl], "(k p) t -> p k t", p=128), rout3,
                  reads=tuple(f"rout{mo}" for mo in range(KC)), writes=())
        p.emit(nc)
    st.close()
    return nc


def run_l0post(xT, ao, W_O):
    if "l0post" not in _cache:
        _cache["l0post"] = mk_l0post()
    nc = _cache["l0post"]
    in_maps = []
    for c in range(NCORE):
        sl = slice(c * TPC, (c + 1) * TPC)
        in_maps.append(dict(xc=np.ascontiguousarray(xT[:, sl]),
                            aoc=np.ascontiguousarray(ao[:, sl]),
                            WO=W_O))
    res = run_bass_kernel_spmd(nc, in_maps, core_ids=list(range(NCORE)))
    return np.concatenate([r["r1"] for r in res.results], axis=1)


# version 04
# speedup vs baseline: 1.0083x; 1.0083x over previous
from contextlib import ExitStack
import numpy as np
import ml_dtypes
import concourse.bass as bass
import concourse.mybir as mybir
from concourse.bass_utils import run_bass_kernel_spmd

F32 = mybir.dt.float32
BF16 = mybir.dt.bfloat16
AF = mybir.ActivationFunctionType
ALU = mybir.AluOpType
f = np.float32
BF = ml_dtypes.bfloat16

NCORE = 8
TOK = 16384
TPC = TOK // NCORE
D = 1024
SEQ = 8192

ENGINES = ("tensor", "vector", "scalar", "gpsimd", "sync")
DMA_Q = {"sync": "dma_sync", "gpsimd": "dma_gp", "scalar": "dma_act", "vector": "dma_dve"}


class Prog:

    def __init__(self):
        self.vins = []
        self.deps = []
        self.last_w = {}
        self.readers = {}

    def _record(self, veng, issue_eng, emitter, reads, writes, partial=False):
        vidx = len(self.vins)
        dep = set()
        for k in reads:
            for w in self.last_w.get(k, ()):
                dep.add(w)
        for k in writes:
            for w in self.last_w.get(k, ()):
                dep.add(w)
            for r in self.readers.get(k, ()):
                dep.add(r)
        self.vins.append((veng, issue_eng, emitter))
        self.deps.append(dep)
        for k in reads:
            self.readers.setdefault(k, []).append(vidx)
        for k in writes:
            if partial:
                self.last_w.setdefault(k, []).append(vidx)
            else:
                self.last_w[k] = [vidx]
            self.readers[k] = []
        return vidx

    def op(self, eng, emitter, reads=(), writes=()):
        return self._record(eng, eng, emitter, reads, writes)

    def dma(self, queue, out_ap, in_ap, reads=(), writes=(), partial=False, **kw):
        return self._record(
            DMA_Q[queue], queue,
            (lambda e: e.dma_start(out=out_ap, in_=in_ap, **kw)),
            reads, writes, partial=partial)

    SERIAL = False

    def emit(self, nc):
        n = len(self.vins)
        if Prog.SERIAL:
            for i in range(1, n):
                self.deps[i].add(i - 1)
        targets = set()
        for i in range(n):
            si = self.vins[i][0]
            for d in self.deps[i]:
                if self.vins[d][0] != si:
                    targets.add(d)
        for i in range(n):
            if self.vins[i][0].startswith("dma_"):
                targets.add(i)
        rank = {}
        counts = {}
        for i in range(n):
            if i in targets:
                ve = self.vins[i][0]
                counts[ve] = counts.get(ve, 0) + 1
                rank[i] = counts[ve]

        sem_cms = {ve: nc.semaphore(f"s_{ve}") for ve in counts}
        sems = {ve: cm.__enter__() for ve, cm in sem_cms.items()}

        per_eng = {e: [] for e in ENGINES}
        for i in range(n):
            per_eng[self.vins[i][1]].append(i)

        def emit_engine(e, eng_name):
            done = {}
            for i in per_eng[eng_name]:
                ve = self.vins[i][0]
                need = {}
                for d in self.deps[i]:
                    dve = self.vins[d][0]
                    if dve == ve:
                        continue
                    v = rank[d] * (16 if dve.startswith("dma_") else 1)
                    need[dve] = max(need.get(dve, 0), v)
                for dve in sorted(need):
                    v = need[dve]
                    if done.get(dve, 0) < v:
                        e.wait_ge(sems[dve], v)
                        done[dve] = v
                ins = self.vins[i][2](e)
                if i in targets:
                    ins.then_inc(sems[ve], 16 if ve.startswith("dma_") else 1)
                    if ve.startswith("dma_"):
                        e.wait_ge(sems[ve], rank[i] * 16)
                        done[ve] = max(done.get(ve, 0), rank[i] * 16)
            if eng_name == "sync":
                for ve, c in sorted(counts.items()):
                    if ve.startswith("dma_"):
                        e.wait_ge(sems[ve], c * 16)

        block_cm = nc.Block()
        block = block_cm.__enter__()

        @block.sync
        def _(e):
            emit_engine(e, "sync")

        @block.tensor
        def _(e):
            emit_engine(e, "tensor")

        @block.vector
        def _(e):
            emit_engine(e, "vector")

        @block.scalar
        def _(e):
            emit_engine(e, "scalar")

        @block.gpsimd
        def _(e):
            emit_engine(e, "gpsimd")

        block_cm.__exit__(None, None, None)
        for cm in sem_cms.values():
            cm.__exit__(None, None, None)


def rearr(ap, pat, **kw):
    return ap.rearrange(pat, **kw)



def emit_weight_load(p, w_dram, w_stage, w_bf16, kchunks, dout, key,
                     queue="gpsimd", stage_key="wstage", stage_cols=4096):
    per = max(1, stage_cols // dout)
    for k0 in range(0, kchunks, per):
        k1 = min(k0 + per, kchunks)
        nk = k1 - k0
        p.dma(queue,
              rearr(w_stage[:, 0:nk * dout], "p (k d) -> p k d", k=nk),
              rearr(w_dram[k0 * 128:k1 * 128, :], "(k p) d -> p k d", p=128),
              reads=(), writes=(stage_key,))
        p.op("vector", lambda e, k0=k0, nk=nk: e.tensor_copy(
            w_bf16[:, k0 * dout:(k0 + nk) * dout], w_stage[:, 0:nk * dout]),
            reads=(stage_key,), writes=(key,))


def emit_ln(p, x3, KC, T, ones, onesrow, sq, stats, bcast, psS, psS2, tag, out_keys,
            gamma=None, beta=None, gb_key="gb", out3=None, out_keys2=None, eps=1e-5):
    D_ = KC * 128
    segs = [(a, min(a + 512, T)) for a in range(0, T, 512)]
    for k in range(KC):
        for (a, b) in segs:
            p.op("tensor", lambda e, k=k, a=a, b=b: e.matmul(
                psS[0:1, a:b], ones[:, 0:1], x3[:, k, a:b], start=(k == 0), stop=(k == KC - 1)),
                reads=("ones", out_keys[k]), writes=(tag + "psS", "psS"))
    p.op("vector", lambda e: e.tensor_scalar(stats[0:1, 0:T], psS[0:1, 0:T],
                                             1.0 / D_, None, ALU.mult),
         reads=(tag + "psS", "psS", "statsbuf"), writes=(tag + "mean", "statsbuf"))
    for (a, b) in segs:
        p.op("tensor", lambda e, a=a, b=b: e.matmul(psS[:, a:b], onesrow[0:1, :], stats[0:1, a:b],
                                                    start=True, stop=True),
             reads=("ones", tag + "mean"), writes=(tag + "psS", "psS"))
    p.op("vector", lambda e: e.tensor_copy(bcast[:, 0:T], psS[:, 0:T]),
         reads=(tag + "psS", "psS", "bcastbuf"), writes=(tag + "bmean", "bcastbuf"))
    for k in range(KC):
        p.op("vector", lambda e, k=k: e.tensor_tensor(
            x3[:, k, :], x3[:, k, :], bcast[:, 0:T], ALU.subtract),
            reads=(out_keys[k], tag + "bmean"), writes=(out_keys[k],))
    for k in range(KC):
        p.op("scalar", lambda e, k=k: e.activation(sq[:, 0:T], x3[:, k, :], AF.Square),
             reads=(out_keys[k], "sqbuf"), writes=(tag + "sq", "sqbuf"))
        for (a, b) in segs:
            p.op("tensor", lambda e, k=k, a=a, b=b: e.matmul(
                psS2[0:1, a:b], ones[:, 0:1], sq[:, a:b], start=(k == 0), stop=(k == KC - 1)),
                reads=("ones", tag + "sq"), writes=(tag + "psS2", "psS2"))
    p.op("scalar", lambda e: e.activation(stats[0:1, T:2 * T], psS2[0:1, 0:T],
                                          AF.Ln, bias=f(eps), scale=f(1.0 / D_)),
         reads=(tag + "psS2", "psS2", "statsbuf"), writes=(tag + "lnv", "statsbuf"))
    p.op("scalar", lambda e: e.activation(stats[0:1, 2 * T:3 * T], stats[0:1, T:2 * T],
                                          AF.Exp, scale=f(-0.5)),
         reads=(tag + "lnv", "statsbuf"), writes=(tag + "rstd", "statsbuf"))
    for (a, b) in segs:
        p.op("tensor", lambda e, a=a, b=b: e.matmul(psS2[:, a:b], onesrow[0:1, :],
                                                    stats[0:1, 2 * T + a:2 * T + b],
                                                    start=True, stop=True),
             reads=("ones", tag + "rstd"), writes=(tag + "psS2", "psS2"))
    p.op("vector", lambda e: e.tensor_copy(bcast[:, T:2 * T], psS2[:, 0:T]),
         reads=(tag + "psS2", "psS2", "bcastbuf"), writes=(tag + "brstd", "bcastbuf"))
    for k in range(KC):
        p.op("vector", lambda e, k=k: e.tensor_tensor(
            x3[:, k, :], x3[:, k, :], bcast[:, T:2 * T], ALU.mult),
            reads=(out_keys[k], tag + "brstd"), writes=(out_keys[k],))
        if gamma is not None:
            p.op("vector", lambda e, k=k: e.tensor_scalar(
                out3[:, k, :], x3[:, k, :], gamma[:, k:k + 1], beta[:, k:k + 1],
                ALU.mult, ALU.add),
                reads=(out_keys[k], gb_key), writes=(out_keys2[k],))



def mk_l0ffn(T=TPC, TSUB=512):
    KC = D // 128
    nsub = T // TSUB
    nc = bass.Bass("TRN2", target_bir_lowering=False, debug=False)
    r1 = nc.dram_tensor("r1", [D, 1 + T], F32, kind="ExternalInput").ap()
    WrT = nc.dram_tensor("WrT", [D, D], F32, kind="ExternalInput").ap()
    WkT = nc.dram_tensor("WkT", [D, D], F32, kind="ExternalInput").ap()
    WvT = nc.dram_tensor("WvT", [D, D], F32, kind="ExternalInput").ap()
    mur = nc.dram_tensor("mur", [D, 1], F32, kind="ExternalInput").ap()
    muk = nc.dram_tensor("muk", [D, 1], F32, kind="ExternalInput").ap()
    r2 = nc.dram_tensor("r2", [D, T], F32, kind="ExternalOutput").ap()

    st = ExitStack()
    xr1 = st.enter_context(nc.sbuf_tensor("xr1", [128, KC * (1 + T)], F32))
    wst = st.enter_context(nc.sbuf_tensor("wst", [128, 4096], F32))
    wr = st.enter_context(nc.sbuf_tensor("wr", [128, KC * D], BF16))
    wk = st.enter_context(nc.sbuf_tensor("wk", [128, KC * D], BF16))
    wv = st.enter_context(nc.sbuf_tensor("wv", [128, KC * D], BF16))
    smu = st.enter_context(nc.sbuf_tensor("smu", [128, 2 * KC], F32))
    xrb = st.enter_context(nc.sbuf_tensor("xrb", [128, KC * TSUB], BF16))
    xkb = st.enter_context(nc.sbuf_tensor("xkb", [128, KC * TSUB], BF16))
    dwork = st.enter_context(nc.sbuf_tensor("dwork", [128, TSUB], F32))
    rsig = st.enter_context(nc.sbuf_tensor("rsig", [128, KC * TSUB], F32))
    kact = st.enter_context(nc.sbuf_tensor("kact", [128, KC * TSUB], BF16))
    rout = st.enter_context(nc.sbuf_tensor("rout", [128, KC * TSUB], F32))
    ones = st.enter_context(nc.sbuf_tensor("ones", [128, 1], F32))
    onesrow = st.enter_context(nc.sbuf_tensor("onesrow", [1, 128], F32))
    sq = st.enter_context(nc.sbuf_tensor("sq", [128, TSUB], F32))
    stats = st.enter_context(nc.sbuf_tensor("stats", [1, 4 * TSUB], F32))
    bcast = st.enter_context(nc.sbuf_tensor("bcast", [128, 2 * TSUB], F32))
    psA = st.enter_context(nc.psum_tensor("psA", [128, 512], F32))
    psB = st.enter_context(nc.psum_tensor("psB", [128, 512], F32))
    psC = st.enter_context(nc.psum_tensor("psC", [128, 512], F32))
    psS = st.enter_context(nc.psum_tensor("psS", [128, 512], F32))
    psS2 = st.enter_context(nc.psum_tensor("psS2", [128, 512], F32))
    if True:
        p = Prog()
        x1 = rearr(xr1[:], "p (k t) -> p k t", k=KC)

        p.dma("sync", x1, rearr(r1, "(k p) t -> p k t", p=128), writes=("r1",))
        p.dma("sync", smu[:, 0:KC].unsqueeze(2),
              rearr(mur, "(k p) o -> p k o", p=128), writes=("mu",),
              allow_slow_non_contiguous=True)
        p.dma("sync", smu[:, KC:2 * KC].unsqueeze(2),
              rearr(muk, "(k p) o -> p k o", p=128), writes=("mu",),
              allow_slow_non_contiguous=True)
        emit_weight_load(p, WrT, wst, wr, KC, D, "wr")
        emit_weight_load(p, WkT, wst, wk, KC, D, "wk")
        emit_weight_load(p, WvT, wst, wv, KC, D, "wv")
        p.op("gpsimd", lambda e: e.memset(ones[:], 1.0), writes=("ones",))
        p.op("gpsimd", lambda e: e.memset(onesrow[:], 1.0), writes=("ones",))

        xrb3 = rearr(xrb[:], "p (k t) -> p k t", k=KC)
        xkb3 = rearr(xkb[:], "p (k t) -> p k t", k=KC)
        rsig3 = rearr(rsig[:], "p (k t) -> p k t", k=KC)
        kact3 = rearr(kact[:], "p (k t) -> p k t", k=KC)
        rout3 = rearr(rout[:], "p (k t) -> p k t", k=KC)

        for s in range(nsub):
            base = 1 + s * TSUB
            for k in range(KC):
                cur = x1[:, k, base:base + TSUB]
                prv = x1[:, k, base - 1:base - 1 + TSUB]
                p.op("vector", lambda e, cur=cur, prv=prv: e.tensor_tensor(
                    dwork[:], cur, prv, ALU.subtract), reads=("r1",), writes=("dw",))
                p.op("vector", lambda e, k=k, prv=prv: e.scalar_tensor_tensor(
                    xrb3[:, k, :], dwork[:], smu[:, k:k + 1], prv, ALU.mult, ALU.add),
                    reads=("dw", "mu", "r1"), writes=(f"xrb{k}",))
                p.op("vector", lambda e, k=k, prv=prv: e.scalar_tensor_tensor(
                    xkb3[:, k, :], dwork[:], smu[:, KC + k:KC + k + 1], prv, ALU.mult, ALU.add),
                    reads=("dw", "mu", "r1"), writes=(f"xkb{k}",))
            for m in range(KC):
                for k in range(KC):
                    p.op("tensor", lambda e, m=m, k=k: e.matmul(
                        psA[:, 0:TSUB], wr[:, k * D + m * 128:k * D + (m + 1) * 128],
                        xrb3[:, k, :], start=(k == 0), stop=(k == KC - 1)),
                        reads=("wr", f"xrb{k}"), writes=("psA",))
                p.op("scalar", lambda e, m=m: e.activation(
                    rsig3[:, m, :], psA[:, 0:TSUB], AF.Sigmoid),
                    reads=("psA",), writes=(f"rsig{m}",))
                for k in range(KC):
                    p.op("tensor", lambda e, m=m, k=k: e.matmul(
                        psB[:, 0:TSUB], wk[:, k * D + m * 128:k * D + (m + 1) * 128],
                        xkb3[:, k, :], start=(k == 0), stop=(k == KC - 1)),
                        reads=("wk", f"xkb{k}"), writes=("psB",))
                p.op("vector", lambda e, m=m: e.tensor_scalar(
                    sq[:], psB[:, 0:TSUB], 0.0, None, ALU.max),
                    reads=("psB", "sqbuf"), writes=("sqv", "sqbuf"))
                p.op("scalar", lambda e, m=m: e.activation(
                    kact3[:, m, :], sq[:], AF.Square),
                    reads=("sqv",), writes=(f"kact{m}",))
            for m in range(KC):
                for k in range(KC):
                    p.op("tensor", lambda e, m=m, k=k: e.matmul(
                        psC[:, 0:TSUB], wv[:, k * D + m * 128:k * D + (m + 1) * 128],
                        kact3[:, k, :], start=(k == 0), stop=(k == KC - 1)),
                        reads=("wv", f"kact{k}"), writes=("psC",))
                p.op("vector", lambda e, m=m: e.tensor_tensor(
                    rout3[:, m, :], rsig3[:, m, :], psC[:, 0:TSUB], ALU.mult),
                    reads=("psC", f"rsig{m}"), writes=(f"rout{m}",))
                p.op("vector", lambda e, m=m, base=base: e.tensor_tensor(
                    rout3[:, m, :], rout3[:, m, :], x1[:, m, base:base + TSUB], ALU.add),
                    reads=("r1", f"rout{m}"), writes=(f"rout{m}",))
            emit_ln(p, rout3, KC, TSUB, ones, onesrow, sq, stats, bcast, psS, psS2,
                    tag=f"ln{s}", out_keys=[f"rout{m}" for m in range(KC)])
            p.dma("sync", rearr(r2[:, s * TSUB:(s + 1) * TSUB], "(k p) t -> p k t", p=128),
                  rout3, reads=tuple(f"rout{m}" for m in range(KC)), writes=())
        p.emit(nc)
    st.close()
    return nc



_cache = {}


def run_l0ffn(r1T, WrT, WkT, WvT, mu_r, mu_k):
    if "l0ffn" not in _cache:
        _cache["l0ffn"] = mk_l0ffn()
    nc = _cache["l0ffn"]
    halo = np.concatenate([np.zeros((D, 1), np.float32), r1T], axis=1)
    in_maps = []
    for c in range(NCORE):
        t0 = c * TPC
        chunk = np.ascontiguousarray(halo[:, t0:t0 + TPC + 1])
        if c % 4 == 0:
            chunk[:, 0] = 0.0
        in_maps.append(dict(
            r1=chunk, WrT=WrT, WkT=WkT, WvT=WvT,
            mur=np.ascontiguousarray(mu_r.reshape(D, 1)),
            muk=np.ascontiguousarray(mu_k.reshape(D, 1))))
    res = run_bass_kernel_spmd(nc, in_maps, core_ids=list(range(NCORE)))
    return np.concatenate([r["r2"] for r in res.results], axis=1)


def kernel(**inputs):
    inp = inputs
    x = inp["L0_attn_x"]
    xT = np.ascontiguousarray(x.reshape(16384, 1024).T)
    ao = run_l0attn(xT, inp)
    r1T = run_l0post(xT, ao, inp["L0_attn_W_O"])
    r2T = run_l0ffn(r1T,
                    np.ascontiguousarray(inp["L0_ffn_W_r"].T),
                    np.ascontiguousarray(inp["L0_ffn_W_k"].T),
                    np.ascontiguousarray(inp["L0_ffn_W_v"].T),
                    inp["L0_ffn_mu_r"], inp["L0_ffn_mu_k"])
    ssm, dt, zz, Bn, Cn = run_mambapre(r2T, inp)
    y = run_scan(ssm, dt, Bn, Cn, inp["L1_attn_A_log"], inp["L1_attn_D_param"])
    r3T = run_mambapost(y, zz, r2T, inp["L1_attn_W_out"])
    r4T = run_l1ffn(r3T, inp, layer=0)
    outT = run_l1ffn(r4T, inp, layer=1, r3T=r3T)
    return np.ascontiguousarray(outT.T).reshape(2, 8192, 1024).astype(np.float32)



def mk_mambapre(T=TPC, TSUB=512):
    KC = D // 128
    MC = 2048 // 128
    nsub = T // TSUB
    nc = bass.Bass("TRN2", target_bir_lowering=False, debug=False)
    r2 = nc.dram_tensor("r2", [D, 3 + T], F32, kind="ExternalInput").ap()
    WinT = nc.dram_tensor("WinT", [D, 4096], F32, kind="ExternalInput").ap()
    gam = nc.dram_tensor("gam", [D, 1], F32, kind="ExternalInput").ap()
    bet = nc.dram_tensor("bet", [D, 1], F32, kind="ExternalInput").ap()
    cw = nc.dram_tensor("cw", [2048, 4], F32, kind="ExternalInput").ap()
    cb = nc.dram_tensor("cb", [2048, 1], F32, kind="ExternalInput").ap()
    WxT = nc.dram_tensor("WxT", [2048, 33], F32, kind="ExternalInput").ap()
    WdtT = nc.dram_tensor("WdtT", [33, 2048], F32, kind="ExternalInput").ap()
    bdt = nc.dram_tensor("bdt", [2048, 1], F32, kind="ExternalInput").ap()
    hm = nc.dram_tensor("hm", [1, 3], F32, kind="ExternalInput").ap()
    ssm_o = nc.dram_tensor("ssm_o", [2048, T], F32, kind="ExternalOutput").ap()
    dt_o = nc.dram_tensor("dt_o", [2048, T], F32, kind="ExternalOutput").ap()
    z_o = nc.dram_tensor("z_o", [2048, T], F32, kind="ExternalOutput").ap()
    B_o = nc.dram_tensor("B_o", [16, T], F32, kind="ExternalOutput").ap()
    C_o = nc.dram_tensor("C_o", [16, T], F32, kind="ExternalOutput").ap()

    st = ExitStack()
    r2c = st.enter_context(nc.sbuf_tensor("r2c", [128, KC * (TSUB + 3)], F32))
    xnb = st.enter_context(nc.sbuf_tensor("xnb", [128, KC * (TSUB + 3)], BF16))
    wst = st.enter_context(nc.sbuf_tensor("wst", [128, 4096], F32))
    wib = st.enter_context(nc.sbuf_tensor("wib", [128, 2, KC * 512], BF16))
    sgb = st.enter_context(nc.sbuf_tensor("sgb", [128, 2 * KC], F32))
    scw = st.enter_context(nc.sbuf_tensor("scw", [128, MC * 4], F32))
    scb = st.enter_context(nc.sbuf_tensor("scb", [128, MC], F32))
    sbdt = st.enter_context(nc.sbuf_tensor("sbdt", [128, MC], F32))
    swx = st.enter_context(nc.sbuf_tensor("swx", [128, MC * 33], BF16))
    swdt = st.enter_context(nc.sbuf_tensor("swdt", [33, 2048], BF16))
    shm = st.enter_context(nc.sbuf_tensor("shm", [128, 3], F32))
    xsp = st.enter_context(nc.sbuf_tensor("xsp", [128, MC * (TSUB + 3)], F32))
    ssm = st.enter_context(nc.sbuf_tensor("ssm", [128, MC * TSUB], F32))
    ssmb = st.enter_context(nc.sbuf_tensor("ssmb", [128, MC * TSUB], BF16))
    zt = st.enter_context(nc.sbuf_tensor("zt", [128, 2, TSUB], F32))
    dtt = st.enter_context(nc.sbuf_tensor("dtt", [128, 2, TSUB], F32))
    dtw = st.enter_context(nc.sbuf_tensor("dtw", [128, 3, TSUB], F32))
    xp = st.enter_context(nc.sbuf_tensor("xp", [64, TSUB], F32))
    xpb = st.enter_context(nc.sbuf_tensor("xpb", [64, TSUB], BF16))
    ones = st.enter_context(nc.sbuf_tensor("ones", [128, 1], F32))
    onesrow = st.enter_context(nc.sbuf_tensor("onesrow", [1, 128], F32))
    sq = st.enter_context(nc.sbuf_tensor("sq", [128, TSUB + 3], F32))
    stats = st.enter_context(nc.sbuf_tensor("stats", [1, 4 * (TSUB + 3)], F32))
    bcast = st.enter_context(nc.sbuf_tensor("bcast", [128, 2 * (TSUB + 3)], F32))
    psA = st.enter_context(nc.psum_tensor("psA", [128, 512], F32))
    psB = st.enter_context(nc.psum_tensor("psB", [128, 512], F32))
    psS = st.enter_context(nc.psum_tensor("psS", [128, 512], F32))
    psS2 = st.enter_context(nc.psum_tensor("psS2", [128, 512], F32))
    if True:
        p = Prog()
        r2c3 = rearr(r2c[:], "p (k t) -> p k t", k=KC)
        xnb3 = rearr(xnb[:], "p (k t) -> p k t", k=KC)
        xsp3 = rearr(xsp[:], "p (k t) -> p k t", k=MC)
        ssm3 = rearr(ssm[:], "p (k t) -> p k t", k=MC)
        ssmb3 = rearr(ssmb[:], "p (k t) -> p k t", k=MC)

        p.dma("sync", sgb[:, 0:KC].unsqueeze(2), rearr(gam, "(k p) o -> p k o", p=128),
              writes=("gb",), allow_slow_non_contiguous=True)
        p.dma("sync", sgb[:, KC:2 * KC].unsqueeze(2), rearr(bet, "(k p) o -> p k o", p=128),
              writes=("gb",), allow_slow_non_contiguous=True)
        p.dma("sync", rearr(scw[:], "p (k j) -> p k j", k=MC), rearr(cw, "(k p) j -> p k j", p=128),
              writes=("cw",))
        p.dma("sync", scb[:].unsqueeze(2), rearr(cb, "(k p) o -> p k o", p=128),
              writes=("cw",), allow_slow_non_contiguous=True)
        p.dma("sync", sbdt[:].unsqueeze(2), rearr(bdt, "(k p) o -> p k o", p=128),
              writes=("cw",), allow_slow_non_contiguous=True)
        p.dma("sync", shm[:], hm.partition_broadcast(128), writes=("hm",))
        p.op("gpsimd", lambda e: e.memset(ones[:], 1.0), writes=("ones",))
        p.op("gpsimd", lambda e: e.memset(onesrow[:], 1.0), writes=("ones",))
        emit_weight_load(p, WxT, wst, swx, MC, 33, "wx")
        p.dma("sync", wst[0:33, 0:2048], WdtT, writes=("wstage",))
        p.op("vector", lambda e: e.tensor_copy(swdt[:], wst[0:33, 0:2048]),
             reads=("wstage",), writes=("wdt",))

        for s in range(nsub):
            par = s % 2
            p.dma("sync", r2c3, rearr(r2[:, s * TSUB:s * TSUB + TSUB + 3], "(k p) t -> p k t", p=128),
                  reads=("r2c_free",), writes=("r2c",))
            emit_ln(p, r2c3[:, :, 0:3], KC, 3, ones, onesrow, sq, stats, bcast, psS, psS2,
                    tag=f"mlnh{s}", out_keys=["r2c"] * KC,
                    gamma=sgb[:, 0:KC], beta=sgb[:, KC:2 * KC], gb_key="gb",
                    out3=xnb3[:, :, 0:3], out_keys2=[f"xnb{k}" for k in range(KC)])
            emit_ln(p, r2c3[:, :, 3:3 + TSUB], KC, TSUB, ones, onesrow, sq, stats, bcast, psS, psS2,
                    tag=f"mln{s}", out_keys=["r2c"] * KC,
                    gamma=sgb[:, 0:KC], beta=sgb[:, KC:2 * KC], gb_key="gb",
                    out3=xnb3[:, :, 3:3 + TSUB], out_keys2=[f"xnb{k}" for k in range(KC)])
            p.op("vector", lambda e: e.tensor_copy(sq[:, 0:1], r2c3[:, 0, 0:1]),
                 reads=("r2c", "sqbuf") + tuple(f"xnb{k}" for k in range(KC)),
                 writes=("r2c_free", "sqbuf"))
            for mg in range(8):
                gpar = mg % 2
                p.dma("gpsimd", rearr(wst[:], "p (k d) -> p k d", k=KC),
                      rearr(WinT[:, mg * 512:(mg + 1) * 512], "(k p) d -> p k d", p=128),
                      reads=(f"wib{gpar}",), writes=("wstage",)) if s == 0 else None
                if s == 0:
                    p.op("vector", lambda e, gpar=gpar: e.tensor_copy(
                        wib[:, gpar, :], wst[:]), reads=("wstage",), writes=(f"wib{gpar}",))
                if s > 0:
                    p.dma("gpsimd", rearr(wst[:], "p (k d) -> p k d", k=KC),
                          rearr(WinT[:, mg * 512:(mg + 1) * 512], "(k p) d -> p k d", p=128),
                          reads=(f"wib{gpar}",), writes=("wstage",))
                    p.op("vector", lambda e, gpar=gpar: e.tensor_copy(
                        wib[:, gpar, :], wst[:]), reads=("wstage",), writes=(f"wib{gpar}",))
                for mi in range(4):
                    m = mg * 4 + mi
                    for k in range(KC):
                        p.op("tensor", lambda e, gpar=gpar, mi=mi, k=k: e.matmul(
                            psA[:, 0:TSUB],
                            wib[:, gpar, k * 512 + mi * 128:k * 512 + (mi + 1) * 128],
                            xnb3[:, k, 3:3 + TSUB], start=(k == 0), stop=(k == KC - 1)),
                            reads=(f"wib{gpar}",) + tuple(f"xnb{k}" for k in range(KC)),
                            writes=("psA",))
                    if m < MC:
                        if s == 0:
                            for k in range(KC):
                                p.op("tensor", lambda e, gpar=gpar, mi=mi, k=k: e.matmul(
                                    psB[:, 0:3],
                                    wib[:, gpar, k * 512 + mi * 128:k * 512 + (mi + 1) * 128],
                                    xnb3[:, k, 0:3], start=(k == 0), stop=(k == KC - 1)),
                                    reads=(f"wib{gpar}",) + tuple(f"xnb{k}" for k in range(KC)),
                                    writes=("psB",))
                            p.op("vector", lambda e, m=m: e.tensor_tensor(
                                xsp3[:, m, 0:3], psB[:, 0:3], shm[:], ALU.mult),
                                reads=("psB", "hm"), writes=(f"xsp{m}",))
                        p.op("vector", lambda e, m=m: e.tensor_copy(
                            xsp3[:, m, 3:3 + TSUB], psA[:, 0:TSUB]),
                            reads=("psA",), writes=(f"xsp{m}",))
                    else:
                        zpar = m % 2
                        p.op("vector", lambda e, zpar=zpar: e.tensor_copy(
                            zt[:, zpar, :], psA[:, 0:TSUB]),
                            reads=("psA", f"zt{zpar}"), writes=(f"zt{zpar}",))
                        p.dma("gpsimd", rearr(z_o[(m - MC) * 128:(m - MC + 1) * 128,
                                              s * TSUB:(s + 1) * TSUB], "p t -> p t"),
                              zt[:, zpar, :], reads=(f"zt{zpar}",), writes=())
            for m in range(MC):
                p.op("vector", lambda e, m=m: e.tensor_scalar(
                    sq[:, 0:TSUB], xsp3[:, m, 0:TSUB], scw[:, m * 4:m * 4 + 1], None, ALU.mult),
                    reads=(f"xsp{m}", "cw", "sqbuf"), writes=("cacc", "sqbuf"))
                for j in (1, 2, 3):
                    p.op("vector", lambda e, m=m, j=j: e.scalar_tensor_tensor(
                        sq[:, 0:TSUB], xsp3[:, m, j:j + TSUB], scw[:, m * 4 + j:m * 4 + j + 1],
                        sq[:, 0:TSUB], ALU.mult, ALU.add),
                        reads=(f"xsp{m}", "cw", "cacc", "sqbuf"), writes=("cacc", "sqbuf"))
                p.op("scalar", lambda e, m=m: e.activation(
                    ssm3[:, m, :], sq[:, 0:TSUB], AF.Silu, bias=scb[:, m:m + 1]),
                    reads=("cacc", "cw"), writes=(f"ssm{m}",))
                p.op("vector", lambda e, m=m: e.tensor_copy(ssmb3[:, m, :], ssm3[:, m, :]),
                     reads=(f"ssm{m}",), writes=(f"ssmb{m}",))
                p.dma("sync", ssm_o[m * 128:(m + 1) * 128, s * TSUB:(s + 1) * TSUB],
                      ssm3[:, m, :], reads=(f"ssm{m}",), writes=())
                p.op("vector", lambda e, m=m: e.tensor_copy(
                    xsp3[:, m, 0:3], xsp3[:, m, TSUB:TSUB + 3]),
                    reads=(f"xsp{m}",), writes=(f"xsp{m}",))
            for k in range(MC):
                p.op("tensor", lambda e, k=k: e.matmul(
                    psB[0:33, 0:TSUB], swx[:, k * 33:(k + 1) * 33], ssmb3[:, k, :],
                    start=(k == 0), stop=(k == MC - 1)),
                    reads=("wx", f"ssmb{k}"), writes=("psB",))
            p.op("vector", lambda e: e.tensor_copy(xp[0:33, :], psB[0:33, 0:TSUB]),
                 reads=("psB",), writes=("xp",))
            p.op("vector", lambda e: e.tensor_copy(xpb[0:33, :], xp[0:33, :]),
                 reads=("xp",), writes=("xpb",))
            p.dma("sync", B_o[:, s * TSUB:(s + 1) * TSUB], xp[1:17, :], reads=("xp",), writes=())
            p.dma("sync", C_o[:, s * TSUB:(s + 1) * TSUB], xp[17:33, :], reads=("xp",), writes=())
            for m in range(MC):
                p.op("tensor", lambda e, m=m: e.matmul(
                    psA[:, 0:TSUB], swdt[0:33, m * 128:(m + 1) * 128], xpb[0:33, :],
                    start=True, stop=True),
                    reads=("wdt", "xpb"), writes=("psA",))
                p.op("vector", lambda e, m=m: e.tensor_scalar(
                    dtw[:, 0, :], psA[:, 0:TSUB], sbdt[:, m:m + 1], None, ALU.add),
                    reads=("psA", "cw"), writes=("dtw0",))
                p.op("scalar", lambda e: e.activation(dtw[:, 1, :], dtw[:, 0, :], AF.Abs),
                     reads=("dtw0",), writes=("dtw1",))
                p.op("scalar", lambda e: e.activation(dtw[:, 2, :], dtw[:, 1, :], AF.Exp, scale=f(-1.0)),
                     reads=("dtw1",), writes=("dtw2",))
                p.op("scalar", lambda e: e.activation(dtw[:, 1, :], dtw[:, 2, :], AF.Ln, bias=f(1.0)),
                     reads=("dtw2",), writes=("dtw1",))
                p.op("vector", lambda e: e.tensor_scalar(dtw[:, 2, :], dtw[:, 0, :], 0.0, None, ALU.max),
                     reads=("dtw0",), writes=("dtw2",))
                dpar = m % 2
                p.op("vector", lambda e, dpar=dpar: e.scalar_tensor_tensor(
                    dtt[:, dpar, :], dtw[:, 2, :], 0.001, dtw[:, 1, :], ALU.add, ALU.add),
                    reads=("dtw2", "dtw1", f"dtt{dpar}"), writes=(f"dtt{dpar}",))
                p.dma("gpsimd", dt_o[m * 128:(m + 1) * 128, s * TSUB:(s + 1) * TSUB],
                      dtt[:, dpar, :], reads=(f"dtt{dpar}",), writes=())
        p.emit(nc)
    st.close()
    return nc


def run_mambapre(r2T, inp):
    if "mambapre" not in _cache:
        _cache["mambapre"] = mk_mambapre()
    nc = _cache["mambapre"]
    halo = np.concatenate([np.zeros((D, 3), np.float32), r2T], axis=1)
    WinT = np.ascontiguousarray(inp["L1_attn_W_in"].T)
    WxT = np.ascontiguousarray(inp["L1_attn_W_xproj"].T)
    WdtT = np.ascontiguousarray(inp["L1_attn_W_dt"].T)
    in_maps = []
    for c in range(NCORE):
        t0 = c * TPC
        chunk = np.ascontiguousarray(halo[:, t0:t0 + TPC + 3])
        m = np.ones((1, 3), np.float32)
        if c % 4 == 0:
            chunk[:, 0:3] = 0.0
            m[:] = 0.0
        in_maps.append(dict(
            r2=chunk, WinT=WinT,
            gam=np.ascontiguousarray(inp["L1_attn_ln_gamma"].reshape(D, 1)),
            bet=np.ascontiguousarray(inp["L1_attn_ln_beta"].reshape(D, 1)),
            cw=np.ascontiguousarray(inp["L1_attn_conv_w"][:, 0, :]),
            cb=np.ascontiguousarray(inp["L1_attn_conv_b"].reshape(2048, 1)),
            WxT=WxT, WdtT=WdtT,
            bdt=np.ascontiguousarray(inp["L1_attn_b_dt"].reshape(2048, 1)),
            hm=m))
    res = run_bass_kernel_spmd(nc, in_maps, core_ids=list(range(NCORE)))
    ssm = np.concatenate([r["ssm_o"] for r in res.results], axis=1)
    dt = np.concatenate([r["dt_o"] for r in res.results], axis=1)
    zz = np.concatenate([r["z_o"] for r in res.results], axis=1)
    Bn = np.concatenate([r["B_o"] for r in res.results], axis=1)
    Cn = np.concatenate([r["C_o"] for r in res.results], axis=1)
    return ssm, dt, zz, Bn, Cn



DBG_SCAN = False


def mk_scan(L=SEQ, TP=128):
    nch = 64
    nc = bass.Bass("TRN2", target_bir_lowering=False, debug=False)
    ssm = nc.dram_tensor("ssm", [256, 2 * L], F32, kind="ExternalInput").ap()
    dts = nc.dram_tensor("dts", [256, 2 * L], F32, kind="ExternalInput").ap()
    Bn = nc.dram_tensor("Bn", [16, 2 * L], F32, kind="ExternalInput").ap()
    Cn = nc.dram_tensor("Cn", [16, 2 * L], F32, kind="ExternalInput").ap()
    Alog = nc.dram_tensor("Alog", [256, 16], F32, kind="ExternalInput").ap()
    Dp = nc.dram_tensor("Dp", [256, 1], F32, kind="ExternalInput").ap()
    y_o = nc.dram_tensor("y_o", [256, 2 * L], F32, kind="ExternalOutput").ap()
    dbg = nc.dram_tensor("dbg", [128, 7 * 64 * TP], F32, kind="ExternalOutput").ap() if DBG_SCAN else None

    from concourse.ap import AP as RawAP

    def chunk_ap(t_ap, t0, inner, TPc):
        if inner == 4:
            return RawAP(t_ap.tensor, t0, [[L, 2], [4 * 2 * L, 64], [2 * L, 4], [1, TPc]])
        else:
            return RawAP(t_ap.tensor, t0, [[0, 64], [2 * L, 16], [1, TPc]])
    Alog3 = RawAP(Alog.tensor, 0, [[0, 2], [64, 64], [1, 64]])
    Dp3 = RawAP(Dp.tensor, 0, [[0, 2], [4, 64], [1, 4]])

    st = ExitStack()
    atile = st.enter_context(nc.sbuf_tensor("atile", [128, 64], F32))
    ninva = st.enter_context(nc.sbuf_tensor("ninva", [128, 64], F32))
    dtile = st.enter_context(nc.sbuf_tensor("dtile", [128, 4], F32))
    tmp64 = st.enter_context(nc.sbuf_tensor("tmp64", [128, 64], F32))
    dtc_b = [st.enter_context(nc.sbuf_tensor(f"dtc{i}", [128, 4 * TP], F32)) for i in range(2)]
    xc_b = [st.enter_context(nc.sbuf_tensor(f"xcb{i}", [128, 4 * TP], F32)) for i in range(2)]
    brep_b = [st.enter_context(nc.sbuf_tensor(f"brep{i}", [128, 16 * TP], F32)) for i in range(2)]
    crep_b = [st.enter_context(nc.sbuf_tensor(f"crep{i}", [128, 16 * TP], F32)) for i in range(2)]
    ybuf_b = [st.enter_context(nc.sbuf_tensor(f"ybuf{i}", [128, 4 * TP], F32)) for i in range(2)]
    abuf = st.enter_context(nc.sbuf_tensor("abuf", [128, 64 * TP], F32))
    ubuf = st.enter_context(nc.sbuf_tensor("ubuf", [128, 64 * TP], F32))
    obuf = st.enter_context(nc.sbuf_tensor("obuf", [128, 64 * TP], F32))
    hist = st.enter_context(nc.sbuf_tensor("hist", [128, 64 * TP], F32))
    ysm = st.enter_context(nc.sbuf_tensor("ysm", [128, 4 * TP], F32))
    ydx = st.enter_context(nc.sbuf_tensor("ydx", [128, 4 * TP], F32))
    if True:
        p = Prog()
        p.dma("sync", atile[:], Alog3, writes=("alog",))
        p.dma("sync", dtile[:], Dp3, writes=("dtile",))
        p.op("scalar", lambda e: e.activation(atile[:], atile[:], AF.Exp),
             reads=("alog",), writes=("alog",))
        p.op("vector", lambda e: e.tensor_scalar(atile[:], atile[:], -1.0, None, ALU.mult),
             reads=("alog",), writes=("atile",))
        p.op("vector", lambda e: e.memset(hist[:, (TP - 1) * 64:TP * 64], 0.0),
             writes=("hist",))
        p.op("vector", lambda e: e.memset(obuf[:], 0.0), writes=("omd",))
        p.op("vector", lambda e: e.tensor_scalar(tmp64[:], atile[:], 1e-8, None, ALU.add),
             reads=("atile",), writes=("t64",))
        p.op("vector", lambda e: e.memset(ubuf[:], 0.0), writes=("ubuf",))
        p.op("vector", lambda e: e.reciprocal(ninva[:], tmp64[:]),
             reads=("t64",), writes=("ninva",))
        p.op("vector", lambda e: e.memset(abuf[:], 0.0), writes=("aexp",))
        p.op("vector", lambda e: e.tensor_scalar(ninva[:], ninva[:], -1.0, None, ALU.mult),
             reads=("ninva",), writes=("ninva",))
        p.op("vector", lambda e: e.memset(ysm[:], 0.0), writes=("ysm",))

        a4 = rearr(abuf[:], "p (t c n) -> p t c n", t=TP, c=4)
        u4 = rearr(ubuf[:], "p (t c n) -> p t c n", t=TP, c=4)
        o4 = rearr(obuf[:], "p (t c n) -> p t c n", t=TP, c=4)
        h4 = rearr(hist[:], "p (t c n) -> p t c n", t=TP, c=4)
        A_ap = rearr(atile[:], "p (c n) -> p c n", c=4).unsqueeze(1).broadcast_to([128, TP, 4, 16])
        NI_ap = rearr(ninva[:], "p (c n) -> p c n", c=4).unsqueeze(1).broadcast_to([128, TP, 4, 16])
        D_ap = dtile[:].unsqueeze(1).broadcast_to([128, TP, 4])

        for c in range(nch):
            par = c % 2
            t0 = c * TP
            p.dma("sync", rearr(dtc_b[par][:], "p (c t) -> p c t", c=4).opt(),
                  chunk_ap(dts, t0, 4, TP), reads=(f"dtcF{par}",), writes=(f"dtc{par}",))
            p.dma("sync", rearr(xc_b[par][:], "p (c t) -> p c t", c=4).opt(),
                  chunk_ap(ssm, t0, 4, TP), reads=(f"xcF{par}",), writes=(f"xc{par}",))
            for bb in range(2):
                p.dma("sync", rearr(brep_b[par][bb * 64:(bb + 1) * 64, :], "p (n t) -> p n t", n=16),
                      chunk_ap(Bn, bb * L + t0, 16, TP), reads=(f"brF{par}",), writes=(f"br{par}",), partial=True)
                p.dma("sync", rearr(crep_b[par][bb * 64:(bb + 1) * 64, :], "p (n t) -> p n t", n=16),
                      chunk_ap(Cn, bb * L + t0, 16, TP), reads=(f"crF{par}",), writes=(f"cr{par}",), partial=True)
            dt_ap = rearr(dtc_b[par][:], "p (c t) -> p t c", c=4).unsqueeze(3).broadcast_to([128, TP, 4, 16])
            x_ap = rearr(xc_b[par][:], "p (c t) -> p t c", c=4).unsqueeze(3).broadcast_to([128, TP, 4, 16])
            xtc_ap = rearr(xc_b[par][:], "p (c t) -> p t c", c=4)
            b_ap = rearr(brep_b[par][:], "p (n t) -> p t n", n=16).unsqueeze(2).broadcast_to([128, TP, 4, 16])
            c_ap = rearr(crep_b[par][:], "p (n t) -> p t n", n=16).unsqueeze(2).broadcast_to([128, TP, 4, 16])
            p.op("vector", lambda e, dt_ap=dt_ap: e.tensor_tensor(o4, A_ap, dt_ap, ALU.mult),
                 reads=("atile", f"dtc{par}", "omd", "och"), writes=("am", f"dtcF{par}"))
            p.op("scalar", lambda e: e.activation(abuf[:], obuf[:], AF.Exp),
                 reads=("am", "aexp"), writes=("aexp",))
            p.op("vector", lambda e, b_ap=b_ap, x_ap=x_ap: e.tensor_tensor(u4, b_ap, x_ap, ALU.mult),
                 reads=(f"br{par}", f"xc{par}", "ubuf", "uok"), writes=("ubuf", f"brF{par}"))
            p.op("vector", lambda e: e.tensor_tensor(u4, u4, NI_ap, ALU.mult),
                 reads=("ubuf", "ninva"), writes=("ubuf",))
            p.op("vector", lambda e: e.tensor_scalar(obuf[:], abuf[:], -1.0, None, ALU.add),
                 reads=("aexp", "am"), writes=("omd",))
            p.op("vector", lambda e: e.tensor_tensor(ubuf[:], ubuf[:], obuf[:], ALU.mult),
                 reads=("ubuf", "omd"), writes=("ubuf", "uok"))
            for t in range(TP):
                prev = hist[:, (t - 1) * 64:t * 64] if t > 0 else hist[:, (TP - 1) * 64:TP * 64]
                rd = ("aexp", "ubuf", "uok", "hist") if t == 0 else ()
                wr = ()
                if t == TP - 1:
                    rd = ("aexp", "ubuf", "uok")
                    wr = ("hist",)
                p.op("vector", lambda e, t=t, prev=prev: e.tensor_tensor(
                    tmp64[:], abuf[:, t * 64:(t + 1) * 64], prev, ALU.mult), reads=rd)
                p.op("vector", lambda e, t=t: e.tensor_tensor(
                    tmp64[:], tmp64[:], ubuf[:, t * 64:(t + 1) * 64], ALU.add))
                p.op("vector", lambda e, t=t: e.tensor_scalar(
                    hist[:, t * 64:(t + 1) * 64], tmp64[:], 10.0, -10.0, ALU.min, ALU.max),
                    writes=wr)
            if DBG_SCAN and c == 1:
                p.dma("sync", rearr(dbg[:, 0:64 * TP], "p f -> p f"), abuf[:], reads=("aexp",))
                p.dma("sync", rearr(dbg[:, 64 * TP:2 * 64 * TP], "p f -> p f"), ubuf[:], reads=("ubuf", "uok"))
                p.dma("sync", rearr(dbg[:, 2 * 64 * TP:3 * 64 * TP], "p f -> p f"), hist[:], reads=("hist",))
                p.dma("sync", rearr(dbg[:, 3 * 64 * TP:3 * 64 * TP + 4 * TP], "p f -> p f"),
                      dtc_b[par][:], reads=(f"dtc{par}",))
                p.dma("sync", rearr(dbg[:, 4 * 64 * TP:4 * 64 * TP + 4 * TP], "p f -> p f"),
                      xc_b[par][:], reads=(f"xc{par}",))
                p.dma("sync", rearr(dbg[:, 5 * 64 * TP:5 * 64 * TP + 16 * TP], "p f -> p f"),
                      brep_b[par][:], reads=(f"br{par}",))
                p.dma("sync", rearr(dbg[:, 6 * 64 * TP:6 * 64 * TP + 16 * TP], "p f -> p f"),
                      crep_b[par][:], reads=(f"cr{par}",))
            p.op("vector", lambda e, c_ap=c_ap: e.tensor_tensor(o4, h4, c_ap, ALU.mult),
                 reads=("hist", f"cr{par}", "omd"), writes=("och", f"crF{par}"))
            p.op("vector", lambda e: e.reduce_sum(
                ysm[:].unsqueeze(2),
                rearr(obuf[:], "p (tc n) -> p tc n", n=16),
                axis=mybir.AxisListType.X),
                reads=("och",), writes=("ysm",))
            p.op("vector", lambda e, xtc_ap=xtc_ap: e.tensor_tensor(
                rearr(ydx[:], "p (t c) -> p t c", t=TP), xtc_ap, D_ap, ALU.mult),
                reads=(f"xc{par}", "dtile", "ydx"), writes=("ydx", f"xcF{par}"))
            p.op("vector", lambda e, par=par: e.tensor_tensor(
                rearr(ybuf_b[par][:], "p (c t) -> p t c", c=4), 
                rearr(ysm[:], "p (t c) -> p t c", t=TP),
                rearr(ydx[:], "p (t c) -> p t c", t=TP), ALU.add),
                reads=("ysm", "ydx", f"ybF{par}"), writes=(f"yb{par}",))
            p.dma("gpsimd", chunk_ap(y_o, t0, 4, TP),
                  rearr(ybuf_b[par][:], "p (c t) -> p c t", c=4).opt(),
                  reads=(f"yb{par}",), writes=(f"ybF{par}",))
        p.emit(nc)
    st.close()
    return nc


def run_scan(ssm, dt, Bn, Cn, A_log, D_param):
    if "scan" not in _cache:
        _cache["scan"] = mk_scan()
    nc = _cache["scan"]
    in_maps = []
    for c in range(NCORE):
        ch0 = c * 256
        in_maps.append(dict(
            ssm=np.ascontiguousarray(ssm[ch0:ch0 + 256]),
            dts=np.ascontiguousarray(dt[ch0:ch0 + 256]),
            Bn=Bn, Cn=Cn,
            Alog=np.ascontiguousarray(A_log[ch0:ch0 + 256]),
            Dp=np.ascontiguousarray(D_param[ch0:ch0 + 256].reshape(256, 1))))
    res = run_bass_kernel_spmd(nc, in_maps, core_ids=list(range(NCORE)))
    return np.concatenate([r["y_o"] for r in res.results], axis=0)



def mk_mambapost(T=TPC, TSUB=512):
    KC = D // 128
    MC = 2048 // 128
    nsub = T // TSUB
    nc = bass.Bass("TRN2", target_bir_lowering=False, debug=False)
    yv = nc.dram_tensor("yv", [2048, T], F32, kind="ExternalInput").ap()
    zv = nc.dram_tensor("zv", [2048, T], F32, kind="ExternalInput").ap()
    r2 = nc.dram_tensor("r2", [D, T], F32, kind="ExternalInput").ap()
    WoT = nc.dram_tensor("WoT", [2048, D], F32, kind="ExternalInput").ap()
    r3 = nc.dram_tensor("r3", [D, T], F32, kind="ExternalOutput").ap()

    st = ExitStack()
    r2c = st.enter_context(nc.sbuf_tensor("r2c", [128, KC * TSUB], F32))
    yc = st.enter_context(nc.sbuf_tensor("yc", [128, MC * TSUB], F32))
    zc = st.enter_context(nc.sbuf_tensor("zc", [128, 2 * TSUB], F32))
    gb = st.enter_context(nc.sbuf_tensor("gb", [128, MC * TSUB], BF16))
    wst = st.enter_context(nc.sbuf_tensor("wst", [128, 4096], F32))
    wo = st.enter_context(nc.sbuf_tensor("wo", [128, MC * D], BF16))
    rout = st.enter_context(nc.sbuf_tensor("rout", [128, KC * TSUB], F32))
    ones = st.enter_context(nc.sbuf_tensor("ones", [128, 1], F32))
    onesrow = st.enter_context(nc.sbuf_tensor("onesrow", [1, 128], F32))
    sq = st.enter_context(nc.sbuf_tensor("sq", [128, TSUB], F32))
    stats = st.enter_context(nc.sbuf_tensor("stats", [1, 4 * TSUB], F32))
    bcast = st.enter_context(nc.sbuf_tensor("bcast", [128, 2 * TSUB], F32))
    psA = st.enter_context(nc.psum_tensor("psA", [128, 512], F32))
    psS = st.enter_context(nc.psum_tensor("psS", [128, 512], F32))
    psS2 = st.enter_context(nc.psum_tensor("psS2", [128, 512], F32))
    if True:
        p = Prog()
        r2c3 = rearr(r2c[:], "p (k t) -> p k t", k=KC)
        yc3 = rearr(yc[:], "p (k t) -> p k t", k=MC)
        gb3 = rearr(gb[:], "p (k t) -> p k t", k=MC)
        rout3 = rearr(rout[:], "p (k t) -> p k t", k=KC)
        p.op("gpsimd", lambda e: e.memset(ones[:], 1.0), writes=("ones",))
        p.op("gpsimd", lambda e: e.memset(onesrow[:], 1.0), writes=("ones",))
        emit_weight_load(p, WoT, wst, wo, MC, D, "wo")
        for s in range(nsub):
            sl = slice(s * TSUB, (s + 1) * TSUB)
            p.dma("sync", r2c3, rearr(r2[:, sl], "(k p) t -> p k t", p=128),
                  reads=("r2f",), writes=("r2c",))
            p.dma("sync", yc3, rearr(yv[:, sl], "(k p) t -> p k t", p=128),
                  reads=("ycf",), writes=("ycv",))
            for m in range(MC):
                zpar = m % 2
                p.dma("sync", zc[:, zpar * TSUB:(zpar + 1) * TSUB],
                      zv[m * 128:(m + 1) * 128, sl],
                      reads=(f"szf{zpar}",), writes=(f"sz{zpar}",))
                p.op("scalar", lambda e, zpar=zpar: e.activation(
                    sq[:], zc[:, zpar * TSUB:(zpar + 1) * TSUB], AF.Silu),
                    reads=(f"sz{zpar}", "sqbuf"), writes=("sz2", "sqbuf", f"szf{zpar}"))
                p.op("vector", lambda e, m=m: e.tensor_tensor(
                    gb3[:, m, :], yc3[:, m, :], sq[:], ALU.mult),
                    reads=("ycv", "sz2"), writes=(f"gb{m}",))
            for mo in range(KC):
                for k in range(MC):
                    p.op("tensor", lambda e, mo=mo, k=k: e.matmul(
                        psA[:, 0:TSUB], wo[:, k * D + mo * 128:k * D + (mo + 1) * 128],
                        gb3[:, k, :], start=(k == 0), stop=(k == MC - 1)),
                        reads=("wo", f"gb{k}"), writes=("psA",))
                p.op("vector", lambda e, mo=mo: e.tensor_tensor(
                    rout3[:, mo, :], psA[:, 0:TSUB], r2c3[:, mo, :], ALU.add),
                    reads=("psA", "r2c"), writes=(f"rout{mo}",))
            p.op("vector", lambda e: e.tensor_copy(sq[:, 0:1], yc3[:, 0, 0:1]),
                 reads=("ycv", "r2c", "sqbuf") + tuple(f"rout{mo}" for mo in range(KC)),
                 writes=("ycf", "r2f", "sqbuf"))
            emit_ln(p, rout3, KC, TSUB, ones, onesrow, sq, stats, bcast, psS, psS2,
                    tag=f"ln{s}", out_keys=[f"rout{mo}" for mo in range(KC)])
            p.dma("sync", rearr(r3[:, sl], "(k p) t -> p k t", p=128), rout3,
                  reads=tuple(f"rout{mo}" for mo in range(KC)), writes=())
        p.emit(nc)
    st.close()
    return nc


def run_mambapost(y, z, r2T, W_out):
    if "mambapost" not in _cache:
        _cache["mambapost"] = mk_mambapost()
    nc = _cache["mambapost"]
    WoT = np.ascontiguousarray(W_out.T)
    in_maps = []
    for c in range(NCORE):
        sl = slice(c * TPC, (c + 1) * TPC)
        in_maps.append(dict(
            yv=np.ascontiguousarray(y[:, sl]), zv=np.ascontiguousarray(z[:, sl]),
            r2=np.ascontiguousarray(r2T[:, sl]), WoT=WoT))
    res = run_bass_kernel_spmd(nc, in_maps, core_ids=list(range(NCORE)))
    return np.concatenate([r["r3"] for r in res.results], axis=1)



DBG_ATTN = False


def mk_l0attn(S=SEQ):
    KC = D // 128
    nqc = S // 512
    ntb = S // 512
    nc = bass.Bass("TRN2", target_bir_lowering=False, debug=False)
    xb = nc.dram_tensor("xb", [D, S], F32, kind="ExternalInput").ap()
    WDQ = nc.dram_tensor("WDQ", [D, 256], F32, kind="ExternalInput").ap()
    WDKV = nc.dram_tensor("WDKV", [D, 256], F32, kind="ExternalInput").ap()
    WUQ = nc.dram_tensor("WUQ", [256, 256], F32, kind="ExternalInput").ap()
    WQR = nc.dram_tensor("WQR", [256, 128], F32, kind="ExternalInput").ap()
    WUK = nc.dram_tensor("WUK", [256, 256], F32, kind="ExternalInput").ap()
    WUV = nc.dram_tensor("WUV", [256, 256], F32, kind="ExternalInput").ap()
    WKR = nc.dram_tensor("WKR", [D, 32], F32, kind="ExternalInput").ap()
    rc = nc.dram_tensor("rc", [32, S], F32, kind="ExternalInput").ap()
    rs = nc.dram_tensor("rs", [32, S], F32, kind="ExternalInput").ap()
    cmask = nc.dram_tensor("cmask", [4, 128, 512], F32, kind="ExternalInput").ap()
    ao = nc.dram_tensor("ao", [256, S], BF16, kind="ExternalOutput").ap()
    dbg = nc.dram_tensor("dbg", [128, 4 * 512 + 6 * 512], F32, kind="ExternalOutput").ap() if DBG_ATTN else None

    SC = float(1.0 / np.sqrt(96.0))
    EBIAS = float(-80.0 * SC)
    PLO = float(np.exp(-160.0 * SC))

    st = ExitStack()
    xb16 = st.enter_context(nc.sbuf_tensor("xb16", [128, KC * 512], BF16))
    wdq = st.enter_context(nc.sbuf_tensor("wdq", [128, KC * 256], BF16))
    wdkv = st.enter_context(nc.sbuf_tensor("wdkv", [128, KC * 256], BF16))
    wu = st.enter_context(nc.sbuf_tensor("wu", [128, 3 * 512 + 512], BF16))
    wqr = st.enter_context(nc.sbuf_tensor("wqr", [128, 512], BF16))
    wqr2 = st.enter_context(nc.sbuf_tensor("wqr2", [128, 512], BF16))
    wkr = st.enter_context(nc.sbuf_tensor("wkr", [128, KC * 64], BF16))
    wst = st.enter_context(nc.sbuf_tensor("wst", [128, 1024], F32))
    cq = st.enter_context(nc.sbuf_tensor("cq", [128, 2 * S], BF16))
    ckv = st.enter_context(nc.sbuf_tensor("ckv", [128, 2 * S], BF16))
    krr = st.enter_context(nc.sbuf_tensor("krr", [128, S], BF16))
    rct = st.enter_context(nc.sbuf_tensor("rct", [128, S], BF16))
    rst = st.enter_context(nc.sbuf_tensor("rst", [128, S], BF16))
    kc96 = st.enter_context(nc.sbuf_tensor("kc96", [128, S], BF16))
    qc96 = st.enter_context(nc.sbuf_tensor("qc96", [128, S], BF16))
    qr96 = st.enter_context(nc.sbuf_tensor("qr96", [128, S], BF16))
    vtok = st.enter_context(nc.sbuf_tensor("vtok", [128, 64 * 65], BF16))
    probs = st.enter_context(nc.sbuf_tensor("probs", [128, 512], BF16))
    smask = st.enter_context(nc.sbuf_tensor("smask", [128, 4 * 512], BF16))
    attnf = st.enter_context(nc.sbuf_tensor("attnf", [128, 512], F32))
    sums = st.enter_context(nc.sbuf_tensor("sums", [128, 2 * 512], F32))
    onesrow = st.enter_context(nc.sbuf_tensor("onesrow", [1, 128], F32))
    aob = st.enter_context(nc.sbuf_tensor("aob", [128, 512], BF16))
    psK = st.enter_context(nc.psum_tensor("psK", [128, 512], F32))
    psQ = st.enter_context(nc.psum_tensor("psQ", [128, 512], F32))
    psR = st.enter_context(nc.psum_tensor("psR", [128, 512], F32))
    psSc = st.enter_context(nc.psum_tensor("psSc", [128, 512], F32))
    psAt = st.enter_context(nc.psum_tensor("psAt", [128, 512], F32))
    psB2 = st.enter_context(nc.psum_tensor("psB2", [128, 512], F32))
    if True:
        p = Prog()
        xstg3 = rearr(wst[:, 0:1024], "p (k t) -> p k t", k=2)
        xb3 = rearr(xb16[:], "p (k t) -> p k t", k=KC)
        cq3 = rearr(cq[:], "p (m t) -> p m t", m=2)
        ckv3 = rearr(ckv[:], "p (m t) -> p m t", m=2)
        wkr3 = rearr(wkr[:], "p (k d) -> p k d", k=KC)
        emit_weight_load(p, WDQ, wst, wdq, KC, 256, "wdq", stage_cols=1024)
        emit_weight_load(p, WDKV, wst, wdkv, KC, 256, "wdkv", stage_cols=1024)
        for i, wsrc in enumerate((WUQ, WUK, WUV)):
            p.dma("sync", rearr(wst[:, 0:512], "p (k d) -> p k d", k=2),
                  rearr(wsrc, "(k p) d -> p k d", p=128), reads=("wstage",), writes=("wstage",))
            p.op("vector", lambda e, i=i: e.tensor_copy(wu[:, i * 512:(i + 1) * 512], wst[:, 0:512]),
                 reads=("wstage",), writes=("wu",))
        p.dma("sync", rearr(wst[:, 0:256], "p (k d) -> p k d", k=2),
              rearr(WQR, "(k p) d -> p k d", p=128), reads=("wstage",), writes=("wstage",))
        p.op("vector", lambda e: e.tensor_copy(wqr[:, 0:256], wst[:, 0:256]),
             reads=("wstage",), writes=("wqr",))
        wq1 = rearr(wqr[:, 0:256], "p (k d) -> p k d", k=2)
        wq2 = rearr(wqr[:, 256:512], "p (k d) -> p k d", k=2)
        p.op("vector", lambda e: e.tensor_scalar(
            wq2[:, :, 0:128:2], wq1[:, :, 1:128:2], -1.0, None, ALU.mult),
            reads=("wqr",), writes=("wqrs",))
        p.op("vector", lambda e: e.tensor_copy(wq2[:, :, 1:128:2], wq1[:, :, 0:128:2]),
             reads=("wqr",), writes=("wqrs",))
        wq2v = rearr(wu[:, 1536:2048], "p (k h f d) -> p k h f d", k=2, h=4, f=2)
        p.op("vector", lambda e: e.tensor_copy(
            wq2v[:, :, :, 0, :], rearr(wqr[:, 0:256], "p (k h d) -> p k h d", k=2, h=4)),
            reads=("wqr",), writes=("wqr2", "wu"))
        p.op("vector", lambda e: e.tensor_copy(
            wq2v[:, :, :, 1, :], rearr(wqr[:, 256:512], "p (k h d) -> p k h d", k=2, h=4)),
            reads=("wqrs",), writes=("wqr2", "wu"))
        p.dma("sync", rearr(wst[:, 0:256], "p (k d) -> p k d", k=KC),
              rearr(WKR, "(k p) d -> p k d", p=128), reads=("wstage", "wqr", "wqrs"), writes=("wstage",))
        p.op("vector", lambda e: e.tensor_copy(
            wkr3[:, :, 0:32], rearr(wst[:, 0:256], "p (k d) -> p k d", k=KC)),
            reads=("wstage",), writes=("wkr",))
        p.op("vector", lambda e: e.tensor_scalar(
            wkr3[:, :, 32:64:2], wkr3[:, :, 1:32:2], -1.0, None, ALU.mult),
            reads=("wkr",), writes=("wkrs",))
        p.op("vector", lambda e: e.tensor_copy(wkr3[:, :, 33:64:2], wkr3[:, :, 0:32:2]),
             reads=("wkr",), writes=("wkrs",))
        for tt, dst in ((rc, rct), (rs, rst)):
            for cc in range(S // 2048):
                p.dma("sync", wst[0:32, 0:512], tt[:, cc * 2048:cc * 2048 + 512], reads=("wstage",), writes=("wstage",))
                p.op("vector", lambda e, dst=dst, cc=cc: e.tensor_copy(
                    dst[0:32, cc * 2048:cc * 2048 + 512], wst[0:32, 0:512]),
                    reads=("wstage",), writes=("rope",))
        for tt, dst in ((rc, rct), (rs, rst)):
            for cc in range(S // 512):
                if cc % 4 == 0:
                    continue
                p.dma("sync", wst[0:32, 0:512], tt[:, cc * 512:(cc + 1) * 512], reads=("wstage",), writes=("wstage",))
                p.op("vector", lambda e, dst=dst, cc=cc: e.tensor_copy(
                    dst[0:32, cc * 512:(cc + 1) * 512], wst[0:32, 0:512]),
                    reads=("wstage",), writes=("rope",))
        for vv in range(4):
            p.dma("sync", wst[:, 0:512], cmask[vv], reads=("wstage",), writes=("wstage",))
            p.op("vector", lambda e, vv=vv: e.tensor_copy(
                smask[:, vv * 512:(vv + 1) * 512], wst[:, 0:512]),
                reads=("wstage",), writes=("cmask",))
        p.op("gpsimd", lambda e: e.memset(onesrow[:], 1.0), writes=("ones",))
        for tb in range(ntb):
            for hh in range(4):
                p.dma("sync", xstg3,
                      rearr(xb[hh * 256:(hh + 1) * 256, tb * 512:(tb + 1) * 512], "(k p) t -> p k t", p=128),
                      reads=("wstage",), writes=("wstage",))
                p.op("vector", lambda e, hh=hh: e.tensor_copy(xb3[:, hh * 2:(hh + 1) * 2, :], xstg3),
                     reads=("wstage",), writes=("xs", "wstage"))
            for wbuf, wkey, ob3, okey, pst, pk in (
                    (wdq, "wdq", cq3, "cq", psK, "psK"), (wdkv, "wdkv", ckv3, "ckv", psQ, "psQ")):
                for m in range(2):
                    for k in range(KC):
                        p.op("tensor", lambda e, m=m, k=k, wbuf=wbuf, pst=pst: e.matmul(
                            pst[:, 0:512], wbuf[:, k * 256 + m * 128:k * 256 + (m + 1) * 128],
                            xb3[:, k, :], start=(k == 0), stop=(k == KC - 1)),
                            reads=(wkey, "xs"), writes=(pk,))
                    p.op("scalar", lambda e, m=m, tb=tb, ob3=ob3, pst=pst: e.copy(
                        ob3[:, m, tb * 512:(tb + 1) * 512], pst[:, 0:512]),
                        reads=(pk,), writes=(okey,))
            for half in range(2):
                for k in range(KC):
                    p.op("tensor", lambda e, k=k, half=half: e.matmul(
                        psR[0:32, 0:512], wkr3[:, k, half * 32:(half + 1) * 32], xb3[:, k, :],
                        start=(k == 0), stop=(k == KC - 1)),
                        reads=("wkr", "wkrs", "xs"), writes=("psR",))
                p.op("vector", lambda e, half=half: e.tensor_copy(
                    wst[0:32, half * 512:(half + 1) * 512], psR[0:32, 0:512]),
                    reads=("psR", "wstage"), writes=("wstage",))
            p.op("vector", lambda e, tb=tb: e.tensor_tensor(
                wst[0:32, 0:512], wst[0:32, 0:512], rct[0:32, tb * 512:(tb + 1) * 512], ALU.mult),
                reads=("wstage", "rope"), writes=("wstage",))
            p.op("vector", lambda e, tb=tb: e.tensor_tensor(
                wst[0:32, 512:1024], wst[0:32, 512:1024], rst[0:32, tb * 512:(tb + 1) * 512], ALU.mult),
                reads=("wstage", "rope"), writes=("wstage",))
            p.op("vector", lambda e, tb=tb: e.tensor_tensor(
                krr[0:32, tb * 512:(tb + 1) * 512], wst[0:32, 0:512], wst[0:32, 512:1024], ALU.add),
                reads=("wstage",), writes=("krrot",))
        for h in range(4):
            for tb in range(ntb):
                for k in range(2):
                    p.op("tensor", lambda e, h=h, tb=tb, k=k: e.matmul(
                        psK[0:64, 0:512], wu[:, 512 + k * 256 + h * 64:512 + k * 256 + (h + 1) * 64],
                        ckv3[:, k, tb * 512:(tb + 1) * 512], start=(k == 0), stop=(k == 1)),
                        reads=("wu", "ckv"), writes=("psK",))
                p.op("scalar", lambda e, tb=tb: e.copy(
                    kc96[0:64, tb * 512:(tb + 1) * 512], psK[0:64, 0:512]),
                    reads=("psK",), writes=("k96c",))
                for k in range(2):
                    p.op("tensor", lambda e, h=h, tb=tb, k=k: e.matmul(
                        psQ[0:64, 0:512], wu[:, k * 256 + h * 64:k * 256 + (h + 1) * 64],
                        cq3[:, k, tb * 512:(tb + 1) * 512], start=(k == 0), stop=(k == 1)),
                        reads=("wu", "cq"), writes=("psQ",))
                p.op("scalar", lambda e, tb=tb: e.copy(
                    qc96[0:64, tb * 512:(tb + 1) * 512], psQ[0:64, 0:512]),
                    reads=("psQ",), writes=("q96c",))
                for k in range(2):
                    p.op("tensor", lambda e, h=h, tb=tb, k=k: e.matmul(
                        psQ[0:64, 0:512],
                        wu[:, 1536 + k * 256 + h * 64:1536 + k * 256 + (h + 1) * 64],
                        cq3[:, k, tb * 512:(tb + 1) * 512], start=(k == 0), stop=(k == 1)),
                        reads=("wqr2", "cq"), writes=("psQ",))
                p.op("vector", lambda e: e.tensor_copy(
                    wst[0:32, 0:512], psQ[0:32, 0:512]),
                    reads=("psQ", "wstage"), writes=("wstage",))
                p.op("vector", lambda e: e.tensor_copy(
                    wst[0:32, 512:1024], psQ[32:64, 0:512]),
                    reads=("psQ", "wstage"), writes=("wstage",))
                if DBG_ATTN and h == 0 and tb == 0:
                    p.dma("gpsimd", dbg[0:32, 4096:4608], wst[0:32, 0:512], reads=("wstage",))
                    p.dma("gpsimd", dbg[0:32, 4608:5120], wst[0:32, 512:1024], reads=("wstage",))
                p.op("vector", lambda e, tb=tb: e.tensor_tensor(
                    wst[0:32, 0:512], wst[0:32, 0:512], rct[0:32, tb * 512:(tb + 1) * 512], ALU.mult),
                    reads=("wstage", "rope"), writes=("wstage",))
                p.op("vector", lambda e, tb=tb: e.tensor_tensor(
                    wst[0:32, 512:1024], wst[0:32, 512:1024], rst[0:32, tb * 512:(tb + 1) * 512], ALU.mult),
                    reads=("wstage", "rope"), writes=("wstage",))
                if DBG_ATTN and h == 0 and tb == 0:
                    p.dma("gpsimd", dbg[0:32, 1024:1536], wst[0:32, 0:512], reads=("wstage",))
                    p.dma("gpsimd", dbg[0:32, 1536:2048], wst[0:32, 512:1024], reads=("wstage",))
                    p.dma("gpsimd", dbg[0:32, 0:512], rct[0:32, 0:512], reads=("rope",))
                    p.dma("gpsimd", dbg[0:32, 512:1024], rst[0:32, 0:512], reads=("rope",))
                p.op("vector", lambda e, tb=tb: e.tensor_tensor(
                    qr96[0:32, tb * 512:(tb + 1) * 512], wst[0:32, 0:512], wst[0:32, 512:1024], ALU.add),
                    reads=("wstage",), writes=("q96r",))
            p.op("gpsimd", lambda e: e.memset(vtok[:], 0.0), reads=("vuse",), writes=("vtok",))
            p.op("gpsimd", lambda e: e.memset(
                RawAP2(vtok, 64, 65), 1.0), reads=("vtok",), writes=("vtok",))
            for blk in range(S // 128):
                for k in range(2):
                    p.op("tensor", lambda e, h=h, blk=blk, k=k: e.matmul(
                        psB2[:, 0:64], ckv3[:, k, blk * 128:(blk + 1) * 128],
                        wu[:, 1024 + k * 256 + h * 64:1024 + k * 256 + (h + 1) * 64],
                        start=(k == 0), stop=(k == 1)),
                        reads=("wu", "ckv"), writes=("psB2",))
                p.op("vector", lambda e, blk=blk: e.tensor_copy(
                    vtok[:, blk * 65:blk * 65 + 64], psB2[:, 0:64]),
                    reads=("psB2", "vtok"), writes=("vtok",))
            for qc in range(nqc):
                nkc = 4 * qc + 4
                for kc in range(nkc):
                    par = kc % 2
                    p.op("tensor", lambda e, qc=qc, kc=kc: e.matmul(
                        psSc[:, 0:512], kc96[0:64, kc * 128:(kc + 1) * 128],
                        qc96[0:64, qc * 512:(qc + 1) * 512], start=True, stop=False),
                        reads=("k96c", "q96c", "psScE"), writes=("psSc",))
                    p.op("tensor", lambda e, qc=qc, kc=kc: e.matmul(
                        psSc[:, 0:512], krr[0:32, kc * 128:(kc + 1) * 128],
                        qr96[0:32, qc * 512:(qc + 1) * 512], start=False, stop=True),
                        reads=("krrot", "q96r", "psScE"), writes=("psSc",))
                    p.op("scalar", lambda e, par=par: e.activation(
                        probs[:, 0:512], psSc[:, 0:512],
                        AF.Exp, bias=f(EBIAS), scale=f(SC)),
                        reads=("psSc", f"prF{par}"), writes=(f"pr{par}", "psScE"))
                    p.op("vector", lambda e, par=par: e.tensor_scalar(
                        probs[:, 0:512], probs[:, 0:512],
                        1.0, PLO, ALU.min, ALU.max),
                        reads=(f"pr{par}",), writes=(f"pr{par}",))
                    j = kc - 4 * qc
                    if j >= 0:
                        p.op("vector", lambda e, par=par, j=j: e.tensor_tensor(
                            probs[:, 0:512],
                            probs[:, 0:512],
                            rearr(smask[:], "p (j t) -> p j t", j=4)[:, j, :], ALU.mult),
                            reads=(f"pr{par}", "cmask"), writes=(f"pr{par}",))
                    p.op("tensor", lambda e, qc=qc, kc=kc, par=par: e.matmul(
                        psAt[0:65, 0:512], vtok[:, kc * 65:(kc + 1) * 65],
                        probs[:, 0:512],
                        start=(kc == 0), stop=(kc == nkc - 1)),
                        reads=("vtok", f"pr{par}"), writes=("psAt", f"prF{par}"))
                if DBG_ATTN and h == 0 and qc == 0:

                    p.dma("gpsimd", dbg[0:64, 4 * 512:5 * 512], qc96[0:64, 0:512], reads=("q96c",))
                    p.dma("gpsimd", dbg[0:64, 5 * 512:6 * 512], kc96[0:64, 0:512], reads=("k96c",))
                    p.dma("gpsimd", dbg[0:32, 6 * 512:7 * 512], qr96[0:32, 0:512], reads=("q96r",))
                    p.dma("gpsimd", dbg[0:32, 7 * 512:8 * 512], krr[0:32, 0:512], reads=("krrot",))

                p.op("vector", lambda e: e.tensor_copy(sums[0:1, 0:512], psAt[64:65, 0:512]),
                     reads=("psAt", "sumsb"), writes=("sums1", "sumsb"))
                p.op("vector", lambda e: e.reciprocal(sums[0:1, 512:1024], sums[0:1, 0:512]),
                     reads=("sums1", "sumsb"), writes=("sums2", "sumsb"))
                p.op("tensor", lambda e: e.matmul(
                    psB2[0:64, 0:512], onesrow[0:1, 0:64], sums[0:1, 512:1024],
                    start=True, stop=True),
                    reads=("ones", "sums2"), writes=("psB2",))
                p.op("scalar", lambda e: e.copy(attnf[0:64, 0:512], psAt[0:64, 0:512]),
                     reads=("psAt", "attnf"), writes=("attnf",))
                p.op("vector", lambda e: e.tensor_tensor(
                    aob[0:64, 0:512], attnf[0:64, 0:512], psB2[0:64, 0:512], ALU.mult),
                    reads=("attnf", "psB2", "aobF"), writes=("aob",))
                p.dma("sync", ao[h * 64:(h + 1) * 64, qc * 512:(qc + 1) * 512], aob[0:64, 0:512],
                      reads=("aob",), writes=("aobF",))
            p.op("vector", lambda e: e.tensor_copy(sums[0:1, 0:1], vtok[0:1, 0:1]),
                 reads=("vtok", "sumsb"), writes=("vuse", "sumsb"))
        p.emit(nc)
    st.close()
    return nc


def RawAP2(t, off, step):
    from concourse.ap import AP as _AP
    n = t[:].shape[1] // step
    return _AP(t[:].tensor, off, [[t[:].shape[1], 128], [step, n]])


def run_l0attn(xT, inp):
    if "l0attn" not in _cache:
        _cache["l0attn"] = mk_l0attn()
    nc = _cache["l0attn"]
    d = 32
    inv = 10000.0 ** (-np.arange(0, d, 2, dtype=np.float64) / d)
    ang = np.arange(SEQ, dtype=np.float64)[:, None] * inv
    cos_t = np.zeros((32, SEQ), np.float32)
    sin_t = np.zeros((32, SEQ), np.float32)
    for i in range(16):
        cos_t[2 * i] = np.cos(ang[:, i]); cos_t[2 * i + 1] = np.cos(ang[:, i])
        sin_t[2 * i] = np.sin(ang[:, i]); sin_t[2 * i + 1] = np.sin(ang[:, i])
    cm = np.zeros((4, 128, 512), np.float32)
    for j in range(4):
        for kk in range(128):
            cm[j, kk, :] = (np.arange(512) >= kk + j * 128)
    in_maps = []
    for c in range(NCORE):
        b = c // 4
        h0 = (c % 4) * 4
        in_maps.append(dict(
            xb=np.ascontiguousarray(xT[:, b * SEQ:(b + 1) * SEQ]),
            WDQ=inp["L0_attn_W_DQ"], WDKV=inp["L0_attn_W_DKV"],
            WUQ=np.ascontiguousarray(inp["L0_attn_W_UQ"][:, h0 * 64:(h0 + 4) * 64]),
            WQR=np.ascontiguousarray(inp["L0_attn_W_QR"][:, h0 * 32:(h0 + 4) * 32]),
            WUK=np.ascontiguousarray(inp["L0_attn_W_UK"][:, h0 * 64:(h0 + 4) * 64]),
            WUV=np.ascontiguousarray(inp["L0_attn_W_UV"][:, h0 * 64:(h0 + 4) * 64]),
            WKR=inp["L0_attn_W_KR"],
            rc=cos_t, rs=sin_t, cmask=cm))
    res = run_bass_kernel_spmd(nc, in_maps, core_ids=list(range(NCORE)))
    out = np.zeros((1024, TOK), dtype=BF)
    for c in range(NCORE):
        b = c // 4
        h0 = (c % 4) * 4
        out[h0 * 64:(h0 + 4) * 64, b * SEQ:(b + 1) * SEQ] = res.results[c]["ao"]
    return out



L1_NO_ATTN = False
L1_NO_FFN = False


def mk_l1ffn(final, T=TPC):
    KC = 8
    NW = 6
    NJ = 22
    FS = 256
    nc = bass.Bass("TRN2", target_bir_lowering=False, debug=False)
    rin = nc.dram_tensor("rin", [D, 1025 + T], F32, kind="ExternalInput").ap()
    wqkv = nc.dram_tensor("wqkv", [D, 1536], F32, kind="ExternalInput").ap()
    qsc = nc.dram_tensor("qsc", [128, 2], F32, kind="ExternalInput").ap()
    ksc = nc.dram_tensor("ksc", [128, 2], F32, kind="ExternalInput").ap()
    wo_d = nc.dram_tensor("wo", [512, D], F32, kind="ExternalInput").ap()
    lng = nc.dram_tensor("lng", [D, 1], F32, kind="ExternalInput").ap()
    lnb = nc.dram_tensor("lnb", [D, 1], F32, kind="ExternalInput").ap()
    w1p = nc.dram_tensor("w1p", [D, NJ * 256], F32, kind="ExternalInput").ap()
    b1p = nc.dram_tensor("b1p", [128, 2 * NJ], F32, kind="ExternalInput").ap()
    w2p = nc.dram_tensor("w2p", [NJ * 128, D], F32, kind="ExternalInput").ap()
    b2 = nc.dram_tensor("b2", [D, 1], F32, kind="ExternalInput").ap()
    tabs = nc.dram_tensor("tabs", [128, 4 * (1024 + T)], F32, kind="ExternalInput").ap()
    amask = nc.dram_tensor("amask", [128, 1408 + 1024], F32, kind="ExternalInput").ap()
    eye = nc.dram_tensor("eye", [128, 128], F32, kind="ExternalInput").ap()
    bonesd = nc.dram_tensor("bonesd", [2, 128], F32, kind="ExternalInput").ap()
    r3c = nc.dram_tensor("r3c", [D, T], F32, kind="ExternalInput").ap()
    rout = nc.dram_tensor("rout", [D, T], F32, kind="ExternalOutput").ap()

    st = ExitStack()
    x1 = st.enter_context(nc.sbuf_tensor("x1", [128, KC * 513], F32))
    x1b = st.enter_context(nc.sbuf_tensor("x1b", [128, KC * 513], BF16))
    wq = st.enter_context(nc.sbuf_tensor("wq", [128, KC * 1536], BF16))
    wrot = st.enter_context(nc.sbuf_tensor("wrot", [128, KC * 1024], BF16))
    wst = st.enter_context(nc.sbuf_tensor("wst", [128, 1024], F32))
    wob = st.enter_context(nc.sbuf_tensor("wob", [64, 8 * D], BF16))
    w1b = st.enter_context(nc.sbuf_tensor("w1b", [128, KC * 256], BF16))
    w2b = st.enter_context(nc.sbuf_tensor("w2b", [128, D], BF16))
    sqs = st.enter_context(nc.sbuf_tensor("sqs", [128, 4], F32))
    sgb = st.enter_context(nc.sbuf_tensor("sgb", [128, 2 * KC], F32))
    sb2 = st.enter_context(nc.sbuf_tensor("sb2", [128, KC], F32))
    sb1 = st.enter_context(nc.sbuf_tensor("sb1", [128, 2 * NJ], F32))
    stab = st.enter_context(nc.sbuf_tensor("stab", [128, 4 * 512], BF16))
    smask = st.enter_context(nc.sbuf_tensor("smask", [128, 1408 + 1024], BF16))
    eyeb = st.enter_context(nc.sbuf_tensor("eyeb", [128, 128], BF16))
    bones = st.enter_context(nc.sbuf_tensor("bones", [2, 128], F32))
    honesr = st.enter_context(nc.sbuf_tensor("honesr", [128, 2], F32))
    onesrow = st.enter_context(nc.sbuf_tensor("onesrow", [1, 128], F32))
    ones = st.enter_context(nc.sbuf_tensor("ones", [128, 1], F32))
    qraw = st.enter_context(nc.sbuf_tensor("qraw", [128, 512], BF16))
    qrot = st.enter_context(nc.sbuf_tensor("qrot", [128, 512], BF16))
    qu = st.enter_context(nc.sbuf_tensor("qu", [128, 2 * 4 * 512], BF16))
    ku = st.enter_context(nc.sbuf_tensor("ku", [128, 2 * 4 * 512], BF16))
    vt = st.enter_context(nc.sbuf_tensor("vt", [128, 2 * 8 * 4 * 65], BF16))
    normb = st.enter_context(nc.sbuf_tensor("normb", [2, 512], F32))
    sc = st.enter_context(nc.sbuf_tensor("sc", [128, 1024], F32))
    rmax = st.enter_context(nc.sbuf_tensor("rmax", [128, 2], F32))
    probs = st.enter_context(nc.sbuf_tensor("probs", [128, 1024], BF16))
    atn = st.enter_context(nc.sbuf_tensor("atn", [64, 8 * 128], BF16))
    attnf = st.enter_context(nc.sbuf_tensor("attnf", [128, 128], F32))
    sums = st.enter_context(nc.sbuf_tensor("sums", [128, 2 * 128], F32))
    x2 = st.enter_context(nc.sbuf_tensor("x2", [128, KC * 513], F32))
    lnbuf = st.enter_context(nc.sbuf_tensor("lnbuf", [128, KC * FS], BF16))
    hbuf = st.enter_context(nc.sbuf_tensor("hbuf", [128, NJ * FS], BF16))
    gel = st.enter_context(nc.sbuf_tensor("gel", [128, FS], F32))
    rob = st.enter_context(nc.sbuf_tensor("rob", [128, KC * 512], F32))
    sq = st.enter_context(nc.sbuf_tensor("sq", [128, 512], F32))
    stats = st.enter_context(nc.sbuf_tensor("stats", [1, 3 * 512], F32))
    bcast = st.enter_context(nc.sbuf_tensor("bcast", [128, 2 * 512], F32))
    psA = st.enter_context(nc.psum_tensor("psA", [128, 512], F32))
    psG = st.enter_context(nc.psum_tensor("psG", [128, 512], F32))
    ps0 = st.enter_context(nc.psum_tensor("ps0", [128, 512], F32))
    ps1 = st.enter_context(nc.psum_tensor("ps1", [128, 512], F32))
    ps2 = st.enter_context(nc.psum_tensor("ps2", [128, 512], F32))
    psS = st.enter_context(nc.psum_tensor("psS", [128, 512], F32))
    psS2 = st.enter_context(nc.psum_tensor("psS2", [128, 512], F32))
    psT = st.enter_context(nc.psum_tensor("psT", [128, 256], BF16))
    if True:
        p = Prog()
        x13 = rearr(x1[:], "p (k t) -> p k t", k=KC)
        x1b3 = rearr(x1b[:], "p (k t) -> p k t", k=KC)
        x23 = rearr(x2[:], "p (k t) -> p k t", k=KC)
        qu4 = rearr(qu[:], "p (w i t) -> p w i t", w=2, i=4)
        ku4 = rearr(ku[:], "p (w i t) -> p w i t", w=2, i=4)
        vt4 = rearr(vt[:], "p (w h b) -> p w h b", w=2, h=8)
        lnb3 = rearr(lnbuf[:], "p (k t) -> p k t", k=KC)
        rob3 = rearr(rob[:], "p (k t) -> p k t", k=KC)
        hb3 = rearr(hbuf[:], "p (j t) -> p j t", j=NJ)
        stab3 = rearr(stab[:], "p (a t) -> p a t", a=4)
        wq4 = rearr(wq[:], "p (k c) -> p k c", k=KC)
        wr4 = rearr(wrot[:], "p (k c) -> p k c", k=KC)
        p.op("gpsimd", lambda e: e.memset(onesrow[:], 1.0), writes=("ones",))
        p.op("gpsimd", lambda e: e.memset(ones[:], 1.0), writes=("ones",))
        p.op("gpsimd", lambda e: e.memset(honesr[:], 0.0), writes=("hones",))
        p.op("gpsimd", lambda e: e.memset(honesr[0:64, 0:1], 1.0), writes=("hones",))
        p.op("gpsimd", lambda e: e.memset(honesr[64:128, 1:2], 1.0), writes=("hones",))
        p.dma("sync", bones[:], bonesd, writes=("bones",))
        for cc in range(3):
            c0, c1 = cc * 1024, min((cc + 1) * 1024, 2432)
            p.dma("sync", wst[:, 0:c1 - c0], amask[:, c0:c1], reads=("wstage",), writes=("wstage",))
            p.op("vector", lambda e, c0=c0, c1=c1: e.tensor_copy(
                smask[:, c0:c1], wst[:, 0:c1 - c0]),
                reads=("wstage",), writes=("amask",))
        p.dma("sync", wst[:, 0:128], eye, reads=("wstage",), writes=("wstage",))
        p.op("vector", lambda e: e.tensor_copy(eyeb[:], wst[:, 0:128]), reads=("wstage",), writes=("eye", "wstage"))
        p.dma("sync", sqs[:, 0:2], qsc, writes=("sqs",))
        p.dma("sync", sqs[:, 2:4], ksc, writes=("sqs",))
        p.dma("sync", sgb[:, 0:KC].unsqueeze(2), rearr(lng, "(k p) o -> p k o", p=128),
              writes=("gb",), allow_slow_non_contiguous=True)
        p.dma("sync", sgb[:, KC:2 * KC].unsqueeze(2), rearr(lnb, "(k p) o -> p k o", p=128),
              writes=("gb",), allow_slow_non_contiguous=True)
        p.dma("sync", sb2[:].unsqueeze(2), rearr(b2, "(k p) o -> p k o", p=128),
              writes=("gb",), allow_slow_non_contiguous=True)
        p.dma("sync", sb1[:], b1p, writes=("gb",))
        for k_ in range(KC):
            for hh in range(2):
                p.dma("sync", wst[:, 0:768], wqkv[k_ * 128:(k_ + 1) * 128, hh * 768:(hh + 1) * 768],
                      reads=("wstage",), writes=("wstage",))
                p.op("vector", lambda e, k_=k_, hh=hh: e.tensor_copy(
                    wq[:, k_ * 1536 + hh * 768:k_ * 1536 + (hh + 1) * 768], wst[:, 0:768]),
                    reads=("wstage",), writes=("wq",))
        for h_ in range(8):
            p.dma("sync", wst[0:64, 0:1024], wo_d[h_ * 64:(h_ + 1) * 64, :],
                  reads=("wstage",), writes=("wstage",))
            p.op("vector", lambda e, h_=h_: e.tensor_copy(
                wob[0:64, h_ * D:(h_ + 1) * D], wst[0:64, 0:1024]),
                reads=("wstage",), writes=("wo",))
        for half in range(2):
            src_ = rearr(wq4[:, :, half * 512:(half + 1) * 512], "p k (h d) -> p k h d", h=8)
            dst_ = rearr(wr4[:, :, half * 512:(half + 1) * 512], "p k (h d) -> p k h d", h=8)
            p.op("vector", lambda e, src_=src_, dst_=dst_: e.tensor_scalar(
                dst_[:, :, :, 0:32], src_[:, :, :, 32:64], -1.0, None, ALU.mult),
                reads=("wq",), writes=("wrot",))
            p.op("vector", lambda e, src_=src_, dst_=dst_: e.tensor_copy(
                dst_[:, :, :, 32:64], src_[:, :, :, 0:32]),
                reads=("wq",), writes=("wrot",))

        qblocks = {1: [3], 2: [0, 1, 2, 3], 3: [0, 1, 2, 3], 4: [0, 1, 2, 3], 5: [0, 1, 2, 3]}

        for g in range(NW):
            gp = g % 2
            p.dma("sync", x13,
                  rearr(rin[:, g * 512:g * 512 + 513], "(k p) t -> p k t", p=128),
                  reads=("x1F",), writes=("x1",))
            p.op("vector", lambda e: e.tensor_copy(x1b3, x13),
                 reads=("x1", "x1bF"), writes=("x1b",))
            p.dma("gpsimd", stab3,
                  rearr(tabs[:, :], "p (a t) -> p a t", a=4)[:, :, g * 512:(g + 1) * 512],
                  reads=("tabB",), writes=("tab",))
            for m in range(8):
                which = 0 if m < 4 else 1
                i = m % 4
                uk = f"qu{gp}" if which == 0 else f"ku{gp}"
                xu4 = qu4 if which == 0 else ku4
                sc0 = 0 if which == 0 else 2
                ta, tb_ = (0, 1) if which == 0 else (2, 3)
                for k in range(KC):
                    sh = 0 if k < 4 else -1
                    p.op("tensor", lambda e, m=m, k=k, sh=sh, gp=gp: e.matmul(
                        psA[:, 0:512], wq4[:, k, m * 128:(m + 1) * 128],
                        x1b3[:, k, 1 + sh:513 + sh], start=(k == 0), stop=(k == KC - 1)),
                        reads=("wq", "x1b"), writes=("psA",))
                for k in range(KC):
                    sh = 0 if k < 4 else -1
                    p.op("tensor", lambda e, m=m, k=k, sh=sh, gp=gp: e.matmul(
                        psG[:, 0:512], wr4[:, k, m * 128:(m + 1) * 128],
                        x1b3[:, k, 1 + sh:513 + sh], start=(k == 0), stop=(k == KC - 1)),
                        reads=("wrot", "x1b"), writes=("psG",))
                p.op("vector", lambda e: e.tensor_copy(qraw[:], psA[:, 0:512]),
                     reads=("psA", "qraw"), writes=("qraw",))
                p.op("vector", lambda e: e.tensor_copy(qrot[:], psG[:, 0:512]),
                     reads=("psG", "qrot"), writes=("qrot",))
                p.op("scalar", lambda e: e.activation(sq[:], qraw[:], AF.Square),
                     reads=("qraw", "sqb"), writes=("sq2", "sqb"))
                p.op("tensor", lambda e: e.matmul(
                    psS2[0:2, 0:512], honesr[:], sq[:], start=True, stop=True),
                    reads=("hones", "sq2"), writes=("psS2",))
                p.op("vector", lambda e: e.tensor_scalar(
                    normb[:], psS2[0:2, 0:512], 1e-24, None, ALU.max),
                    reads=("psS2", "nb"), writes=("nb",))
                p.op("scalar", lambda e: e.activation(normb[:], normb[:], AF.Ln),
                     reads=("nb",), writes=("nb",))
                p.op("scalar", lambda e: e.activation(normb[:], normb[:], AF.Exp, scale=f(-0.5)),
                     reads=("nb",), writes=("nb",))
                p.op("tensor", lambda e: e.matmul(
                    psS2[:, 0:512], bones[:], normb[:], start=True, stop=True),
                    reads=("bones", "nb"), writes=("psS2",))
                e1 = wst[:, 0:512]
                e2 = wst[:, 512:1024]
                p.op("vector", lambda e, e1=e1: e.tensor_tensor(e1, qraw[:], psS2[:, 0:512], ALU.mult),
                     reads=("qraw", "psS2", "wstage"), writes=("wstage",))
                p.op("vector", lambda e, e2=e2: e.tensor_tensor(e2, qrot[:], psS2[:, 0:512], ALU.mult),
                     reads=("qrot", "psS2", "wstage"), writes=("wstage",))
                p.op("vector", lambda e, e1=e1, sc0=sc0: e.tensor_scalar(
                    e1, e1, sqs[:, sc0:sc0 + 1], None, ALU.mult),
                    reads=("sqs", "wstage"), writes=("wstage",))
                p.op("vector", lambda e, e2=e2, sc0=sc0: e.tensor_scalar(
                    e2, e2, sqs[:, sc0 + 1:sc0 + 2], None, ALU.mult),
                    reads=("sqs", "wstage"), writes=("wstage",))
                p.op("vector", lambda e, e1=e1, ta=ta: e.tensor_tensor(
                    e1, e1, stab3[:, ta, :], ALU.mult),
                    reads=("tab", "wstage"), writes=("wstage",))
                p.op("vector", lambda e, e2=e2, tb_=tb_: e.tensor_tensor(
                    e2, e2, stab3[:, tb_, :], ALU.mult),
                    reads=("tab", "wstage"), writes=("wstage",))
                p.op("vector", lambda e, e1=e1, e2=e2, xu4=xu4, i=i, gp=gp: e.tensor_tensor(
                    xu4[:, gp, i, :], e1, e2, ALU.add),
                    reads=("wstage", uk), writes=(uk,))
            p.op("gpsimd", lambda e, gp=gp: e.memset(vt4[:, gp, :, :], 0.0),
                 reads=(f"vtF{gp}",), writes=(f"vt{gp}",))
            for h in range(8):
                p.op("gpsimd", lambda e, gp=gp, h=h: e.memset(
                    VOnesAP(vt, gp, h), 1.0), reads=(f"vt{gp}",), writes=(f"vt{gp}",))
            for h in range(8):
                for blk in range(4):
                    for k in range(KC):
                        sh = 0 if k < 4 else -1
                        p.op("tensor", lambda e, h=h, blk=blk, k=k, sh=sh, gp=gp: e.matmul(
                            psS[:, 0:64], x1b3[:, k, 1 + sh + blk * 128:1 + sh + (blk + 1) * 128],
                            wq4[:, k, 1024 + h * 64:1024 + (h + 1) * 64],
                            start=(k == 0), stop=(k == KC - 1)),
                            reads=("wq", "x1b"), writes=("psS",))
                    p.op("vector", lambda e, gp=gp, h=h, blk=blk: e.tensor_copy(
                        vt4[:, gp, h, blk * 65:blk * 65 + 64], psS[:, 0:64]),
                        reads=("psS", f"vt{gp}"), writes=(f"vt{gp}",))
            if g in qblocks and not L1_NO_ATTN:
                for qb in qblocks[g]:
                    for h in range(8):
                        pi = h // 2
                        base = 64 * (h % 2)
                        for kw in range(2):
                            kgp = (g - 1 + kw) % 2
                            ps_t = ps0 if kw == 0 else ps1
                            pkey = "ps0" if kw == 0 else "ps1"
                            p.op("tensor", lambda e, base=base, pi=pi, qb=qb, kgp=kgp, ps_t=ps_t, gp=gp: e.matmul(
                                ps_t[:, 0:512],
                                qu4[:, gp, pi, qb * 128:(qb + 1) * 128][base:base + 64, :],
                                ku4[:, kgp, pi, :][base:base + 64, :],
                                start=True, stop=True),
                                reads=(f"qu{gp}", f"ku{(g - 1) % 2}", f"ku{gp}", pkey + "E"), writes=(pkey,))
                        off = (3 - qb) * 128
                        mvar = smask[:, off:off + 1024]
                        p.op("vector", lambda e, mvar=mvar: e.tensor_tensor(
                            sc[:, 0:512], ps0[:, 0:512], mvar[:, 0:512], ALU.mult),
                            reads=("ps0", "amask", "scb"), writes=("scb", "ps0E"))
                        p.op("vector", lambda e, mvar=mvar: e.tensor_tensor(
                            sc[:, 512:1024], ps1[:, 0:512], mvar[:, 512:1024], ALU.mult),
                            reads=("ps1", "amask", "scb"), writes=("scb", "ps1E"))
                        if g == 2:
                            p.op("vector", lambda e: e.tensor_tensor(
                                sc[:], sc[:], smask[:, 1408:2432], ALU.mult),
                                reads=("amask", "scb"), writes=("scb",))
                        p.op("vector", lambda e: e.reduce_max(
                            rmax[:, 0:1].unsqueeze(2), sc[:].unsqueeze(1), axis=mybir.AxisListType.X),
                            reads=("scb",), writes=("rmaxb",))
                        p.op("vector", lambda e: e.tensor_scalar(
                            sc[:], sc[:], rmax[:, 0:1], None, ALU.subtract),
                            reads=("scb", "rmaxb"), writes=("scb",))
                        p.op("scalar", lambda e: e.activation(
                            probs[:], sc[:], AF.Exp, scale=f(8.0)),
                            reads=("scb", "prb"), writes=("prb",))
                        p.op("vector", lambda e, mvar=mvar: e.tensor_tensor(
                            probs[:], probs[:], mvar, ALU.mult),
                            reads=("prb", "amask"), writes=("prb",))
                        if g == 2:
                            p.op("vector", lambda e: e.tensor_tensor(
                                probs[:], probs[:], smask[:, 1408:2432], ALU.mult),
                                reads=("prb", "amask"), writes=("prb",))
                        for blk in range(8):
                            p.op("tensor", lambda e, blk=blk: e.transpose(
                                psT[:, 0:128], probs[:, blk * 128:(blk + 1) * 128], eyeb[:]),
                                reads=("prb", "eye", "psTE"), writes=("psT",))
                            p.op("vector", lambda e, blk=blk: e.tensor_copy(
                                probs[:, blk * 128:(blk + 1) * 128], psT[:, 0:128]),
                                reads=("psT", "prb"), writes=("prT", "prb", "psTE"))
                        for blk in range(8):
                            kw = blk // 4
                            kgp = (g - 1 + kw) % 2
                            p.op("tensor", lambda e, blk=blk, kgp=kgp, h=h: e.matmul(
                                psS2[0:65, 0:128], vt4[:, kgp, h, (blk % 4) * 65:(blk % 4 + 1) * 65],
                                probs[:, blk * 128:(blk + 1) * 128],
                                start=(blk == 0), stop=(blk == 7)),
                                reads=(f"vt{kgp}", "prT", "prb", "psS2E"), writes=("psS2",))
                        p.op("vector", lambda e: e.tensor_copy(sums[0:1, 0:128], psS2[64:65, 0:128]),
                             reads=("psS2", "sumsb"), writes=("sums1", "sumsb"))
                        p.op("vector", lambda e: e.tensor_copy(attnf[0:64, 0:128], sc[0:64, 0:128]),
                             reads=("scb", "attnf"), writes=("attnf",))
                        p.op("vector", lambda e: e.reciprocal(sums[0:1, 128:256], sums[0:1, 0:128]),
                             reads=("sums1", "sumsb"), writes=("sums2", "sumsb"))
                        p.op("tensor", lambda e: e.matmul(
                            ps2[0:64, 0:128],
                            onesrow[0:1, 0:64], sums[0:1, 128:256], start=True, stop=True),
                            reads=("ones", "sums2", "ps2E"), writes=("ps2",))
                        p.op("scalar", lambda e: e.copy(attnf[0:64, 0:128], psS2[0:64, 0:128]),
                             reads=("psS2", "attnf"), writes=("attnf", "psS2E"))
                        p.op("vector", lambda e, h=h: e.tensor_tensor(
                            atn[0:64, h * 128:(h + 1) * 128],
                            attnf[0:64, 0:128], ps2[0:64, 0:128], ALU.mult),
                            reads=("attnf", "ps2", "atnb"), writes=("atnb", "ps2E"))
                    for mo in range(KC):
                        for h2 in range(8):
                            p.op("tensor", lambda e, mo=mo, h2=h2: e.matmul(
                                ps2[:, 0:128],
                                wob[0:64, h2 * D + mo * 128:h2 * D + (mo + 1) * 128],
                                atn[0:64, h2 * 128:(h2 + 1) * 128],
                                start=(h2 == 0), stop=(h2 == 7)),
                                reads=("wo", "atnb", "ps2E"), writes=("ps2",))
                        if g >= 2:
                            p.op("vector", lambda e, mo=mo, qb=qb, gp=gp: e.tensor_tensor(
                                x23[:, mo, 1 + qb * 128:1 + (qb + 1) * 128], ps2[:, 0:128],
                                x13[:, mo, 1 + qb * 128:1 + (qb + 1) * 128], ALU.add),
                                reads=("ps2", "x1", "x2b"), writes=("x2b", "ps2E"))
                        else:
                            p.op("vector", lambda e, mo=mo, gp=gp: e.tensor_tensor(
                                x23[:, mo, 0:1], ps2[:, 127:128],
                                x13[:, mo, 512:513], ALU.add),
                                reads=("ps2", "x1", "x2b"), writes=("x2b", "ps2E"))
            p.op("vector", lambda e: e.tensor_copy(sq[:, 0:1], x13[:, 0, 0:1]),
                 reads=("x1", "x1b", "sqb"), writes=("x1F", "x1bF", "sqb"))
            if g >= 2 and not L1_NO_FFN:
                w = g - 2
                for kk in range(KC):
                    sh = 0 if kk < 4 else -1
                    p.op("vector", lambda e, kk=kk, sh=sh: e.tensor_copy(
                        rob3[:, kk, :], x23[:, kk, 1 + sh:513 + sh]),
                        reads=("x2b", "robb"), writes=(f"rob{kk}", "robb"))
                emit_ln(p, rob3, KC, 512, ones, onesrow, sq, stats, bcast, psS, psS2,
                        tag=f"fln{g}", out_keys=[f"rob{kk}" for kk in range(KC)])
                for fs in range(2):
                    fsl = slice(fs * FS, (fs + 1) * FS)
                    for kk in range(KC):
                        p.op("vector", lambda e, kk=kk, fsl=fsl: e.tensor_scalar(
                            lnb3[:, kk, :], rob3[:, kk, fsl], sgb[:, kk:kk + 1],
                            sgb[:, KC + kk:KC + kk + 1], ALU.mult, ALU.add),
                            reads=(f"rob{kk}", "gb", f"lnbF{kk}"), writes=(f"lnb{kk}",))
                    for j in range(NJ):
                        for hh in range(2):
                            p.dma("gpsimd", rearr(wst[:, 0:1024], "p (k c) -> p k c", k=4),
                                  rearr(w1p[512 * hh:512 * (hh + 1), j * 256:(j + 1) * 256],
                                        "(k p) c -> p k c", p=128),
                                  reads=("w1b", "wstage"), writes=("wstage1", "wstage"))
                            p.op("vector", lambda e, hh=hh: e.tensor_copy(
                                w1b[:, hh * 1024:(hh + 1) * 1024], wst[:, 0:1024]),
                                reads=("wstage1",), writes=("w1b",))
                        w1b3 = rearr(w1b[:], "p (k c) -> p k c", k=KC)
                        for k in range(KC):
                            p.op("tensor", lambda e, k=k: e.matmul(
                                psA[:, 0:FS], w1b3[:, k, 0:128], lnb3[:, k, :],
                                start=(k == 0), stop=(k == KC - 1)),
                                reads=("w1b",) + tuple(f"lnb{kk}" for kk in range(KC)),
                                writes=("psA",))
                        for k in range(KC):
                            p.op("tensor", lambda e, k=k: e.matmul(
                                psG[:, 0:FS], w1b3[:, k, 128:256], lnb3[:, k, :],
                                start=(k == 0), stop=(k == KC - 1)),
                                reads=("w1b",) + tuple(f"lnb{kk}" for kk in range(KC)),
                                writes=("psG",))
                        p.op("vector", lambda e, j=j: e.tensor_scalar(
                            gel[:], psG[:, 0:FS], sb1[:, NJ + j:NJ + j + 1], None, ALU.add),
                            reads=("psG", "gb", "gelb"), writes=("gelb",))
                        p.op("scalar", lambda e: e.activation(gel[:], gel[:], AF.Gelu),
                             reads=("gelb",), writes=("gelb",))
                        p.op("vector", lambda e, j=j: e.tensor_scalar(
                            sq[:, 0:FS], psA[:, 0:FS], sb1[:, j:j + 1], None, ALU.add),
                            reads=("psA", "gb", "sqb"), writes=("sq2", "sqb"))
                        p.op("vector", lambda e, j=j: e.tensor_tensor(
                            hb3[:, j, :], sq[:, 0:FS], gel[:], ALU.mult),
                            reads=("sq2", "gelb", "hbb"), writes=(f"hb{j}", "hbb"))
                    for mg in range(2):
                        banks = (ps0, ps1, ps2, psA)
                        bkeys = ("ps0", "ps1", "ps2", "psA")
                        for j in range(NJ):
                            p.dma("sync", w2b_stage(wst), w2p[j * 128:(j + 1) * 128, :],
                                  reads=("w2b", "wstage"), writes=("wstage2", "wstage"))
                            p.op("vector", lambda e: e.tensor_copy(w2b[:, 0:D], wst[:, 0:1024]),
                                 reads=("wstage2",), writes=("w2b",))
                            for mi in range(4):
                                mo = mg * 4 + mi
                                p.op("tensor", lambda e, mi=mi, mo=mo, j=j, banks=banks: e.matmul(
                                    banks[mi][:, 0:FS], w2b[:, mo * 128:(mo + 1) * 128],
                                    hb3[:, j, :], start=(j == 0), stop=(j == NJ - 1)),
                                    reads=("w2b", f"hb{j}", bkeys[mi] + "E"), writes=(bkeys[mi],))
                        for mi in range(4):
                            mo = mg * 4 + mi
                            p.op("vector", lambda e, mi=mi, mo=mo, banks=banks, fsl=fsl: e.tensor_scalar(
                                sq[:, 0:FS], banks[mi][:, 0:FS], sb2[:, mo:mo + 1], None, ALU.add),
                                reads=(bkeys[mi], "gb", "sqb"), writes=("sq2", "sqb", bkeys[mi] + "E"))
                            p.op("vector", lambda e, mi=mi, mo=mo, fsl=fsl, fs=fs: e.tensor_tensor(
                                rob3[:, mo, fsl], sq[:, 0:FS], x23[:, mo, 1 + fs * FS:1 + (fs + 1) * FS], ALU.add),
                                reads=("sq2", "x2b", f"rob{mo}", "robb2"),
                                writes=(f"robo{mo}{fs}", f"rob{mo}", "robb2"))
                    p.op("vector", lambda e: e.tensor_copy(gel[:, 0:1], lnb3[:, 0, 0:1]),
                         reads=tuple(f"lnb{kk}" for kk in range(KC)) + ("gelb",),
                         writes=tuple(f"lnbF{kk}" for kk in range(KC)) + ("gelb",))
                if final:
                    p.op("vector", lambda e: e.tensor_copy(x23[:, :, 0:1], x23[:, :, 512:513]),
                         reads=("x2b",), writes=("x2b",))
                    p.dma("sync", x23[:, :, 1:513],
                          rearr(r3c[:, w * 512:(w + 1) * 512], "(k p) t -> p k t", p=128),
                          reads=("x2b",), writes=("x2b", "r3cb"))
                    for kk in range(KC):
                        p.op("vector", lambda e, kk=kk: e.tensor_tensor(
                            rob3[:, kk, :], rob3[:, kk, :], x23[:, kk, 1:513], ALU.add),
                            reads=("r3cb", "x2b", f"robo{kk}0", f"robo{kk}1"), writes=(f"rob{kk}",))
                    emit_ln(p, rob3, KC, 512, ones, onesrow, sq, stats, bcast, psS, psS2,
                            tag=f"oln{g}", out_keys=[f"rob{kk}" for kk in range(KC)])
                    p.dma("sync", rearr(rout[:, w * 512:(w + 1) * 512], "(k p) t -> p k t", p=128),
                          rob3, reads=tuple(f"rob{kk}" for kk in range(KC)) + ("robb",),
                          writes=("robb",))
                else:
                    p.dma("sync", rearr(rout[:, w * 512:(w + 1) * 512], "(k p) t -> p k t", p=128),
                          rob3,
                          reads=tuple(f"robo{kk}{ff}" for kk in range(KC) for ff in range(2)) + ("robb",),
                          writes=("robb",))
                if not final:
                    p.op("vector", lambda e: e.tensor_copy(x23[:, :, 0:1], x23[:, :, 512:513]),
                         reads=("x2b",), writes=("x2b",))
        if L1_NO_FFN:
            for w2_ in range(4):
                p.dma("sync", rearr(rout[:, w2_ * 512:(w2_ + 1) * 512], "(k p) t -> p k t", p=128),
                      x23[:, :, 0:512], reads=("x2b",), writes=())
        p.emit(nc)
    st.close()
    return nc


def VOnesAP(vt, gp, h):
    from concourse.ap import AP as _AP
    rowlen = vt[:].shape[1]
    off = gp * 8 * 4 * 65 + h * 4 * 65 + 64
    return _AP(vt[:].tensor, off, [[rowlen, 128], [65, 4]])


def w2b_stage(wst):
    return wst[:, 0:1024]



def _l1_hosts():
    if "l1h" in _cache:
        return _cache["l1h"]
    DIM_HEAD, WINDOW, n = 64, 512, SEQ
    half = np.arange(0, DIM_HEAD, 2, dtype=np.float64)
    inv = 1.0 / 10000.0 ** (half / DIM_HEAD)
    t = np.arange(n, dtype=np.float64)
    freqs = t[:, None] * inv[None, :]
    freqs = np.concatenate([freqs, freqs], axis=-1)
    base = (half + 0.4 * DIM_HEAD) / (1.4 * DIM_HEAD)
    power = (t - n // 2) / float(WINDOW // 2)
    scale = base[None, :] ** power[:, None]
    scale = np.concatenate([scale, scale], axis=-1)
    cos, sin = np.cos(freqs), np.sin(freqs)
    tabAq = (cos * scale).T.astype(np.float32)
    tabBq = (sin * scale).T.astype(np.float32)
    tabAk = (cos / scale).T.astype(np.float32)
    tabBk = (sin / scale).T.astype(np.float32)
    am = np.zeros((128, 1408 + 1024), np.float32)
    kk_ = np.arange(128)[:, None]
    cc_ = np.arange(1408)[None, :]
    am[:, 0:1408] = ((cc_ >= 384 + kk_) & (cc_ <= 384 + kk_ + 512)).astype(np.float32)
    am[:, 1408:2432] = (np.arange(1024)[None, :] >= 512).astype(np.float32)
    _cache["l1h"] = (tabAq, tabBq, tabAk, tabBk, am)
    return _cache["l1h"]


def _perm_w1(w1l, b1l, w2l):
    NJ, HID = 22, 2730
    w1a, w1g = w1l[:, :HID], w1l[:, HID:]
    b1a, b1g = b1l[:HID], b1l[HID:]
    w1p = np.zeros((D, NJ * 256), np.float32)
    b1p = np.zeros((128, 2 * NJ), np.float32)
    w2p = np.zeros((NJ * 128, D), np.float32)
    for j in range(NJ):
        lo, hi = j * 128, min((j + 1) * 128, HID)
        nn = hi - lo
        w1p[:, j * 256:j * 256 + nn] = w1a[:, lo:hi]
        w1p[:, j * 256 + 128:j * 256 + 128 + nn] = w1g[:, lo:hi]
        b1p[0:nn, j] = b1a[lo:hi]
        b1p[0:nn, NJ + j] = b1g[lo:hi]
        w2p[j * 128:j * 128 + nn, :] = w2l[lo:hi, :]
    return w1p, b1p, w2p


def run_l1ffn(rinT, inp, layer, r3T=None):
    key = f"l1ffn{int(layer == 1)}"
    if key not in _cache:
        _cache[key] = mk_l1ffn(final=(layer == 1))
    nc = _cache[key]
    tabAq, tabBq, tabAk, tabBk, am = _l1_hosts()
    l = layer
    w1p, b1p, w2p = _perm_w1(inp["L1_ffn_w1"][l], inp["L1_ffn_b1"][l], inp["L1_ffn_w2"][l])
    qs = inp["L1_ffn_q_scale"][l]
    ks = inp["L1_ffn_k_scale"][l]
    qsc = np.stack([np.tile(qs, 2), np.tile(np.roll(qs, 32), 2)], axis=1).astype(np.float32)
    kscv = np.stack([np.tile(ks, 2), np.tile(np.roll(ks, 32), 2)], axis=1).astype(np.float32)
    halo = np.concatenate([np.zeros((D, 1025), np.float32), rinT], axis=1)
    eye = np.eye(128, dtype=np.float32)
    in_maps = []
    for c in range(NCORE):
        t0 = c * TPC
        s0 = (c % 4) * TPC
        chunk = np.ascontiguousarray(halo[:, t0:t0 + 1025 + TPC])
        if c % 4 == 0:
            chunk[:, 0:1025] = 0.0
        idx = np.arange(s0 - 1024, s0 + TPC)
        idx = np.clip(idx, 0, SEQ - 1)
        tabs = np.concatenate([np.tile(tt[:, idx], (2, 1))[None] for tt in
                               (tabAq, tabBq, tabAk, tabBk)], axis=0)
        tabs = np.ascontiguousarray(tabs.transpose(1, 0, 2).reshape(128, 4 * 3072))
        amc = am.copy()
        if c % 4 != 0:
            amc[:, 1408:2432] = 1.0
        d = dict(rin=chunk, wqkv=np.ascontiguousarray(inp["L1_ffn_wqkv"][l]),
                 qsc=qsc, ksc=kscv,
                 wo=np.ascontiguousarray(inp["L1_ffn_wo"][l]),
                 lng=np.ascontiguousarray(inp["L1_ffn_ln_g"][l].reshape(D, 1)),
                 lnb=np.ascontiguousarray(inp["L1_ffn_ln_b"][l].reshape(D, 1)),
                 w1p=w1p, b1p=b1p, w2p=w2p,
                 b2=np.ascontiguousarray(inp["L1_ffn_b2"][l].reshape(D, 1)),
                 tabs=tabs, amask=amc, eye=eye,
                 bonesd=np.concatenate([np.concatenate([np.ones((1, 64), np.float32), np.zeros((1, 64), np.float32)], 1),
                                        np.concatenate([np.zeros((1, 64), np.float32), np.ones((1, 64), np.float32)], 1)], 0),
                 r3c=np.ascontiguousarray((r3T if r3T is not None else rinT)[:, t0:t0 + TPC]))
        in_maps.append(d)
    res = run_bass_kernel_spmd(nc, in_maps, core_ids=list(range(NCORE)))
    return np.concatenate([r["rout"] for r in res.results], axis=1)



def mk_l0post(T=TPC, TSUB=512):
    KC = D // 128
    nsub = T // TSUB
    nc = bass.Bass("TRN2", target_bir_lowering=False, debug=False)
    xc_d = nc.dram_tensor("xc", [D, T], F32, kind="ExternalInput").ap()
    aoc = nc.dram_tensor("aoc", [D, T], BF16, kind="ExternalInput").ap()
    WO = nc.dram_tensor("WO", [D, D], F32, kind="ExternalInput").ap()
    r1 = nc.dram_tensor("r1", [D, T], F32, kind="ExternalOutput").ap()

    st = ExitStack()
    xb = st.enter_context(nc.sbuf_tensor("xb", [128, KC * TSUB], F32))
    ab = st.enter_context(nc.sbuf_tensor("ab", [128, KC * TSUB], BF16))
    wst = st.enter_context(nc.sbuf_tensor("wst", [128, 4096], F32))
    wo = st.enter_context(nc.sbuf_tensor("wo", [128, KC * D], BF16))
    rout = st.enter_context(nc.sbuf_tensor("rout", [128, KC * TSUB], F32))
    ones = st.enter_context(nc.sbuf_tensor("ones", [128, 1], F32))
    onesrow = st.enter_context(nc.sbuf_tensor("onesrow", [1, 128], F32))
    sq = st.enter_context(nc.sbuf_tensor("sq", [128, TSUB], F32))
    stats = st.enter_context(nc.sbuf_tensor("stats", [1, 4 * TSUB], F32))
    bcast = st.enter_context(nc.sbuf_tensor("bcast", [128, 2 * TSUB], F32))
    psA = st.enter_context(nc.psum_tensor("psA", [128, 512], F32))
    psS = st.enter_context(nc.psum_tensor("psS", [128, 512], F32))
    psS2 = st.enter_context(nc.psum_tensor("psS2", [128, 512], F32))
    if True:
        p = Prog()
        xb3 = rearr(xb[:], "p (k t) -> p k t", k=KC)
        ab3 = rearr(ab[:], "p (k t) -> p k t", k=KC)
        rout3 = rearr(rout[:], "p (k t) -> p k t", k=KC)
        p.op("gpsimd", lambda e: e.memset(ones[:], 1.0), writes=("ones",))
        p.op("gpsimd", lambda e: e.memset(onesrow[:], 1.0), writes=("ones",))
        emit_weight_load(p, WO, wst, wo, KC, D, "wo")
        for s in range(nsub):
            sl = slice(s * TSUB, (s + 1) * TSUB)
            p.dma("sync", xb3, rearr(xc_d[:, sl], "(k p) t -> p k t", p=128),
                  reads=("xbf",), writes=("xb",))
            p.dma("sync", ab3, rearr(aoc[:, sl], "(k p) t -> p k t", p=128),
                  reads=("abf",), writes=("ab",))
            for mo in range(KC):
                for k in range(KC):
                    p.op("tensor", lambda e, mo=mo, k=k: e.matmul(
                        psA[:, 0:TSUB], wo[:, k * D + mo * 128:k * D + (mo + 1) * 128],
                        ab3[:, k, :], start=(k == 0), stop=(k == KC - 1)),
                        reads=("wo", "ab"), writes=("psA",))
                p.op("vector", lambda e, mo=mo: e.tensor_tensor(
                    rout3[:, mo, :], psA[:, 0:TSUB], xb3[:, mo, :], ALU.add),
                    reads=("psA", "xb"), writes=(f"rout{mo}",))
            p.op("vector", lambda e: e.tensor_copy(sq[:, 0:1], xb3[:, 0, 0:1]),
                 reads=("xb", "ab", "sqbuf") + tuple(f"rout{mo}" for mo in range(KC)),
                 writes=("xbf", "abf", "sqbuf"))
            emit_ln(p, rout3, KC, TSUB, ones, onesrow, sq, stats, bcast, psS, psS2,
                    tag=f"ln{s}", out_keys=[f"rout{mo}" for mo in range(KC)])
            p.dma("sync", rearr(r1[:, sl], "(k p) t -> p k t", p=128), rout3,
                  reads=tuple(f"rout{mo}" for mo in range(KC)), writes=())
        p.emit(nc)
    st.close()
    return nc


def run_l0post(xT, ao, W_O):
    if "l0post" not in _cache:
        _cache["l0post"] = mk_l0post()
    nc = _cache["l0post"]
    in_maps = []
    for c in range(NCORE):
        sl = slice(c * TPC, (c + 1) * TPC)
        in_maps.append(dict(xc=np.ascontiguousarray(xT[:, sl]),
                            aoc=np.ascontiguousarray(ao[:, sl]),
                            WO=W_O))
    res = run_bass_kernel_spmd(nc, in_maps, core_ids=list(range(NCORE)))
    return np.concatenate([r["r1"] for r in res.results], axis=1)


# version 05 (final)
# speedup vs baseline: 1.0236x; 1.0152x over previous
from contextlib import ExitStack
import numpy as np
import ml_dtypes
import concourse.bass as bass
import concourse.mybir as mybir
from concourse.bass_utils import run_bass_kernel_spmd

F32 = mybir.dt.float32
BF16 = mybir.dt.bfloat16
AF = mybir.ActivationFunctionType
ALU = mybir.AluOpType
f = np.float32
BF = ml_dtypes.bfloat16

NCORE = 8
TOK = 16384
TPC = TOK // NCORE
D = 1024
SEQ = 8192

ENGINES = ("tensor", "vector", "scalar", "gpsimd", "sync")
DMA_Q = {"sync": "dma_sync", "gpsimd": "dma_gp", "scalar": "dma_act", "vector": "dma_dve"}


class Prog:

    def __init__(self):
        self.vins = []
        self.deps = []
        self.last_w = {}
        self.readers = {}

    def _record(self, veng, issue_eng, emitter, reads, writes, partial=False):
        vidx = len(self.vins)
        dep = set()
        for k in reads:
            for w in self.last_w.get(k, ()):
                dep.add(w)
        for k in writes:
            for w in self.last_w.get(k, ()):
                dep.add(w)
            for r in self.readers.get(k, ()):
                dep.add(r)
        self.vins.append((veng, issue_eng, emitter))
        self.deps.append(dep)
        for k in reads:
            self.readers.setdefault(k, []).append(vidx)
        for k in writes:
            if partial:
                self.last_w.setdefault(k, []).append(vidx)
            else:
                self.last_w[k] = [vidx]
            self.readers[k] = []
        return vidx

    def op(self, eng, emitter, reads=(), writes=()):
        return self._record(eng, eng, emitter, reads, writes)

    def dma(self, queue, out_ap, in_ap, reads=(), writes=(), partial=False, **kw):
        return self._record(
            DMA_Q[queue], queue,
            (lambda e: e.dma_start(out=out_ap, in_=in_ap, **kw)),
            reads, writes, partial=partial)

    SERIAL = False

    def emit(self, nc):
        n = len(self.vins)
        if Prog.SERIAL:
            for i in range(1, n):
                self.deps[i].add(i - 1)
        targets = set()
        for i in range(n):
            si = self.vins[i][0]
            for d in self.deps[i]:
                if self.vins[d][0] != si:
                    targets.add(d)
        for i in range(n):
            if self.vins[i][0].startswith("dma_"):
                targets.add(i)
        KSLOT = 4
        rank = {}
        counts = {}
        for i in range(n):
            if i in targets:
                ve = self.vins[i][0]
                c = counts.get(ve, 0)
                counts[ve] = c + 1
                if ve.startswith("dma_"):
                    rank[i] = (c % KSLOT, (c // KSLOT + 1) * 16)
                else:
                    rank[i] = c + 1

        sem_cms = {}
        sems = {}
        for ve, c in counts.items():
            if ve.startswith("dma_"):
                for s in range(min(KSLOT, c)):
                    cm = nc.semaphore(f"s_{ve}{s}")
                    sem_cms[(ve, s)] = cm
                    sems[(ve, s)] = cm.__enter__()
            else:
                cm = nc.semaphore(f"s_{ve}")
                sem_cms[ve] = cm
                sems[ve] = cm.__enter__()

        per_eng = {e: [] for e in ENGINES}
        for i in range(n):
            per_eng[self.vins[i][1]].append(i)

        def emit_engine(e, eng_name):
            done = {}
            def wait(key, v):
                if done.get(key, 0) < v:
                    e.wait_ge(sems[key], v)
                    done[key] = v
            for i in per_eng[eng_name]:
                ve = self.vins[i][0]
                for d in sorted(self.deps[i]):
                    dve = self.vins[d][0]
                    if dve == ve and not ve.startswith("dma_"):
                        continue
                    if dve.startswith("dma_"):
                        s, v = rank[d]
                        wait((dve, s), v)
                    else:
                        wait(dve, rank[d])
                if i in targets and ve.startswith("dma_"):
                    s, v = rank[i]
                    if v > 16:
                        wait((ve, s), v - 16)
                ins = self.vins[i][2](e)
                if i in targets:
                    if ve.startswith("dma_"):
                        s, v = rank[i]
                        ins.then_inc(sems[(ve, s)], 16)
                    else:
                        ins.then_inc(sems[ve], 1)
            if eng_name == "sync":
                for ve, c in sorted(counts.items()):
                    if ve.startswith("dma_"):
                        for s in range(min(KSLOT, c)):
                            total = (c - 1 - s) // KSLOT + 1
                            wait((ve, s), total * 16)

        block_cm = nc.Block()
        block = block_cm.__enter__()

        @block.sync
        def _(e):
            emit_engine(e, "sync")

        @block.tensor
        def _(e):
            emit_engine(e, "tensor")

        @block.vector
        def _(e):
            emit_engine(e, "vector")

        @block.scalar
        def _(e):
            emit_engine(e, "scalar")

        @block.gpsimd
        def _(e):
            emit_engine(e, "gpsimd")

        block_cm.__exit__(None, None, None)
        for cm in sem_cms.values():
            cm.__exit__(None, None, None)


def rearr(ap, pat, **kw):
    return ap.rearrange(pat, **kw)



def emit_weight_load(p, w_dram, w_stage, w_bf16, kchunks, dout, key,
                     queue="gpsimd", stage_key="wstage", stage_cols=4096):
    per = max(1, stage_cols // dout)
    for k0 in range(0, kchunks, per):
        k1 = min(k0 + per, kchunks)
        nk = k1 - k0
        p.dma(queue,
              rearr(w_stage[:, 0:nk * dout], "p (k d) -> p k d", k=nk),
              rearr(w_dram[k0 * 128:k1 * 128, :], "(k p) d -> p k d", p=128),
              reads=(), writes=(stage_key,))
        p.op("vector", lambda e, k0=k0, nk=nk: e.tensor_copy(
            w_bf16[:, k0 * dout:(k0 + nk) * dout], w_stage[:, 0:nk * dout]),
            reads=(stage_key,), writes=(key,))


def emit_ln(p, x3, KC, T, ones, onesrow, sq, stats, bcast, psS, psS2, tag, out_keys,
            gamma=None, beta=None, gb_key="gb", out3=None, out_keys2=None, eps=1e-5):
    D_ = KC * 128
    segs = [(a, min(a + 512, T)) for a in range(0, T, 512)]
    for k in range(KC):
        for (a, b) in segs:
            p.op("tensor", lambda e, k=k, a=a, b=b: e.matmul(
                psS[0:1, a:b], ones[:, 0:1], x3[:, k, a:b], start=(k == 0), stop=(k == KC - 1)),
                reads=("ones", out_keys[k]), writes=(tag + "psS", "psS"))
    p.op("vector", lambda e: e.tensor_scalar(stats[0:1, 0:T], psS[0:1, 0:T],
                                             1.0 / D_, None, ALU.mult),
         reads=(tag + "psS", "psS", "statsbuf"), writes=(tag + "mean", "statsbuf"))
    for (a, b) in segs:
        p.op("tensor", lambda e, a=a, b=b: e.matmul(psS[:, a:b], onesrow[0:1, :], stats[0:1, a:b],
                                                    start=True, stop=True),
             reads=("ones", tag + "mean"), writes=(tag + "psS", "psS"))
    p.op("vector", lambda e: e.tensor_copy(bcast[:, 0:T], psS[:, 0:T]),
         reads=(tag + "psS", "psS", "bcastbuf"), writes=(tag + "bmean", "bcastbuf"))
    for k in range(KC):
        p.op("vector", lambda e, k=k: e.tensor_tensor(
            x3[:, k, :], x3[:, k, :], bcast[:, 0:T], ALU.subtract),
            reads=(out_keys[k], tag + "bmean"), writes=(out_keys[k],))
    for k in range(KC):
        p.op("scalar", lambda e, k=k: e.activation(sq[:, 0:T], x3[:, k, :], AF.Square),
             reads=(out_keys[k], "sqbuf"), writes=(tag + "sq", "sqbuf"))
        for (a, b) in segs:
            p.op("tensor", lambda e, k=k, a=a, b=b: e.matmul(
                psS2[0:1, a:b], ones[:, 0:1], sq[:, a:b], start=(k == 0), stop=(k == KC - 1)),
                reads=("ones", tag + "sq"), writes=(tag + "psS2", "psS2"))
    p.op("scalar", lambda e: e.activation(stats[0:1, T:2 * T], psS2[0:1, 0:T],
                                          AF.Ln, bias=f(eps), scale=f(1.0 / D_)),
         reads=(tag + "psS2", "psS2", "statsbuf"), writes=(tag + "lnv", "statsbuf"))
    p.op("scalar", lambda e: e.activation(stats[0:1, 2 * T:3 * T], stats[0:1, T:2 * T],
                                          AF.Exp, scale=f(-0.5)),
         reads=(tag + "lnv", "statsbuf"), writes=(tag + "rstd", "statsbuf"))
    for (a, b) in segs:
        p.op("tensor", lambda e, a=a, b=b: e.matmul(psS2[:, a:b], onesrow[0:1, :],
                                                    stats[0:1, 2 * T + a:2 * T + b],
                                                    start=True, stop=True),
             reads=("ones", tag + "rstd"), writes=(tag + "psS2", "psS2"))
    p.op("vector", lambda e: e.tensor_copy(bcast[:, T:2 * T], psS2[:, 0:T]),
         reads=(tag + "psS2", "psS2", "bcastbuf"), writes=(tag + "brstd", "bcastbuf"))
    for k in range(KC):
        p.op("vector", lambda e, k=k: e.tensor_tensor(
            x3[:, k, :], x3[:, k, :], bcast[:, T:2 * T], ALU.mult),
            reads=(out_keys[k], tag + "brstd"), writes=(out_keys[k],))
        if gamma is not None:
            p.op("vector", lambda e, k=k: e.tensor_scalar(
                out3[:, k, :], x3[:, k, :], gamma[:, k:k + 1], beta[:, k:k + 1],
                ALU.mult, ALU.add),
                reads=(out_keys[k], gb_key), writes=(out_keys2[k],))



def mk_l0ffn(T=TPC, TSUB=512):
    KC = D // 128
    nsub = T // TSUB
    nc = bass.Bass("TRN2", target_bir_lowering=False, debug=False)
    r1 = nc.dram_tensor("r1", [D, 1 + T], F32, kind="ExternalInput").ap()
    WrT = nc.dram_tensor("WrT", [D, D], F32, kind="ExternalInput").ap()
    WkT = nc.dram_tensor("WkT", [D, D], F32, kind="ExternalInput").ap()
    WvT = nc.dram_tensor("WvT", [D, D], F32, kind="ExternalInput").ap()
    mur = nc.dram_tensor("mur", [D, 1], F32, kind="ExternalInput").ap()
    muk = nc.dram_tensor("muk", [D, 1], F32, kind="ExternalInput").ap()
    r2 = nc.dram_tensor("r2", [D, T], F32, kind="ExternalOutput").ap()

    st = ExitStack()
    xr1 = st.enter_context(nc.sbuf_tensor("xr1", [128, KC * (1 + T)], F32))
    wst = st.enter_context(nc.sbuf_tensor("wst", [128, 4096], F32))
    wr = st.enter_context(nc.sbuf_tensor("wr", [128, KC * D], BF16))
    wk = st.enter_context(nc.sbuf_tensor("wk", [128, KC * D], BF16))
    wv = st.enter_context(nc.sbuf_tensor("wv", [128, KC * D], BF16))
    smu = st.enter_context(nc.sbuf_tensor("smu", [128, 2 * KC], F32))
    xrb = st.enter_context(nc.sbuf_tensor("xrb", [128, KC * TSUB], BF16))
    xkb = st.enter_context(nc.sbuf_tensor("xkb", [128, KC * TSUB], BF16))
    dwork = st.enter_context(nc.sbuf_tensor("dwork", [128, TSUB], F32))
    rsig = st.enter_context(nc.sbuf_tensor("rsig", [128, KC * TSUB], F32))
    kact = st.enter_context(nc.sbuf_tensor("kact", [128, KC * TSUB], BF16))
    rout = st.enter_context(nc.sbuf_tensor("rout", [128, KC * TSUB], F32))
    ones = st.enter_context(nc.sbuf_tensor("ones", [128, 1], F32))
    onesrow = st.enter_context(nc.sbuf_tensor("onesrow", [1, 128], F32))
    sq = st.enter_context(nc.sbuf_tensor("sq", [128, TSUB], F32))
    stats = st.enter_context(nc.sbuf_tensor("stats", [1, 4 * TSUB], F32))
    bcast = st.enter_context(nc.sbuf_tensor("bcast", [128, 2 * TSUB], F32))
    psA = st.enter_context(nc.psum_tensor("psA", [128, 512], F32))
    psB = st.enter_context(nc.psum_tensor("psB", [128, 512], F32))
    psC = st.enter_context(nc.psum_tensor("psC", [128, 512], F32))
    psS = st.enter_context(nc.psum_tensor("psS", [128, 512], F32))
    psS2 = st.enter_context(nc.psum_tensor("psS2", [128, 512], F32))
    if True:
        p = Prog()
        x1 = rearr(xr1[:], "p (k t) -> p k t", k=KC)

        p.dma("sync", x1, rearr(r1, "(k p) t -> p k t", p=128), writes=("r1",))
        p.dma("sync", smu[:, 0:KC].unsqueeze(2),
              rearr(mur, "(k p) o -> p k o", p=128), writes=("mu",),
              allow_slow_non_contiguous=True)
        p.dma("sync", smu[:, KC:2 * KC].unsqueeze(2),
              rearr(muk, "(k p) o -> p k o", p=128), writes=("mu",),
              allow_slow_non_contiguous=True)
        emit_weight_load(p, WrT, wst, wr, KC, D, "wr")
        emit_weight_load(p, WkT, wst, wk, KC, D, "wk")
        emit_weight_load(p, WvT, wst, wv, KC, D, "wv")
        p.op("gpsimd", lambda e: e.memset(ones[:], 1.0), writes=("ones",))
        p.op("gpsimd", lambda e: e.memset(onesrow[:], 1.0), writes=("ones",))

        xrb3 = rearr(xrb[:], "p (k t) -> p k t", k=KC)
        xkb3 = rearr(xkb[:], "p (k t) -> p k t", k=KC)
        rsig3 = rearr(rsig[:], "p (k t) -> p k t", k=KC)
        kact3 = rearr(kact[:], "p (k t) -> p k t", k=KC)
        rout3 = rearr(rout[:], "p (k t) -> p k t", k=KC)

        for s in range(nsub):
            base = 1 + s * TSUB
            for k in range(KC):
                cur = x1[:, k, base:base + TSUB]
                prv = x1[:, k, base - 1:base - 1 + TSUB]
                p.op("vector", lambda e, cur=cur, prv=prv: e.tensor_tensor(
                    dwork[:], cur, prv, ALU.subtract), reads=("r1",), writes=("dw",))
                p.op("vector", lambda e, k=k, prv=prv: e.scalar_tensor_tensor(
                    xrb3[:, k, :], dwork[:], smu[:, k:k + 1], prv, ALU.mult, ALU.add),
                    reads=("dw", "mu", "r1"), writes=(f"xrb{k}",))
                p.op("vector", lambda e, k=k, prv=prv: e.scalar_tensor_tensor(
                    xkb3[:, k, :], dwork[:], smu[:, KC + k:KC + k + 1], prv, ALU.mult, ALU.add),
                    reads=("dw", "mu", "r1"), writes=(f"xkb{k}",))
            for m in range(KC):
                for k in range(KC):
                    p.op("tensor", lambda e, m=m, k=k: e.matmul(
                        psA[:, 0:TSUB], wr[:, k * D + m * 128:k * D + (m + 1) * 128],
                        xrb3[:, k, :], start=(k == 0), stop=(k == KC - 1)),
                        reads=("wr", f"xrb{k}"), writes=("psA",))
                p.op("scalar", lambda e, m=m: e.activation(
                    rsig3[:, m, :], psA[:, 0:TSUB], AF.Sigmoid),
                    reads=("psA",), writes=(f"rsig{m}",))
                for k in range(KC):
                    p.op("tensor", lambda e, m=m, k=k: e.matmul(
                        psB[:, 0:TSUB], wk[:, k * D + m * 128:k * D + (m + 1) * 128],
                        xkb3[:, k, :], start=(k == 0), stop=(k == KC - 1)),
                        reads=("wk", f"xkb{k}"), writes=("psB",))
                p.op("vector", lambda e, m=m: e.tensor_scalar(
                    sq[:], psB[:, 0:TSUB], 0.0, None, ALU.max),
                    reads=("psB", "sqbuf"), writes=("sqv", "sqbuf"))
                p.op("scalar", lambda e, m=m: e.activation(
                    kact3[:, m, :], sq[:], AF.Square),
                    reads=("sqv",), writes=(f"kact{m}",))
            for m in range(KC):
                for k in range(KC):
                    p.op("tensor", lambda e, m=m, k=k: e.matmul(
                        psC[:, 0:TSUB], wv[:, k * D + m * 128:k * D + (m + 1) * 128],
                        kact3[:, k, :], start=(k == 0), stop=(k == KC - 1)),
                        reads=("wv", f"kact{k}"), writes=("psC",))
                p.op("vector", lambda e, m=m: e.tensor_tensor(
                    rout3[:, m, :], rsig3[:, m, :], psC[:, 0:TSUB], ALU.mult),
                    reads=("psC", f"rsig{m}"), writes=(f"rout{m}",))
                p.op("vector", lambda e, m=m, base=base: e.tensor_tensor(
                    rout3[:, m, :], rout3[:, m, :], x1[:, m, base:base + TSUB], ALU.add),
                    reads=("r1", f"rout{m}"), writes=(f"rout{m}",))
            emit_ln(p, rout3, KC, TSUB, ones, onesrow, sq, stats, bcast, psS, psS2,
                    tag=f"ln{s}", out_keys=[f"rout{m}" for m in range(KC)])
            p.dma("sync", rearr(r2[:, s * TSUB:(s + 1) * TSUB], "(k p) t -> p k t", p=128),
                  rout3, reads=tuple(f"rout{m}" for m in range(KC)), writes=())
        p.emit(nc)
    st.close()
    return nc



_cache = {}


def run_l0ffn(r1T, WrT, WkT, WvT, mu_r, mu_k):
    if "l0ffn" not in _cache:
        _cache["l0ffn"] = mk_l0ffn()
    nc = _cache["l0ffn"]
    halo = np.concatenate([np.zeros((D, 1), np.float32), r1T], axis=1)
    in_maps = []
    for c in range(NCORE):
        t0 = c * TPC
        chunk = np.ascontiguousarray(halo[:, t0:t0 + TPC + 1])
        if c % 4 == 0:
            chunk[:, 0] = 0.0
        in_maps.append(dict(
            r1=chunk, WrT=WrT, WkT=WkT, WvT=WvT,
            mur=np.ascontiguousarray(mu_r.reshape(D, 1)),
            muk=np.ascontiguousarray(mu_k.reshape(D, 1))))
    res = run_bass_kernel_spmd(nc, in_maps, core_ids=list(range(NCORE)))
    return np.concatenate([r["r2"] for r in res.results], axis=1)


def kernel(**inputs):
    inp = inputs
    x = inp["L0_attn_x"]
    xT = np.ascontiguousarray(x.reshape(16384, 1024).T)
    ao = run_l0attn(xT, inp)
    r1T = run_l0post(xT, ao, inp["L0_attn_W_O"])
    r2T = run_l0ffn(r1T,
                    np.ascontiguousarray(inp["L0_ffn_W_r"].T),
                    np.ascontiguousarray(inp["L0_ffn_W_k"].T),
                    np.ascontiguousarray(inp["L0_ffn_W_v"].T),
                    inp["L0_ffn_mu_r"], inp["L0_ffn_mu_k"])
    ssm, dt, zz, Bn, Cn = run_mambapre(r2T, inp)
    y = run_scan(ssm, dt, Bn, Cn, inp["L1_attn_A_log"], inp["L1_attn_D_param"])
    r3T = run_mambapost(y, zz, r2T, inp["L1_attn_W_out"])
    r4T = run_l1ffn(r3T, inp, layer=0)
    outT = run_l1ffn(r4T, inp, layer=1, r3T=r3T)
    return np.ascontiguousarray(outT.T).reshape(2, 8192, 1024).astype(np.float32)



def mk_mambapre(T=TPC, TSUB=512):
    KC = D // 128
    MC = 2048 // 128
    nsub = T // TSUB
    nc = bass.Bass("TRN2", target_bir_lowering=False, debug=False)
    r2 = nc.dram_tensor("r2", [D, 3 + T], F32, kind="ExternalInput").ap()
    WinT = nc.dram_tensor("WinT", [D, 4096], F32, kind="ExternalInput").ap()
    gam = nc.dram_tensor("gam", [D, 1], F32, kind="ExternalInput").ap()
    bet = nc.dram_tensor("bet", [D, 1], F32, kind="ExternalInput").ap()
    cw = nc.dram_tensor("cw", [2048, 4], F32, kind="ExternalInput").ap()
    cb = nc.dram_tensor("cb", [2048, 1], F32, kind="ExternalInput").ap()
    WxT = nc.dram_tensor("WxT", [2048, 33], F32, kind="ExternalInput").ap()
    WdtT = nc.dram_tensor("WdtT", [33, 2048], F32, kind="ExternalInput").ap()
    bdt = nc.dram_tensor("bdt", [2048, 1], F32, kind="ExternalInput").ap()
    hm = nc.dram_tensor("hm", [1, 3], F32, kind="ExternalInput").ap()
    ssm_o = nc.dram_tensor("ssm_o", [2048, T], F32, kind="ExternalOutput").ap()
    dt_o = nc.dram_tensor("dt_o", [2048, T], F32, kind="ExternalOutput").ap()
    z_o = nc.dram_tensor("z_o", [2048, T], F32, kind="ExternalOutput").ap()
    B_o = nc.dram_tensor("B_o", [16, T], F32, kind="ExternalOutput").ap()
    C_o = nc.dram_tensor("C_o", [16, T], F32, kind="ExternalOutput").ap()

    st = ExitStack()
    r2c = st.enter_context(nc.sbuf_tensor("r2c", [128, KC * (TSUB + 3)], F32))
    xnb = st.enter_context(nc.sbuf_tensor("xnb", [128, KC * (TSUB + 3)], BF16))
    wst = st.enter_context(nc.sbuf_tensor("wst", [128, 4096], F32))
    wib = st.enter_context(nc.sbuf_tensor("wib", [128, 2, KC * 512], BF16))
    sgb = st.enter_context(nc.sbuf_tensor("sgb", [128, 2 * KC], F32))
    scw = st.enter_context(nc.sbuf_tensor("scw", [128, MC * 4], F32))
    scb = st.enter_context(nc.sbuf_tensor("scb", [128, MC], F32))
    sbdt = st.enter_context(nc.sbuf_tensor("sbdt", [128, MC], F32))
    swx = st.enter_context(nc.sbuf_tensor("swx", [128, MC * 33], BF16))
    swdt = st.enter_context(nc.sbuf_tensor("swdt", [33, 2048], BF16))
    shm = st.enter_context(nc.sbuf_tensor("shm", [128, 3], F32))
    xsp = st.enter_context(nc.sbuf_tensor("xsp", [128, MC * (TSUB + 3)], F32))
    ssm = st.enter_context(nc.sbuf_tensor("ssm", [128, MC * TSUB], F32))
    ssmb = st.enter_context(nc.sbuf_tensor("ssmb", [128, MC * TSUB], BF16))
    zt = st.enter_context(nc.sbuf_tensor("zt", [128, 2, TSUB], F32))
    dtt = st.enter_context(nc.sbuf_tensor("dtt", [128, 2, TSUB], F32))
    dtw = st.enter_context(nc.sbuf_tensor("dtw", [128, 3, TSUB], F32))
    xp = st.enter_context(nc.sbuf_tensor("xp", [64, TSUB], F32))
    xpb = st.enter_context(nc.sbuf_tensor("xpb", [64, TSUB], BF16))
    ones = st.enter_context(nc.sbuf_tensor("ones", [128, 1], F32))
    onesrow = st.enter_context(nc.sbuf_tensor("onesrow", [1, 128], F32))
    sq = st.enter_context(nc.sbuf_tensor("sq", [128, TSUB + 3], F32))
    stats = st.enter_context(nc.sbuf_tensor("stats", [1, 4 * (TSUB + 3)], F32))
    bcast = st.enter_context(nc.sbuf_tensor("bcast", [128, 2 * (TSUB + 3)], F32))
    psA = st.enter_context(nc.psum_tensor("psA", [128, 512], F32))
    psB = st.enter_context(nc.psum_tensor("psB", [128, 512], F32))
    psS = st.enter_context(nc.psum_tensor("psS", [128, 512], F32))
    psS2 = st.enter_context(nc.psum_tensor("psS2", [128, 512], F32))
    if True:
        p = Prog()
        r2c3 = rearr(r2c[:], "p (k t) -> p k t", k=KC)
        xnb3 = rearr(xnb[:], "p (k t) -> p k t", k=KC)
        xsp3 = rearr(xsp[:], "p (k t) -> p k t", k=MC)
        ssm3 = rearr(ssm[:], "p (k t) -> p k t", k=MC)
        ssmb3 = rearr(ssmb[:], "p (k t) -> p k t", k=MC)

        p.dma("sync", sgb[:, 0:KC].unsqueeze(2), rearr(gam, "(k p) o -> p k o", p=128),
              writes=("gb",), allow_slow_non_contiguous=True)
        p.dma("sync", sgb[:, KC:2 * KC].unsqueeze(2), rearr(bet, "(k p) o -> p k o", p=128),
              writes=("gb",), allow_slow_non_contiguous=True)
        p.dma("sync", rearr(scw[:], "p (k j) -> p k j", k=MC), rearr(cw, "(k p) j -> p k j", p=128),
              writes=("cw",))
        p.dma("sync", scb[:].unsqueeze(2), rearr(cb, "(k p) o -> p k o", p=128),
              writes=("cw",), allow_slow_non_contiguous=True)
        p.dma("sync", sbdt[:].unsqueeze(2), rearr(bdt, "(k p) o -> p k o", p=128),
              writes=("cw",), allow_slow_non_contiguous=True)
        p.dma("sync", shm[:], hm.partition_broadcast(128), writes=("hm",))
        p.op("gpsimd", lambda e: e.memset(ones[:], 1.0), writes=("ones",))
        p.op("gpsimd", lambda e: e.memset(onesrow[:], 1.0), writes=("ones",))
        emit_weight_load(p, WxT, wst, swx, MC, 33, "wx")
        p.dma("sync", wst[0:33, 0:2048], WdtT, writes=("wstage",))
        p.op("vector", lambda e: e.tensor_copy(swdt[:], wst[0:33, 0:2048]),
             reads=("wstage",), writes=("wdt",))

        for s in range(nsub):
            par = s % 2
            p.dma("sync", r2c3, rearr(r2[:, s * TSUB:s * TSUB + TSUB + 3], "(k p) t -> p k t", p=128),
                  reads=("r2c_free",), writes=("r2c",))
            emit_ln(p, r2c3[:, :, 0:3], KC, 3, ones, onesrow, sq, stats, bcast, psS, psS2,
                    tag=f"mlnh{s}", out_keys=["r2c"] * KC,
                    gamma=sgb[:, 0:KC], beta=sgb[:, KC:2 * KC], gb_key="gb",
                    out3=xnb3[:, :, 0:3], out_keys2=[f"xnb{k}" for k in range(KC)])
            emit_ln(p, r2c3[:, :, 3:3 + TSUB], KC, TSUB, ones, onesrow, sq, stats, bcast, psS, psS2,
                    tag=f"mln{s}", out_keys=["r2c"] * KC,
                    gamma=sgb[:, 0:KC], beta=sgb[:, KC:2 * KC], gb_key="gb",
                    out3=xnb3[:, :, 3:3 + TSUB], out_keys2=[f"xnb{k}" for k in range(KC)])
            p.op("vector", lambda e: e.tensor_copy(sq[:, 0:1], r2c3[:, 0, 0:1]),
                 reads=("r2c", "sqbuf") + tuple(f"xnb{k}" for k in range(KC)),
                 writes=("r2c_free", "sqbuf"))
            for mg in range(8):
                gpar = mg % 2
                p.dma("gpsimd", rearr(wst[:], "p (k d) -> p k d", k=KC),
                      rearr(WinT[:, mg * 512:(mg + 1) * 512], "(k p) d -> p k d", p=128),
                      reads=(f"wib{gpar}",), writes=("wstage",)) if s == 0 else None
                if s == 0:
                    p.op("vector", lambda e, gpar=gpar: e.tensor_copy(
                        wib[:, gpar, :], wst[:]), reads=("wstage",), writes=(f"wib{gpar}",))
                if s > 0:
                    p.dma("gpsimd", rearr(wst[:], "p (k d) -> p k d", k=KC),
                          rearr(WinT[:, mg * 512:(mg + 1) * 512], "(k p) d -> p k d", p=128),
                          reads=(f"wib{gpar}",), writes=("wstage",))
                    p.op("vector", lambda e, gpar=gpar: e.tensor_copy(
                        wib[:, gpar, :], wst[:]), reads=("wstage",), writes=(f"wib{gpar}",))
                for mi in range(4):
                    m = mg * 4 + mi
                    for k in range(KC):
                        p.op("tensor", lambda e, gpar=gpar, mi=mi, k=k: e.matmul(
                            psA[:, 0:TSUB],
                            wib[:, gpar, k * 512 + mi * 128:k * 512 + (mi + 1) * 128],
                            xnb3[:, k, 3:3 + TSUB], start=(k == 0), stop=(k == KC - 1)),
                            reads=(f"wib{gpar}",) + tuple(f"xnb{k}" for k in range(KC)),
                            writes=("psA",))
                    if m < MC:
                        if s == 0:
                            for k in range(KC):
                                p.op("tensor", lambda e, gpar=gpar, mi=mi, k=k: e.matmul(
                                    psB[:, 0:3],
                                    wib[:, gpar, k * 512 + mi * 128:k * 512 + (mi + 1) * 128],
                                    xnb3[:, k, 0:3], start=(k == 0), stop=(k == KC - 1)),
                                    reads=(f"wib{gpar}",) + tuple(f"xnb{k}" for k in range(KC)),
                                    writes=("psB",))
                            p.op("vector", lambda e, m=m: e.tensor_tensor(
                                xsp3[:, m, 0:3], psB[:, 0:3], shm[:], ALU.mult),
                                reads=("psB", "hm"), writes=(f"xsp{m}",))
                        p.op("vector", lambda e, m=m: e.tensor_copy(
                            xsp3[:, m, 3:3 + TSUB], psA[:, 0:TSUB]),
                            reads=("psA",), writes=(f"xsp{m}",))
                    else:
                        zpar = m % 2
                        p.op("vector", lambda e, zpar=zpar: e.tensor_copy(
                            zt[:, zpar, :], psA[:, 0:TSUB]),
                            reads=("psA", f"zt{zpar}"), writes=(f"zt{zpar}",))
                        p.dma("gpsimd", rearr(z_o[(m - MC) * 128:(m - MC + 1) * 128,
                                              s * TSUB:(s + 1) * TSUB], "p t -> p t"),
                              zt[:, zpar, :], reads=(f"zt{zpar}",), writes=())
            for m in range(MC):
                p.op("vector", lambda e, m=m: e.tensor_scalar(
                    sq[:, 0:TSUB], xsp3[:, m, 0:TSUB], scw[:, m * 4:m * 4 + 1], None, ALU.mult),
                    reads=(f"xsp{m}", "cw", "sqbuf"), writes=("cacc", "sqbuf"))
                for j in (1, 2, 3):
                    p.op("vector", lambda e, m=m, j=j: e.scalar_tensor_tensor(
                        sq[:, 0:TSUB], xsp3[:, m, j:j + TSUB], scw[:, m * 4 + j:m * 4 + j + 1],
                        sq[:, 0:TSUB], ALU.mult, ALU.add),
                        reads=(f"xsp{m}", "cw", "cacc", "sqbuf"), writes=("cacc", "sqbuf"))
                p.op("scalar", lambda e, m=m: e.activation(
                    ssm3[:, m, :], sq[:, 0:TSUB], AF.Silu, bias=scb[:, m:m + 1]),
                    reads=("cacc", "cw"), writes=(f"ssm{m}",))
                p.op("vector", lambda e, m=m: e.tensor_copy(ssmb3[:, m, :], ssm3[:, m, :]),
                     reads=(f"ssm{m}",), writes=(f"ssmb{m}",))
                p.dma("sync", ssm_o[m * 128:(m + 1) * 128, s * TSUB:(s + 1) * TSUB],
                      ssm3[:, m, :], reads=(f"ssm{m}",), writes=())
                p.op("vector", lambda e, m=m: e.tensor_copy(
                    xsp3[:, m, 0:3], xsp3[:, m, TSUB:TSUB + 3]),
                    reads=(f"xsp{m}",), writes=(f"xsp{m}",))
            for k in range(MC):
                p.op("tensor", lambda e, k=k: e.matmul(
                    psB[0:33, 0:TSUB], swx[:, k * 33:(k + 1) * 33], ssmb3[:, k, :],
                    start=(k == 0), stop=(k == MC - 1)),
                    reads=("wx", f"ssmb{k}"), writes=("psB",))
            p.op("vector", lambda e: e.tensor_copy(xp[0:33, :], psB[0:33, 0:TSUB]),
                 reads=("psB",), writes=("xp",))
            p.op("vector", lambda e: e.tensor_copy(xpb[0:33, :], xp[0:33, :]),
                 reads=("xp",), writes=("xpb",))
            p.dma("sync", B_o[:, s * TSUB:(s + 1) * TSUB], xp[1:17, :], reads=("xp",), writes=())
            p.dma("sync", C_o[:, s * TSUB:(s + 1) * TSUB], xp[17:33, :], reads=("xp",), writes=())
            for m in range(MC):
                p.op("tensor", lambda e, m=m: e.matmul(
                    psA[:, 0:TSUB], swdt[0:33, m * 128:(m + 1) * 128], xpb[0:33, :],
                    start=True, stop=True),
                    reads=("wdt", "xpb"), writes=("psA",))
                p.op("vector", lambda e, m=m: e.tensor_scalar(
                    dtw[:, 0, :], psA[:, 0:TSUB], sbdt[:, m:m + 1], None, ALU.add),
                    reads=("psA", "cw"), writes=("dtw0",))
                p.op("scalar", lambda e: e.activation(dtw[:, 1, :], dtw[:, 0, :], AF.Abs),
                     reads=("dtw0",), writes=("dtw1",))
                p.op("scalar", lambda e: e.activation(dtw[:, 2, :], dtw[:, 1, :], AF.Exp, scale=f(-1.0)),
                     reads=("dtw1",), writes=("dtw2",))
                p.op("scalar", lambda e: e.activation(dtw[:, 1, :], dtw[:, 2, :], AF.Ln, bias=f(1.0)),
                     reads=("dtw2",), writes=("dtw1",))
                p.op("vector", lambda e: e.tensor_scalar(dtw[:, 2, :], dtw[:, 0, :], 0.0, None, ALU.max),
                     reads=("dtw0",), writes=("dtw2",))
                dpar = m % 2
                p.op("vector", lambda e, dpar=dpar: e.scalar_tensor_tensor(
                    dtt[:, dpar, :], dtw[:, 2, :], 0.001, dtw[:, 1, :], ALU.add, ALU.add),
                    reads=("dtw2", "dtw1", f"dtt{dpar}"), writes=(f"dtt{dpar}",))
                p.dma("gpsimd", dt_o[m * 128:(m + 1) * 128, s * TSUB:(s + 1) * TSUB],
                      dtt[:, dpar, :], reads=(f"dtt{dpar}",), writes=())
        p.emit(nc)
    st.close()
    return nc


def run_mambapre(r2T, inp):
    if "mambapre" not in _cache:
        _cache["mambapre"] = mk_mambapre()
    nc = _cache["mambapre"]
    halo = np.concatenate([np.zeros((D, 3), np.float32), r2T], axis=1)
    WinT = np.ascontiguousarray(inp["L1_attn_W_in"].T)
    WxT = np.ascontiguousarray(inp["L1_attn_W_xproj"].T)
    WdtT = np.ascontiguousarray(inp["L1_attn_W_dt"].T)
    in_maps = []
    for c in range(NCORE):
        t0 = c * TPC
        chunk = np.ascontiguousarray(halo[:, t0:t0 + TPC + 3])
        m = np.ones((1, 3), np.float32)
        if c % 4 == 0:
            chunk[:, 0:3] = 0.0
            m[:] = 0.0
        in_maps.append(dict(
            r2=chunk, WinT=WinT,
            gam=np.ascontiguousarray(inp["L1_attn_ln_gamma"].reshape(D, 1)),
            bet=np.ascontiguousarray(inp["L1_attn_ln_beta"].reshape(D, 1)),
            cw=np.ascontiguousarray(inp["L1_attn_conv_w"][:, 0, :]),
            cb=np.ascontiguousarray(inp["L1_attn_conv_b"].reshape(2048, 1)),
            WxT=WxT, WdtT=WdtT,
            bdt=np.ascontiguousarray(inp["L1_attn_b_dt"].reshape(2048, 1)),
            hm=m))
    res = run_bass_kernel_spmd(nc, in_maps, core_ids=list(range(NCORE)))
    ssm = np.concatenate([r["ssm_o"] for r in res.results], axis=1)
    dt = np.concatenate([r["dt_o"] for r in res.results], axis=1)
    zz = np.concatenate([r["z_o"] for r in res.results], axis=1)
    Bn = np.concatenate([r["B_o"] for r in res.results], axis=1)
    Cn = np.concatenate([r["C_o"] for r in res.results], axis=1)
    return ssm, dt, zz, Bn, Cn



DBG_SCAN = False


def mk_scan(L=SEQ, TP=128):
    nch = 64
    nc = bass.Bass("TRN2", target_bir_lowering=False, debug=False)
    ssm = nc.dram_tensor("ssm", [256, 2 * L], F32, kind="ExternalInput").ap()
    dts = nc.dram_tensor("dts", [256, 2 * L], F32, kind="ExternalInput").ap()
    Bn = nc.dram_tensor("Bn", [16, 2 * L], F32, kind="ExternalInput").ap()
    Cn = nc.dram_tensor("Cn", [16, 2 * L], F32, kind="ExternalInput").ap()
    Alog = nc.dram_tensor("Alog", [256, 16], F32, kind="ExternalInput").ap()
    Dp = nc.dram_tensor("Dp", [256, 1], F32, kind="ExternalInput").ap()
    y_o = nc.dram_tensor("y_o", [256, 2 * L], F32, kind="ExternalOutput").ap()
    dbg = nc.dram_tensor("dbg", [128, 7 * 64 * TP], F32, kind="ExternalOutput").ap() if DBG_SCAN else None

    from concourse.ap import AP as RawAP

    def chunk_ap(t_ap, t0, inner, TPc):
        if inner == 4:
            return RawAP(t_ap.tensor, t0, [[L, 2], [4 * 2 * L, 64], [2 * L, 4], [1, TPc]])
        else:
            return RawAP(t_ap.tensor, t0, [[0, 64], [2 * L, 16], [1, TPc]])
    Alog3 = RawAP(Alog.tensor, 0, [[0, 2], [64, 64], [1, 64]])
    Dp3 = RawAP(Dp.tensor, 0, [[0, 2], [4, 64], [1, 4]])

    st = ExitStack()
    atile = st.enter_context(nc.sbuf_tensor("atile", [128, 64], F32))
    ninva = st.enter_context(nc.sbuf_tensor("ninva", [128, 64], F32))
    dtile = st.enter_context(nc.sbuf_tensor("dtile", [128, 4], F32))
    tmp64 = st.enter_context(nc.sbuf_tensor("tmp64", [128, 64], F32))
    dtc_b = [st.enter_context(nc.sbuf_tensor(f"dtc{i}", [128, 4 * TP], F32)) for i in range(2)]
    xc_b = [st.enter_context(nc.sbuf_tensor(f"xcb{i}", [128, 4 * TP], F32)) for i in range(2)]
    brep_b = [st.enter_context(nc.sbuf_tensor(f"brep{i}", [128, 16 * TP], F32)) for i in range(2)]
    crep_b = [st.enter_context(nc.sbuf_tensor(f"crep{i}", [128, 16 * TP], F32)) for i in range(2)]
    ybuf_b = [st.enter_context(nc.sbuf_tensor(f"ybuf{i}", [128, 4 * TP], F32)) for i in range(2)]
    abuf = st.enter_context(nc.sbuf_tensor("abuf", [128, 64 * TP], F32))
    ubuf = st.enter_context(nc.sbuf_tensor("ubuf", [128, 64 * TP], F32))
    obuf = st.enter_context(nc.sbuf_tensor("obuf", [128, 64 * TP], F32))
    hist = st.enter_context(nc.sbuf_tensor("hist", [128, 64 * TP], F32))
    ysm = st.enter_context(nc.sbuf_tensor("ysm", [128, 4 * TP], F32))
    ydx = st.enter_context(nc.sbuf_tensor("ydx", [128, 4 * TP], F32))
    if True:
        p = Prog()
        p.dma("sync", atile[:], Alog3, writes=("alog",))
        p.dma("sync", dtile[:], Dp3, writes=("dtile",))
        p.op("scalar", lambda e: e.activation(atile[:], atile[:], AF.Exp),
             reads=("alog",), writes=("alog",))
        p.op("vector", lambda e: e.tensor_scalar(atile[:], atile[:], -1.0, None, ALU.mult),
             reads=("alog",), writes=("atile",))
        p.op("vector", lambda e: e.memset(hist[:, (TP - 1) * 64:TP * 64], 0.0),
             writes=("hist",))
        p.op("vector", lambda e: e.memset(obuf[:], 0.0), writes=("omd",))
        p.op("vector", lambda e: e.tensor_scalar(tmp64[:], atile[:], 1e-8, None, ALU.add),
             reads=("atile",), writes=("t64",))
        p.op("vector", lambda e: e.memset(ubuf[:], 0.0), writes=("ubuf",))
        p.op("vector", lambda e: e.reciprocal(ninva[:], tmp64[:]),
             reads=("t64",), writes=("ninva",))
        p.op("vector", lambda e: e.memset(abuf[:], 0.0), writes=("aexp",))
        p.op("vector", lambda e: e.tensor_scalar(ninva[:], ninva[:], -1.0, None, ALU.mult),
             reads=("ninva",), writes=("ninva",))
        p.op("vector", lambda e: e.memset(ysm[:], 0.0), writes=("ysm",))

        a4 = rearr(abuf[:], "p (t c n) -> p t c n", t=TP, c=4)
        u4 = rearr(ubuf[:], "p (t c n) -> p t c n", t=TP, c=4)
        o4 = rearr(obuf[:], "p (t c n) -> p t c n", t=TP, c=4)
        h4 = rearr(hist[:], "p (t c n) -> p t c n", t=TP, c=4)
        A_ap = rearr(atile[:], "p (c n) -> p c n", c=4).unsqueeze(1).broadcast_to([128, TP, 4, 16])
        NI_ap = rearr(ninva[:], "p (c n) -> p c n", c=4).unsqueeze(1).broadcast_to([128, TP, 4, 16])
        D_ap = dtile[:].unsqueeze(1).broadcast_to([128, TP, 4])

        for c in range(nch):
            par = c % 2
            t0 = c * TP
            p.dma("sync", rearr(dtc_b[par][:], "p (c t) -> p c t", c=4).opt(),
                  chunk_ap(dts, t0, 4, TP), reads=(f"dtcF{par}",), writes=(f"dtc{par}",))
            p.dma("sync", rearr(xc_b[par][:], "p (c t) -> p c t", c=4).opt(),
                  chunk_ap(ssm, t0, 4, TP), reads=(f"xcF{par}",), writes=(f"xc{par}",))
            for bb in range(2):
                p.dma("sync", rearr(brep_b[par][bb * 64:(bb + 1) * 64, :], "p (n t) -> p n t", n=16),
                      chunk_ap(Bn, bb * L + t0, 16, TP), reads=(f"brF{par}",), writes=(f"br{par}",), partial=True)
                p.dma("sync", rearr(crep_b[par][bb * 64:(bb + 1) * 64, :], "p (n t) -> p n t", n=16),
                      chunk_ap(Cn, bb * L + t0, 16, TP), reads=(f"crF{par}",), writes=(f"cr{par}",), partial=True)
            dt_ap = rearr(dtc_b[par][:], "p (c t) -> p t c", c=4).unsqueeze(3).broadcast_to([128, TP, 4, 16])
            x_ap = rearr(xc_b[par][:], "p (c t) -> p t c", c=4).unsqueeze(3).broadcast_to([128, TP, 4, 16])
            xtc_ap = rearr(xc_b[par][:], "p (c t) -> p t c", c=4)
            b_ap = rearr(brep_b[par][:], "p (n t) -> p t n", n=16).unsqueeze(2).broadcast_to([128, TP, 4, 16])
            c_ap = rearr(crep_b[par][:], "p (n t) -> p t n", n=16).unsqueeze(2).broadcast_to([128, TP, 4, 16])
            p.op("vector", lambda e, dt_ap=dt_ap: e.tensor_tensor(o4, A_ap, dt_ap, ALU.mult),
                 reads=("atile", f"dtc{par}", "omd", "och"), writes=("am", f"dtcF{par}"))
            p.op("scalar", lambda e: e.activation(abuf[:], obuf[:], AF.Exp),
                 reads=("am", "aexp"), writes=("aexp",))
            p.op("vector", lambda e, b_ap=b_ap, x_ap=x_ap: e.tensor_tensor(u4, b_ap, x_ap, ALU.mult),
                 reads=(f"br{par}", f"xc{par}", "ubuf", "uok"), writes=("ubuf", f"brF{par}"))
            p.op("vector", lambda e: e.tensor_tensor(u4, u4, NI_ap, ALU.mult),
                 reads=("ubuf", "ninva"), writes=("ubuf",))
            p.op("vector", lambda e: e.tensor_scalar(obuf[:], abuf[:], -1.0, None, ALU.add),
                 reads=("aexp", "am"), writes=("omd",))
            p.op("vector", lambda e: e.tensor_tensor(ubuf[:], ubuf[:], obuf[:], ALU.mult),
                 reads=("ubuf", "omd"), writes=("ubuf", "uok"))
            for t in range(TP):
                prev = hist[:, (t - 1) * 64:t * 64] if t > 0 else hist[:, (TP - 1) * 64:TP * 64]
                rd = ("aexp", "ubuf", "uok", "hist") if t == 0 else ()
                wr = ()
                if t == TP - 1:
                    rd = ("aexp", "ubuf", "uok")
                    wr = ("hist",)
                p.op("vector", lambda e, t=t, prev=prev: e.tensor_tensor(
                    tmp64[:], abuf[:, t * 64:(t + 1) * 64], prev, ALU.mult), reads=rd)
                p.op("vector", lambda e, t=t: e.tensor_tensor(
                    tmp64[:], tmp64[:], ubuf[:, t * 64:(t + 1) * 64], ALU.add))
                p.op("vector", lambda e, t=t: e.tensor_scalar(
                    hist[:, t * 64:(t + 1) * 64], tmp64[:], 10.0, -10.0, ALU.min, ALU.max),
                    writes=wr)
            if DBG_SCAN and c == 1:
                p.dma("sync", rearr(dbg[:, 0:64 * TP], "p f -> p f"), abuf[:], reads=("aexp",))
                p.dma("sync", rearr(dbg[:, 64 * TP:2 * 64 * TP], "p f -> p f"), ubuf[:], reads=("ubuf", "uok"))
                p.dma("sync", rearr(dbg[:, 2 * 64 * TP:3 * 64 * TP], "p f -> p f"), hist[:], reads=("hist",))
                p.dma("sync", rearr(dbg[:, 3 * 64 * TP:3 * 64 * TP + 4 * TP], "p f -> p f"),
                      dtc_b[par][:], reads=(f"dtc{par}",))
                p.dma("sync", rearr(dbg[:, 4 * 64 * TP:4 * 64 * TP + 4 * TP], "p f -> p f"),
                      xc_b[par][:], reads=(f"xc{par}",))
                p.dma("sync", rearr(dbg[:, 5 * 64 * TP:5 * 64 * TP + 16 * TP], "p f -> p f"),
                      brep_b[par][:], reads=(f"br{par}",))
                p.dma("sync", rearr(dbg[:, 6 * 64 * TP:6 * 64 * TP + 16 * TP], "p f -> p f"),
                      crep_b[par][:], reads=(f"cr{par}",))
            p.op("vector", lambda e, c_ap=c_ap: e.tensor_tensor(o4, h4, c_ap, ALU.mult),
                 reads=("hist", f"cr{par}", "omd"), writes=("och", f"crF{par}"))
            p.op("vector", lambda e: e.reduce_sum(
                ysm[:].unsqueeze(2),
                rearr(obuf[:], "p (tc n) -> p tc n", n=16),
                axis=mybir.AxisListType.X),
                reads=("och",), writes=("ysm",))
            p.op("vector", lambda e, xtc_ap=xtc_ap: e.tensor_tensor(
                rearr(ydx[:], "p (t c) -> p t c", t=TP), xtc_ap, D_ap, ALU.mult),
                reads=(f"xc{par}", "dtile", "ydx"), writes=("ydx", f"xcF{par}"))
            p.op("vector", lambda e, par=par: e.tensor_tensor(
                rearr(ybuf_b[par][:], "p (c t) -> p t c", c=4), 
                rearr(ysm[:], "p (t c) -> p t c", t=TP),
                rearr(ydx[:], "p (t c) -> p t c", t=TP), ALU.add),
                reads=("ysm", "ydx", f"ybF{par}"), writes=(f"yb{par}",))
            p.dma("gpsimd", chunk_ap(y_o, t0, 4, TP),
                  rearr(ybuf_b[par][:], "p (c t) -> p c t", c=4).opt(),
                  reads=(f"yb{par}",), writes=(f"ybF{par}",))
        p.emit(nc)
    st.close()
    return nc


def run_scan(ssm, dt, Bn, Cn, A_log, D_param):
    if "scan" not in _cache:
        _cache["scan"] = mk_scan()
    nc = _cache["scan"]
    in_maps = []
    for c in range(NCORE):
        ch0 = c * 256
        in_maps.append(dict(
            ssm=np.ascontiguousarray(ssm[ch0:ch0 + 256]),
            dts=np.ascontiguousarray(dt[ch0:ch0 + 256]),
            Bn=Bn, Cn=Cn,
            Alog=np.ascontiguousarray(A_log[ch0:ch0 + 256]),
            Dp=np.ascontiguousarray(D_param[ch0:ch0 + 256].reshape(256, 1))))
    res = run_bass_kernel_spmd(nc, in_maps, core_ids=list(range(NCORE)))
    return np.concatenate([r["y_o"] for r in res.results], axis=0)



def mk_mambapost(T=TPC, TSUB=512):
    KC = D // 128
    MC = 2048 // 128
    nsub = T // TSUB
    nc = bass.Bass("TRN2", target_bir_lowering=False, debug=False)
    yv = nc.dram_tensor("yv", [2048, T], F32, kind="ExternalInput").ap()
    zv = nc.dram_tensor("zv", [2048, T], F32, kind="ExternalInput").ap()
    r2 = nc.dram_tensor("r2", [D, T], F32, kind="ExternalInput").ap()
    WoT = nc.dram_tensor("WoT", [2048, D], F32, kind="ExternalInput").ap()
    r3 = nc.dram_tensor("r3", [D, T], F32, kind="ExternalOutput").ap()

    st = ExitStack()
    r2c = st.enter_context(nc.sbuf_tensor("r2c", [128, KC * TSUB], F32))
    yc = st.enter_context(nc.sbuf_tensor("yc", [128, MC * TSUB], F32))
    zc = st.enter_context(nc.sbuf_tensor("zc", [128, 2 * TSUB], F32))
    gb = st.enter_context(nc.sbuf_tensor("gb", [128, MC * TSUB], BF16))
    wst = st.enter_context(nc.sbuf_tensor("wst", [128, 4096], F32))
    wo = st.enter_context(nc.sbuf_tensor("wo", [128, MC * D], BF16))
    rout = st.enter_context(nc.sbuf_tensor("rout", [128, KC * TSUB], F32))
    ones = st.enter_context(nc.sbuf_tensor("ones", [128, 1], F32))
    onesrow = st.enter_context(nc.sbuf_tensor("onesrow", [1, 128], F32))
    sq = st.enter_context(nc.sbuf_tensor("sq", [128, TSUB], F32))
    stats = st.enter_context(nc.sbuf_tensor("stats", [1, 4 * TSUB], F32))
    bcast = st.enter_context(nc.sbuf_tensor("bcast", [128, 2 * TSUB], F32))
    psA = st.enter_context(nc.psum_tensor("psA", [128, 512], F32))
    psS = st.enter_context(nc.psum_tensor("psS", [128, 512], F32))
    psS2 = st.enter_context(nc.psum_tensor("psS2", [128, 512], F32))
    if True:
        p = Prog()
        r2c3 = rearr(r2c[:], "p (k t) -> p k t", k=KC)
        yc3 = rearr(yc[:], "p (k t) -> p k t", k=MC)
        gb3 = rearr(gb[:], "p (k t) -> p k t", k=MC)
        rout3 = rearr(rout[:], "p (k t) -> p k t", k=KC)
        p.op("gpsimd", lambda e: e.memset(ones[:], 1.0), writes=("ones",))
        p.op("gpsimd", lambda e: e.memset(onesrow[:], 1.0), writes=("ones",))
        emit_weight_load(p, WoT, wst, wo, MC, D, "wo")
        for s in range(nsub):
            sl = slice(s * TSUB, (s + 1) * TSUB)
            p.dma("sync", r2c3, rearr(r2[:, sl], "(k p) t -> p k t", p=128),
                  reads=("r2f",), writes=("r2c",))
            p.dma("sync", yc3, rearr(yv[:, sl], "(k p) t -> p k t", p=128),
                  reads=("ycf",), writes=("ycv",))
            for m in range(MC):
                zpar = m % 2
                p.dma("sync", zc[:, zpar * TSUB:(zpar + 1) * TSUB],
                      zv[m * 128:(m + 1) * 128, sl],
                      reads=(f"szf{zpar}",), writes=(f"sz{zpar}",))
                p.op("scalar", lambda e, zpar=zpar: e.activation(
                    sq[:], zc[:, zpar * TSUB:(zpar + 1) * TSUB], AF.Silu),
                    reads=(f"sz{zpar}", "sqbuf"), writes=("sz2", "sqbuf", f"szf{zpar}"))
                p.op("vector", lambda e, m=m: e.tensor_tensor(
                    gb3[:, m, :], yc3[:, m, :], sq[:], ALU.mult),
                    reads=("ycv", "sz2"), writes=(f"gb{m}",))
            for mo in range(KC):
                for k in range(MC):
                    p.op("tensor", lambda e, mo=mo, k=k: e.matmul(
                        psA[:, 0:TSUB], wo[:, k * D + mo * 128:k * D + (mo + 1) * 128],
                        gb3[:, k, :], start=(k == 0), stop=(k == MC - 1)),
                        reads=("wo", f"gb{k}"), writes=("psA",))
                p.op("vector", lambda e, mo=mo: e.tensor_tensor(
                    rout3[:, mo, :], psA[:, 0:TSUB], r2c3[:, mo, :], ALU.add),
                    reads=("psA", "r2c"), writes=(f"rout{mo}",))
            p.op("vector", lambda e: e.tensor_copy(sq[:, 0:1], yc3[:, 0, 0:1]),
                 reads=("ycv", "r2c", "sqbuf") + tuple(f"rout{mo}" for mo in range(KC)),
                 writes=("ycf", "r2f", "sqbuf"))
            emit_ln(p, rout3, KC, TSUB, ones, onesrow, sq, stats, bcast, psS, psS2,
                    tag=f"ln{s}", out_keys=[f"rout{mo}" for mo in range(KC)])
            p.dma("sync", rearr(r3[:, sl], "(k p) t -> p k t", p=128), rout3,
                  reads=tuple(f"rout{mo}" for mo in range(KC)), writes=())
        p.emit(nc)
    st.close()
    return nc


def run_mambapost(y, z, r2T, W_out):
    if "mambapost" not in _cache:
        _cache["mambapost"] = mk_mambapost()
    nc = _cache["mambapost"]
    WoT = np.ascontiguousarray(W_out.T)
    in_maps = []
    for c in range(NCORE):
        sl = slice(c * TPC, (c + 1) * TPC)
        in_maps.append(dict(
            yv=np.ascontiguousarray(y[:, sl]), zv=np.ascontiguousarray(z[:, sl]),
            r2=np.ascontiguousarray(r2T[:, sl]), WoT=WoT))
    res = run_bass_kernel_spmd(nc, in_maps, core_ids=list(range(NCORE)))
    return np.concatenate([r["r3"] for r in res.results], axis=1)



DBG_ATTN = False


def mk_l0attn(S=SEQ):
    KC = D // 128
    nqc = S // 512
    ntb = S // 512
    nc = bass.Bass("TRN2", target_bir_lowering=False, debug=False)
    xb = nc.dram_tensor("xb", [D, S], F32, kind="ExternalInput").ap()
    WDQ = nc.dram_tensor("WDQ", [D, 256], F32, kind="ExternalInput").ap()
    WDKV = nc.dram_tensor("WDKV", [D, 256], F32, kind="ExternalInput").ap()
    WUQ = nc.dram_tensor("WUQ", [256, 256], F32, kind="ExternalInput").ap()
    WQR = nc.dram_tensor("WQR", [256, 128], F32, kind="ExternalInput").ap()
    WUK = nc.dram_tensor("WUK", [256, 256], F32, kind="ExternalInput").ap()
    WUV = nc.dram_tensor("WUV", [256, 256], F32, kind="ExternalInput").ap()
    WKR = nc.dram_tensor("WKR", [D, 32], F32, kind="ExternalInput").ap()
    rc = nc.dram_tensor("rc", [32, S], F32, kind="ExternalInput").ap()
    rs = nc.dram_tensor("rs", [32, S], F32, kind="ExternalInput").ap()
    cmask = nc.dram_tensor("cmask", [4, 128, 512], F32, kind="ExternalInput").ap()
    ao = nc.dram_tensor("ao", [256, S], BF16, kind="ExternalOutput").ap()
    dbg = nc.dram_tensor("dbg", [128, 4 * 512 + 6 * 512], F32, kind="ExternalOutput").ap() if DBG_ATTN else None

    SC = float(1.0 / np.sqrt(96.0))
    EBIAS = float(-80.0 * SC)
    PLO = float(np.exp(-160.0 * SC))

    st = ExitStack()
    xb16 = st.enter_context(nc.sbuf_tensor("xb16", [128, KC * 512], BF16))
    wdq = st.enter_context(nc.sbuf_tensor("wdq", [128, KC * 256], BF16))
    wdkv = st.enter_context(nc.sbuf_tensor("wdkv", [128, KC * 256], BF16))
    wu = st.enter_context(nc.sbuf_tensor("wu", [128, 3 * 512 + 512], BF16))
    wqr = st.enter_context(nc.sbuf_tensor("wqr", [128, 512], BF16))
    wqr2 = st.enter_context(nc.sbuf_tensor("wqr2", [128, 512], BF16))
    wkr = st.enter_context(nc.sbuf_tensor("wkr", [128, KC * 64], BF16))
    wst = st.enter_context(nc.sbuf_tensor("wst", [128, 1024], F32))
    cq = st.enter_context(nc.sbuf_tensor("cq", [128, 2 * S], BF16))
    ckv = st.enter_context(nc.sbuf_tensor("ckv", [128, 2 * S], BF16))
    krr = st.enter_context(nc.sbuf_tensor("krr", [128, S], BF16))
    rct = st.enter_context(nc.sbuf_tensor("rct", [128, S], BF16))
    rst = st.enter_context(nc.sbuf_tensor("rst", [128, S], BF16))
    kc96 = st.enter_context(nc.sbuf_tensor("kc96", [128, S], BF16))
    qc96 = st.enter_context(nc.sbuf_tensor("qc96", [128, S], BF16))
    qr96 = st.enter_context(nc.sbuf_tensor("qr96", [128, S], BF16))
    vtok = st.enter_context(nc.sbuf_tensor("vtok", [128, 64 * 65], BF16))
    probs = st.enter_context(nc.sbuf_tensor("probs", [128, 512], BF16))
    smask = st.enter_context(nc.sbuf_tensor("smask", [128, 4 * 512], BF16))
    attnf = st.enter_context(nc.sbuf_tensor("attnf", [128, 512], F32))
    sums = st.enter_context(nc.sbuf_tensor("sums", [128, 2 * 512], F32))
    onesrow = st.enter_context(nc.sbuf_tensor("onesrow", [1, 128], F32))
    aob = st.enter_context(nc.sbuf_tensor("aob", [128, 512], BF16))
    psK = st.enter_context(nc.psum_tensor("psK", [128, 512], F32))
    psQ = st.enter_context(nc.psum_tensor("psQ", [128, 512], F32))
    psR = st.enter_context(nc.psum_tensor("psR", [128, 512], F32))
    psSc = st.enter_context(nc.psum_tensor("psSc", [128, 512], F32))
    psAt = st.enter_context(nc.psum_tensor("psAt", [128, 512], F32))
    psB2 = st.enter_context(nc.psum_tensor("psB2", [128, 512], F32))
    if True:
        p = Prog()
        xstg3 = rearr(wst[:, 0:1024], "p (k t) -> p k t", k=2)
        xb3 = rearr(xb16[:], "p (k t) -> p k t", k=KC)
        cq3 = rearr(cq[:], "p (m t) -> p m t", m=2)
        ckv3 = rearr(ckv[:], "p (m t) -> p m t", m=2)
        wkr3 = rearr(wkr[:], "p (k d) -> p k d", k=KC)
        emit_weight_load(p, WDQ, wst, wdq, KC, 256, "wdq", stage_cols=1024)
        emit_weight_load(p, WDKV, wst, wdkv, KC, 256, "wdkv", stage_cols=1024)
        for i, wsrc in enumerate((WUQ, WUK, WUV)):
            p.dma("sync", rearr(wst[:, 0:512], "p (k d) -> p k d", k=2),
                  rearr(wsrc, "(k p) d -> p k d", p=128), reads=("wstage",), writes=("wstage",))
            p.op("vector", lambda e, i=i: e.tensor_copy(wu[:, i * 512:(i + 1) * 512], wst[:, 0:512]),
                 reads=("wstage",), writes=("wu",))
        p.dma("sync", rearr(wst[:, 0:256], "p (k d) -> p k d", k=2),
              rearr(WQR, "(k p) d -> p k d", p=128), reads=("wstage",), writes=("wstage",))
        p.op("vector", lambda e: e.tensor_copy(wqr[:, 0:256], wst[:, 0:256]),
             reads=("wstage",), writes=("wqr",))
        wq1 = rearr(wqr[:, 0:256], "p (k d) -> p k d", k=2)
        wq2 = rearr(wqr[:, 256:512], "p (k d) -> p k d", k=2)
        p.op("vector", lambda e: e.tensor_scalar(
            wq2[:, :, 0:128:2], wq1[:, :, 1:128:2], -1.0, None, ALU.mult),
            reads=("wqr",), writes=("wqrs",))
        p.op("vector", lambda e: e.tensor_copy(wq2[:, :, 1:128:2], wq1[:, :, 0:128:2]),
             reads=("wqr",), writes=("wqrs",))
        wq2v = rearr(wu[:, 1536:2048], "p (k h f d) -> p k h f d", k=2, h=4, f=2)
        p.op("vector", lambda e: e.tensor_copy(
            wq2v[:, :, :, 0, :], rearr(wqr[:, 0:256], "p (k h d) -> p k h d", k=2, h=4)),
            reads=("wqr",), writes=("wqr2", "wu"))
        p.op("vector", lambda e: e.tensor_copy(
            wq2v[:, :, :, 1, :], rearr(wqr[:, 256:512], "p (k h d) -> p k h d", k=2, h=4)),
            reads=("wqrs",), writes=("wqr2", "wu"))
        p.dma("sync", rearr(wst[:, 0:256], "p (k d) -> p k d", k=KC),
              rearr(WKR, "(k p) d -> p k d", p=128), reads=("wstage", "wqr", "wqrs"), writes=("wstage",))
        p.op("vector", lambda e: e.tensor_copy(
            wkr3[:, :, 0:32], rearr(wst[:, 0:256], "p (k d) -> p k d", k=KC)),
            reads=("wstage",), writes=("wkr",))
        p.op("vector", lambda e: e.tensor_scalar(
            wkr3[:, :, 32:64:2], wkr3[:, :, 1:32:2], -1.0, None, ALU.mult),
            reads=("wkr",), writes=("wkrs",))
        p.op("vector", lambda e: e.tensor_copy(wkr3[:, :, 33:64:2], wkr3[:, :, 0:32:2]),
             reads=("wkr",), writes=("wkrs",))
        for tt, dst in ((rc, rct), (rs, rst)):
            for cc in range(S // 2048):
                p.dma("sync", wst[0:32, 0:512], tt[:, cc * 2048:cc * 2048 + 512], reads=("wstage",), writes=("wstage",))
                p.op("vector", lambda e, dst=dst, cc=cc: e.tensor_copy(
                    dst[0:32, cc * 2048:cc * 2048 + 512], wst[0:32, 0:512]),
                    reads=("wstage",), writes=("rope",))
        for tt, dst in ((rc, rct), (rs, rst)):
            for cc in range(S // 512):
                if cc % 4 == 0:
                    continue
                p.dma("sync", wst[0:32, 0:512], tt[:, cc * 512:(cc + 1) * 512], reads=("wstage",), writes=("wstage",))
                p.op("vector", lambda e, dst=dst, cc=cc: e.tensor_copy(
                    dst[0:32, cc * 512:(cc + 1) * 512], wst[0:32, 0:512]),
                    reads=("wstage",), writes=("rope",))
        for vv in range(4):
            p.dma("sync", wst[:, 0:512], cmask[vv], reads=("wstage",), writes=("wstage",))
            p.op("vector", lambda e, vv=vv: e.tensor_copy(
                smask[:, vv * 512:(vv + 1) * 512], wst[:, 0:512]),
                reads=("wstage",), writes=("cmask",))
        p.op("gpsimd", lambda e: e.memset(onesrow[:], 1.0), writes=("ones",))
        for tb in range(ntb):
            for hh in range(4):
                p.dma("sync", xstg3,
                      rearr(xb[hh * 256:(hh + 1) * 256, tb * 512:(tb + 1) * 512], "(k p) t -> p k t", p=128),
                      reads=("wstage",), writes=("wstage",))
                p.op("vector", lambda e, hh=hh: e.tensor_copy(xb3[:, hh * 2:(hh + 1) * 2, :], xstg3),
                     reads=("wstage",), writes=("xs", "wstage"))
            for wbuf, wkey, ob3, okey, pst, pk in (
                    (wdq, "wdq", cq3, "cq", psK, "psK"), (wdkv, "wdkv", ckv3, "ckv", psQ, "psQ")):
                for m in range(2):
                    for k in range(KC):
                        p.op("tensor", lambda e, m=m, k=k, wbuf=wbuf, pst=pst: e.matmul(
                            pst[:, 0:512], wbuf[:, k * 256 + m * 128:k * 256 + (m + 1) * 128],
                            xb3[:, k, :], start=(k == 0), stop=(k == KC - 1)),
                            reads=(wkey, "xs"), writes=(pk,))
                    p.op("scalar", lambda e, m=m, tb=tb, ob3=ob3, pst=pst: e.copy(
                        ob3[:, m, tb * 512:(tb + 1) * 512], pst[:, 0:512]),
                        reads=(pk,), writes=(okey,))
            for half in range(2):
                for k in range(KC):
                    p.op("tensor", lambda e, k=k, half=half: e.matmul(
                        psR[0:32, 0:512], wkr3[:, k, half * 32:(half + 1) * 32], xb3[:, k, :],
                        start=(k == 0), stop=(k == KC - 1)),
                        reads=("wkr", "wkrs", "xs"), writes=("psR",))
                p.op("vector", lambda e, half=half: e.tensor_copy(
                    wst[0:32, half * 512:(half + 1) * 512], psR[0:32, 0:512]),
                    reads=("psR", "wstage"), writes=("wstage",))
            p.op("vector", lambda e, tb=tb: e.tensor_tensor(
                wst[0:32, 0:512], wst[0:32, 0:512], rct[0:32, tb * 512:(tb + 1) * 512], ALU.mult),
                reads=("wstage", "rope"), writes=("wstage",))
            p.op("vector", lambda e, tb=tb: e.tensor_tensor(
                wst[0:32, 512:1024], wst[0:32, 512:1024], rst[0:32, tb * 512:(tb + 1) * 512], ALU.mult),
                reads=("wstage", "rope"), writes=("wstage",))
            p.op("vector", lambda e, tb=tb: e.tensor_tensor(
                krr[0:32, tb * 512:(tb + 1) * 512], wst[0:32, 0:512], wst[0:32, 512:1024], ALU.add),
                reads=("wstage",), writes=("krrot",))
        for h in range(4):
            for tb in range(ntb):
                for k in range(2):
                    p.op("tensor", lambda e, h=h, tb=tb, k=k: e.matmul(
                        psK[0:64, 0:512], wu[:, 512 + k * 256 + h * 64:512 + k * 256 + (h + 1) * 64],
                        ckv3[:, k, tb * 512:(tb + 1) * 512], start=(k == 0), stop=(k == 1)),
                        reads=("wu", "ckv"), writes=("psK",))
                p.op("scalar", lambda e, tb=tb: e.copy(
                    kc96[0:64, tb * 512:(tb + 1) * 512], psK[0:64, 0:512]),
                    reads=("psK",), writes=("k96c",))
                for k in range(2):
                    p.op("tensor", lambda e, h=h, tb=tb, k=k: e.matmul(
                        psQ[0:64, 0:512], wu[:, k * 256 + h * 64:k * 256 + (h + 1) * 64],
                        cq3[:, k, tb * 512:(tb + 1) * 512], start=(k == 0), stop=(k == 1)),
                        reads=("wu", "cq"), writes=("psQ",))
                p.op("scalar", lambda e, tb=tb: e.copy(
                    qc96[0:64, tb * 512:(tb + 1) * 512], psQ[0:64, 0:512]),
                    reads=("psQ",), writes=("q96c",))
                for k in range(2):
                    p.op("tensor", lambda e, h=h, tb=tb, k=k: e.matmul(
                        psQ[0:64, 0:512],
                        wu[:, 1536 + k * 256 + h * 64:1536 + k * 256 + (h + 1) * 64],
                        cq3[:, k, tb * 512:(tb + 1) * 512], start=(k == 0), stop=(k == 1)),
                        reads=("wqr2", "cq"), writes=("psQ",))
                p.op("vector", lambda e: e.tensor_copy(
                    wst[0:32, 0:512], psQ[0:32, 0:512]),
                    reads=("psQ", "wstage"), writes=("wstage",))
                p.op("vector", lambda e: e.tensor_copy(
                    wst[0:32, 512:1024], psQ[32:64, 0:512]),
                    reads=("psQ", "wstage"), writes=("wstage",))
                if DBG_ATTN and h == 0 and tb == 0:
                    p.dma("gpsimd", dbg[0:32, 4096:4608], wst[0:32, 0:512], reads=("wstage",))
                    p.dma("gpsimd", dbg[0:32, 4608:5120], wst[0:32, 512:1024], reads=("wstage",))
                p.op("vector", lambda e, tb=tb: e.tensor_tensor(
                    wst[0:32, 0:512], wst[0:32, 0:512], rct[0:32, tb * 512:(tb + 1) * 512], ALU.mult),
                    reads=("wstage", "rope"), writes=("wstage",))
                p.op("vector", lambda e, tb=tb: e.tensor_tensor(
                    wst[0:32, 512:1024], wst[0:32, 512:1024], rst[0:32, tb * 512:(tb + 1) * 512], ALU.mult),
                    reads=("wstage", "rope"), writes=("wstage",))
                if DBG_ATTN and h == 0 and tb == 0:
                    p.dma("gpsimd", dbg[0:32, 1024:1536], wst[0:32, 0:512], reads=("wstage",))
                    p.dma("gpsimd", dbg[0:32, 1536:2048], wst[0:32, 512:1024], reads=("wstage",))
                    p.dma("gpsimd", dbg[0:32, 0:512], rct[0:32, 0:512], reads=("rope",))
                    p.dma("gpsimd", dbg[0:32, 512:1024], rst[0:32, 0:512], reads=("rope",))
                p.op("vector", lambda e, tb=tb: e.tensor_tensor(
                    qr96[0:32, tb * 512:(tb + 1) * 512], wst[0:32, 0:512], wst[0:32, 512:1024], ALU.add),
                    reads=("wstage",), writes=("q96r",))
            p.op("gpsimd", lambda e: e.memset(vtok[:], 0.0), reads=("vuse",), writes=("vtok",))
            p.op("gpsimd", lambda e: e.memset(
                RawAP2(vtok, 64, 65), 1.0), reads=("vtok",), writes=("vtok",))
            for blk in range(S // 128):
                for k in range(2):
                    p.op("tensor", lambda e, h=h, blk=blk, k=k: e.matmul(
                        psB2[:, 0:64], ckv3[:, k, blk * 128:(blk + 1) * 128],
                        wu[:, 1024 + k * 256 + h * 64:1024 + k * 256 + (h + 1) * 64],
                        start=(k == 0), stop=(k == 1)),
                        reads=("wu", "ckv"), writes=("psB2",))
                p.op("vector", lambda e, blk=blk: e.tensor_copy(
                    vtok[:, blk * 65:blk * 65 + 64], psB2[:, 0:64]),
                    reads=("psB2", "vtok"), writes=("vtok",))
            for qc in range(nqc):
                nkc = 4 * qc + 4
                for kc in range(nkc):
                    par = kc % 2
                    p.op("tensor", lambda e, qc=qc, kc=kc: e.matmul(
                        psSc[:, 0:512], kc96[0:64, kc * 128:(kc + 1) * 128],
                        qc96[0:64, qc * 512:(qc + 1) * 512], start=True, stop=False),
                        reads=("k96c", "q96c", "psScE"), writes=("psSc",))
                    p.op("tensor", lambda e, qc=qc, kc=kc: e.matmul(
                        psSc[:, 0:512], krr[0:32, kc * 128:(kc + 1) * 128],
                        qr96[0:32, qc * 512:(qc + 1) * 512], start=False, stop=True),
                        reads=("krrot", "q96r", "psScE"), writes=("psSc",))
                    p.op("scalar", lambda e, par=par: e.activation(
                        probs[:, 0:512], psSc[:, 0:512],
                        AF.Exp, bias=f(EBIAS), scale=f(SC)),
                        reads=("psSc", f"prF{par}"), writes=(f"pr{par}", "psScE"))
                    p.op("vector", lambda e, par=par: e.tensor_scalar(
                        probs[:, 0:512], probs[:, 0:512],
                        1.0, PLO, ALU.min, ALU.max),
                        reads=(f"pr{par}",), writes=(f"pr{par}",))
                    j = kc - 4 * qc
                    if j >= 0:
                        p.op("vector", lambda e, par=par, j=j: e.tensor_tensor(
                            probs[:, 0:512],
                            probs[:, 0:512],
                            rearr(smask[:], "p (j t) -> p j t", j=4)[:, j, :], ALU.mult),
                            reads=(f"pr{par}", "cmask"), writes=(f"pr{par}",))
                    p.op("tensor", lambda e, qc=qc, kc=kc, par=par: e.matmul(
                        psAt[0:65, 0:512], vtok[:, kc * 65:(kc + 1) * 65],
                        probs[:, 0:512],
                        start=(kc == 0), stop=(kc == nkc - 1)),
                        reads=("vtok", f"pr{par}"), writes=("psAt", f"prF{par}"))
                if DBG_ATTN and h == 0 and qc == 0:

                    p.dma("gpsimd", dbg[0:64, 4 * 512:5 * 512], qc96[0:64, 0:512], reads=("q96c",))
                    p.dma("gpsimd", dbg[0:64, 5 * 512:6 * 512], kc96[0:64, 0:512], reads=("k96c",))
                    p.dma("gpsimd", dbg[0:32, 6 * 512:7 * 512], qr96[0:32, 0:512], reads=("q96r",))
                    p.dma("gpsimd", dbg[0:32, 7 * 512:8 * 512], krr[0:32, 0:512], reads=("krrot",))

                p.op("vector", lambda e: e.tensor_copy(sums[0:1, 0:512], psAt[64:65, 0:512]),
                     reads=("psAt", "sumsb"), writes=("sums1", "sumsb"))
                p.op("vector", lambda e: e.reciprocal(sums[0:1, 512:1024], sums[0:1, 0:512]),
                     reads=("sums1", "sumsb"), writes=("sums2", "sumsb"))
                p.op("tensor", lambda e: e.matmul(
                    psB2[0:64, 0:512], onesrow[0:1, 0:64], sums[0:1, 512:1024],
                    start=True, stop=True),
                    reads=("ones", "sums2"), writes=("psB2",))
                p.op("scalar", lambda e: e.copy(attnf[0:64, 0:512], psAt[0:64, 0:512]),
                     reads=("psAt", "attnf"), writes=("attnf",))
                p.op("vector", lambda e: e.tensor_tensor(
                    aob[0:64, 0:512], attnf[0:64, 0:512], psB2[0:64, 0:512], ALU.mult),
                    reads=("attnf", "psB2", "aobF"), writes=("aob",))
                p.dma("sync", ao[h * 64:(h + 1) * 64, qc * 512:(qc + 1) * 512], aob[0:64, 0:512],
                      reads=("aob",), writes=("aobF",))
            p.op("vector", lambda e: e.tensor_copy(sums[0:1, 0:1], vtok[0:1, 0:1]),
                 reads=("vtok", "sumsb"), writes=("vuse", "sumsb"))
        p.emit(nc)
    st.close()
    return nc


def RawAP2(t, off, step):
    from concourse.ap import AP as _AP
    n = t[:].shape[1] // step
    return _AP(t[:].tensor, off, [[t[:].shape[1], 128], [step, n]])


def run_l0attn(xT, inp):
    if "l0attn" not in _cache:
        _cache["l0attn"] = mk_l0attn()
    nc = _cache["l0attn"]
    d = 32
    inv = 10000.0 ** (-np.arange(0, d, 2, dtype=np.float64) / d)
    ang = np.arange(SEQ, dtype=np.float64)[:, None] * inv
    cos_t = np.zeros((32, SEQ), np.float32)
    sin_t = np.zeros((32, SEQ), np.float32)
    for i in range(16):
        cos_t[2 * i] = np.cos(ang[:, i]); cos_t[2 * i + 1] = np.cos(ang[:, i])
        sin_t[2 * i] = np.sin(ang[:, i]); sin_t[2 * i + 1] = np.sin(ang[:, i])
    cm = np.zeros((4, 128, 512), np.float32)
    for j in range(4):
        for kk in range(128):
            cm[j, kk, :] = (np.arange(512) >= kk + j * 128)
    in_maps = []
    for c in range(NCORE):
        b = c // 4
        h0 = (c % 4) * 4
        in_maps.append(dict(
            xb=np.ascontiguousarray(xT[:, b * SEQ:(b + 1) * SEQ]),
            WDQ=inp["L0_attn_W_DQ"], WDKV=inp["L0_attn_W_DKV"],
            WUQ=np.ascontiguousarray(inp["L0_attn_W_UQ"][:, h0 * 64:(h0 + 4) * 64]),
            WQR=np.ascontiguousarray(inp["L0_attn_W_QR"][:, h0 * 32:(h0 + 4) * 32]),
            WUK=np.ascontiguousarray(inp["L0_attn_W_UK"][:, h0 * 64:(h0 + 4) * 64]),
            WUV=np.ascontiguousarray(inp["L0_attn_W_UV"][:, h0 * 64:(h0 + 4) * 64]),
            WKR=inp["L0_attn_W_KR"],
            rc=cos_t, rs=sin_t, cmask=cm))
    res = run_bass_kernel_spmd(nc, in_maps, core_ids=list(range(NCORE)))
    out = np.zeros((1024, TOK), dtype=BF)
    for c in range(NCORE):
        b = c // 4
        h0 = (c % 4) * 4
        out[h0 * 64:(h0 + 4) * 64, b * SEQ:(b + 1) * SEQ] = res.results[c]["ao"]
    return out



L1_NO_ATTN = False
L1_NO_FFN = False


def mk_l1ffn(final, T=TPC):
    KC = 8
    NW = 6
    NJ = 22
    FS = 256
    nc = bass.Bass("TRN2", target_bir_lowering=False, debug=False)
    rin = nc.dram_tensor("rin", [D, 1025 + T], F32, kind="ExternalInput").ap()
    wqkv = nc.dram_tensor("wqkv", [D, 1536], F32, kind="ExternalInput").ap()
    qsc = nc.dram_tensor("qsc", [128, 2], F32, kind="ExternalInput").ap()
    ksc = nc.dram_tensor("ksc", [128, 2], F32, kind="ExternalInput").ap()
    wo_d = nc.dram_tensor("wo", [512, D], F32, kind="ExternalInput").ap()
    lng = nc.dram_tensor("lng", [D, 1], F32, kind="ExternalInput").ap()
    lnb = nc.dram_tensor("lnb", [D, 1], F32, kind="ExternalInput").ap()
    w1p = nc.dram_tensor("w1p", [D, NJ * 256], F32, kind="ExternalInput").ap()
    b1p = nc.dram_tensor("b1p", [128, 2 * NJ], F32, kind="ExternalInput").ap()
    w2p = nc.dram_tensor("w2p", [NJ * 128, D], F32, kind="ExternalInput").ap()
    b2 = nc.dram_tensor("b2", [D, 1], F32, kind="ExternalInput").ap()
    tabs = nc.dram_tensor("tabs", [128, 4 * (1024 + T)], F32, kind="ExternalInput").ap()
    amask = nc.dram_tensor("amask", [128, 1408 + 1024], F32, kind="ExternalInput").ap()
    eye = nc.dram_tensor("eye", [128, 128], F32, kind="ExternalInput").ap()
    bonesd = nc.dram_tensor("bonesd", [2, 128], F32, kind="ExternalInput").ap()
    r3c = nc.dram_tensor("r3c", [D, T], F32, kind="ExternalInput").ap()
    rout = nc.dram_tensor("rout", [D, T], F32, kind="ExternalOutput").ap()

    st = ExitStack()
    x1 = st.enter_context(nc.sbuf_tensor("x1", [128, KC * 513], F32))
    x1b = st.enter_context(nc.sbuf_tensor("x1b", [128, KC * 513], BF16))
    wq = st.enter_context(nc.sbuf_tensor("wq", [128, KC * 1536], BF16))
    wrot = st.enter_context(nc.sbuf_tensor("wrot", [128, KC * 1024], BF16))
    wst = st.enter_context(nc.sbuf_tensor("wst", [128, 1024], F32))
    wob = st.enter_context(nc.sbuf_tensor("wob", [64, 8 * D], BF16))
    w1b = st.enter_context(nc.sbuf_tensor("w1b", [128, KC * 256], BF16))
    w2b = st.enter_context(nc.sbuf_tensor("w2b", [128, D], BF16))
    sqs = st.enter_context(nc.sbuf_tensor("sqs", [128, 4], F32))
    sgb = st.enter_context(nc.sbuf_tensor("sgb", [128, 2 * KC], F32))
    sb2 = st.enter_context(nc.sbuf_tensor("sb2", [128, KC], F32))
    sb1 = st.enter_context(nc.sbuf_tensor("sb1", [128, 2 * NJ], F32))
    stab = st.enter_context(nc.sbuf_tensor("stab", [128, 4 * 512], BF16))
    smask = st.enter_context(nc.sbuf_tensor("smask", [128, 1408 + 1024], BF16))
    eyeb = st.enter_context(nc.sbuf_tensor("eyeb", [128, 128], BF16))
    bones = st.enter_context(nc.sbuf_tensor("bones", [2, 128], F32))
    honesr = st.enter_context(nc.sbuf_tensor("honesr", [128, 2], F32))
    onesrow = st.enter_context(nc.sbuf_tensor("onesrow", [1, 128], F32))
    ones = st.enter_context(nc.sbuf_tensor("ones", [128, 1], F32))
    qraw = st.enter_context(nc.sbuf_tensor("qraw", [128, 512], BF16))
    qrot = st.enter_context(nc.sbuf_tensor("qrot", [128, 512], BF16))
    qu = st.enter_context(nc.sbuf_tensor("qu", [128, 2 * 4 * 512], BF16))
    ku = st.enter_context(nc.sbuf_tensor("ku", [128, 2 * 4 * 512], BF16))
    vt = st.enter_context(nc.sbuf_tensor("vt", [128, 2 * 8 * 4 * 65], BF16))
    normb = st.enter_context(nc.sbuf_tensor("normb", [2, 512], F32))
    sc = st.enter_context(nc.sbuf_tensor("sc", [128, 1024], F32))
    rmax = st.enter_context(nc.sbuf_tensor("rmax", [128, 2], F32))
    probs = st.enter_context(nc.sbuf_tensor("probs", [128, 1024], BF16))
    atn = st.enter_context(nc.sbuf_tensor("atn", [64, 8 * 128], BF16))
    attnf = st.enter_context(nc.sbuf_tensor("attnf", [128, 128], F32))
    sums = st.enter_context(nc.sbuf_tensor("sums", [128, 2 * 128], F32))
    x2 = st.enter_context(nc.sbuf_tensor("x2", [128, KC * 513], F32))
    lnbuf = st.enter_context(nc.sbuf_tensor("lnbuf", [128, KC * FS], BF16))
    hbuf = st.enter_context(nc.sbuf_tensor("hbuf", [128, NJ * FS], BF16))
    gel = st.enter_context(nc.sbuf_tensor("gel", [128, FS], F32))
    rob = st.enter_context(nc.sbuf_tensor("rob", [128, KC * 512], F32))
    sq = st.enter_context(nc.sbuf_tensor("sq", [128, 512], F32))
    stats = st.enter_context(nc.sbuf_tensor("stats", [1, 3 * 512], F32))
    bcast = st.enter_context(nc.sbuf_tensor("bcast", [128, 2 * 512], F32))
    psA = st.enter_context(nc.psum_tensor("psA", [128, 512], F32))
    psG = st.enter_context(nc.psum_tensor("psG", [128, 512], F32))
    ps0 = st.enter_context(nc.psum_tensor("ps0", [128, 512], F32))
    ps1 = st.enter_context(nc.psum_tensor("ps1", [128, 512], F32))
    ps2 = st.enter_context(nc.psum_tensor("ps2", [128, 512], F32))
    psS = st.enter_context(nc.psum_tensor("psS", [128, 512], F32))
    psS2 = st.enter_context(nc.psum_tensor("psS2", [128, 512], F32))
    psT = st.enter_context(nc.psum_tensor("psT", [128, 256], BF16))
    if True:
        p = Prog()
        x13 = rearr(x1[:], "p (k t) -> p k t", k=KC)
        x1b3 = rearr(x1b[:], "p (k t) -> p k t", k=KC)
        x23 = rearr(x2[:], "p (k t) -> p k t", k=KC)
        qu4 = rearr(qu[:], "p (w i t) -> p w i t", w=2, i=4)
        ku4 = rearr(ku[:], "p (w i t) -> p w i t", w=2, i=4)
        vt4 = rearr(vt[:], "p (w h b) -> p w h b", w=2, h=8)
        lnb3 = rearr(lnbuf[:], "p (k t) -> p k t", k=KC)
        rob3 = rearr(rob[:], "p (k t) -> p k t", k=KC)
        hb3 = rearr(hbuf[:], "p (j t) -> p j t", j=NJ)
        stab3 = rearr(stab[:], "p (a t) -> p a t", a=4)
        wq4 = rearr(wq[:], "p (k c) -> p k c", k=KC)
        wr4 = rearr(wrot[:], "p (k c) -> p k c", k=KC)
        p.op("gpsimd", lambda e: e.memset(onesrow[:], 1.0), writes=("ones",))
        p.op("gpsimd", lambda e: e.memset(ones[:], 1.0), writes=("ones",))
        p.op("gpsimd", lambda e: e.memset(honesr[:], 0.0), writes=("hones",))
        p.op("gpsimd", lambda e: e.memset(honesr[0:64, 0:1], 1.0), writes=("hones",))
        p.op("gpsimd", lambda e: e.memset(honesr[64:128, 1:2], 1.0), writes=("hones",))
        p.dma("sync", bones[:], bonesd, writes=("bones",))
        for cc in range(3):
            c0, c1 = cc * 1024, min((cc + 1) * 1024, 2432)
            p.dma("sync", wst[:, 0:c1 - c0], amask[:, c0:c1], reads=("wstage",), writes=("wstage",))
            p.op("vector", lambda e, c0=c0, c1=c1: e.tensor_copy(
                smask[:, c0:c1], wst[:, 0:c1 - c0]),
                reads=("wstage",), writes=("amask",))
        p.dma("sync", wst[:, 0:128], eye, reads=("wstage",), writes=("wstage",))
        p.op("vector", lambda e: e.tensor_copy(eyeb[:], wst[:, 0:128]), reads=("wstage",), writes=("eye", "wstage"))
        p.dma("sync", sqs[:, 0:2], qsc, writes=("sqs",))
        p.dma("sync", sqs[:, 2:4], ksc, writes=("sqs",))
        p.dma("sync", sgb[:, 0:KC].unsqueeze(2), rearr(lng, "(k p) o -> p k o", p=128),
              writes=("gb",), allow_slow_non_contiguous=True)
        p.dma("sync", sgb[:, KC:2 * KC].unsqueeze(2), rearr(lnb, "(k p) o -> p k o", p=128),
              writes=("gb",), allow_slow_non_contiguous=True)
        p.dma("sync", sb2[:].unsqueeze(2), rearr(b2, "(k p) o -> p k o", p=128),
              writes=("gb",), allow_slow_non_contiguous=True)
        p.dma("sync", sb1[:], b1p, writes=("gb",))
        for k_ in range(KC):
            for hh in range(2):
                p.dma("sync", wst[:, 0:768], wqkv[k_ * 128:(k_ + 1) * 128, hh * 768:(hh + 1) * 768],
                      reads=("wstage",), writes=("wstage",))
                p.op("vector", lambda e, k_=k_, hh=hh: e.tensor_copy(
                    wq[:, k_ * 1536 + hh * 768:k_ * 1536 + (hh + 1) * 768], wst[:, 0:768]),
                    reads=("wstage",), writes=("wq",))
        for h_ in range(8):
            p.dma("sync", wst[0:64, 0:1024], wo_d[h_ * 64:(h_ + 1) * 64, :],
                  reads=("wstage",), writes=("wstage",))
            p.op("vector", lambda e, h_=h_: e.tensor_copy(
                wob[0:64, h_ * D:(h_ + 1) * D], wst[0:64, 0:1024]),
                reads=("wstage",), writes=("wo",))
        for half in range(2):
            src_ = rearr(wq4[:, :, half * 512:(half + 1) * 512], "p k (h d) -> p k h d", h=8)
            dst_ = rearr(wr4[:, :, half * 512:(half + 1) * 512], "p k (h d) -> p k h d", h=8)
            p.op("vector", lambda e, src_=src_, dst_=dst_: e.tensor_scalar(
                dst_[:, :, :, 0:32], src_[:, :, :, 32:64], -1.0, None, ALU.mult),
                reads=("wq",), writes=("wrot",))
            p.op("vector", lambda e, src_=src_, dst_=dst_: e.tensor_copy(
                dst_[:, :, :, 32:64], src_[:, :, :, 0:32]),
                reads=("wq",), writes=("wrot",))

        qblocks = {1: [3], 2: [0, 1, 2, 3], 3: [0, 1, 2, 3], 4: [0, 1, 2, 3], 5: [0, 1, 2, 3]}

        for g in range(NW):
            gp = g % 2
            p.dma("sync", x13,
                  rearr(rin[:, g * 512:g * 512 + 513], "(k p) t -> p k t", p=128),
                  reads=("x1F",), writes=("x1",))
            p.op("vector", lambda e: e.tensor_copy(x1b3, x13),
                 reads=("x1", "x1bF"), writes=("x1b",))
            p.dma("gpsimd", stab3,
                  rearr(tabs[:, :], "p (a t) -> p a t", a=4)[:, :, g * 512:(g + 1) * 512],
                  reads=("tabB",), writes=("tab",))
            for m in range(8):
                which = 0 if m < 4 else 1
                i = m % 4
                uk = f"qu{gp}" if which == 0 else f"ku{gp}"
                xu4 = qu4 if which == 0 else ku4
                sc0 = 0 if which == 0 else 2
                ta, tb_ = (0, 1) if which == 0 else (2, 3)
                for k in range(KC):
                    sh = 0 if k < 4 else -1
                    p.op("tensor", lambda e, m=m, k=k, sh=sh, gp=gp: e.matmul(
                        psA[:, 0:512], wq4[:, k, m * 128:(m + 1) * 128],
                        x1b3[:, k, 1 + sh:513 + sh], start=(k == 0), stop=(k == KC - 1)),
                        reads=("wq", "x1b"), writes=("psA",))
                for k in range(KC):
                    sh = 0 if k < 4 else -1
                    p.op("tensor", lambda e, m=m, k=k, sh=sh, gp=gp: e.matmul(
                        psG[:, 0:512], wr4[:, k, m * 128:(m + 1) * 128],
                        x1b3[:, k, 1 + sh:513 + sh], start=(k == 0), stop=(k == KC - 1)),
                        reads=("wrot", "x1b"), writes=("psG",))
                p.op("vector", lambda e: e.tensor_copy(qraw[:], psA[:, 0:512]),
                     reads=("psA", "qraw"), writes=("qraw",))
                p.op("vector", lambda e: e.tensor_copy(qrot[:], psG[:, 0:512]),
                     reads=("psG", "qrot"), writes=("qrot",))
                p.op("scalar", lambda e: e.activation(sq[:], qraw[:], AF.Square),
                     reads=("qraw", "sqb"), writes=("sq2", "sqb"))
                p.op("tensor", lambda e: e.matmul(
                    psS2[0:2, 0:512], honesr[:], sq[:], start=True, stop=True),
                    reads=("hones", "sq2"), writes=("psS2",))
                p.op("vector", lambda e: e.tensor_scalar(
                    normb[:], psS2[0:2, 0:512], 1e-24, None, ALU.max),
                    reads=("psS2", "nb"), writes=("nb",))
                p.op("scalar", lambda e: e.activation(normb[:], normb[:], AF.Ln),
                     reads=("nb",), writes=("nb",))
                p.op("scalar", lambda e: e.activation(normb[:], normb[:], AF.Exp, scale=f(-0.5)),
                     reads=("nb",), writes=("nb",))
                p.op("tensor", lambda e: e.matmul(
                    psS2[:, 0:512], bones[:], normb[:], start=True, stop=True),
                    reads=("bones", "nb"), writes=("psS2",))
                e1 = wst[:, 0:512]
                e2 = wst[:, 512:1024]
                p.op("vector", lambda e, e1=e1: e.tensor_tensor(e1, qraw[:], psS2[:, 0:512], ALU.mult),
                     reads=("qraw", "psS2", "wstage"), writes=("wstage",))
                p.op("vector", lambda e, e2=e2: e.tensor_tensor(e2, qrot[:], psS2[:, 0:512], ALU.mult),
                     reads=("qrot", "psS2", "wstage"), writes=("wstage",))
                p.op("vector", lambda e, e1=e1, sc0=sc0: e.tensor_scalar(
                    e1, e1, sqs[:, sc0:sc0 + 1], None, ALU.mult),
                    reads=("sqs", "wstage"), writes=("wstage",))
                p.op("vector", lambda e, e2=e2, sc0=sc0: e.tensor_scalar(
                    e2, e2, sqs[:, sc0 + 1:sc0 + 2], None, ALU.mult),
                    reads=("sqs", "wstage"), writes=("wstage",))
                p.op("vector", lambda e, e1=e1, ta=ta: e.tensor_tensor(
                    e1, e1, stab3[:, ta, :], ALU.mult),
                    reads=("tab", "wstage"), writes=("wstage",))
                p.op("vector", lambda e, e2=e2, tb_=tb_: e.tensor_tensor(
                    e2, e2, stab3[:, tb_, :], ALU.mult),
                    reads=("tab", "wstage"), writes=("wstage",))
                p.op("vector", lambda e, e1=e1, e2=e2, xu4=xu4, i=i, gp=gp: e.tensor_tensor(
                    xu4[:, gp, i, :], e1, e2, ALU.add),
                    reads=("wstage", uk), writes=(uk,))
            p.op("gpsimd", lambda e, gp=gp: e.memset(vt4[:, gp, :, :], 0.0),
                 reads=(f"vtF{gp}",), writes=(f"vt{gp}",))
            for h in range(8):
                p.op("gpsimd", lambda e, gp=gp, h=h: e.memset(
                    VOnesAP(vt, gp, h), 1.0), reads=(f"vt{gp}",), writes=(f"vt{gp}",))
            for h in range(8):
                for blk in range(4):
                    for k in range(KC):
                        sh = 0 if k < 4 else -1
                        p.op("tensor", lambda e, h=h, blk=blk, k=k, sh=sh, gp=gp: e.matmul(
                            psS[:, 0:64], x1b3[:, k, 1 + sh + blk * 128:1 + sh + (blk + 1) * 128],
                            wq4[:, k, 1024 + h * 64:1024 + (h + 1) * 64],
                            start=(k == 0), stop=(k == KC - 1)),
                            reads=("wq", "x1b"), writes=("psS",))
                    p.op("vector", lambda e, gp=gp, h=h, blk=blk: e.tensor_copy(
                        vt4[:, gp, h, blk * 65:blk * 65 + 64], psS[:, 0:64]),
                        reads=("psS", f"vt{gp}"), writes=(f"vt{gp}",))
            if g in qblocks and not L1_NO_ATTN:
                for qb in qblocks[g]:
                    for h in range(8):
                        pi = h // 2
                        base = 64 * (h % 2)
                        for kw in range(2):
                            kgp = (g - 1 + kw) % 2
                            ps_t = ps0 if kw == 0 else ps1
                            pkey = "ps0" if kw == 0 else "ps1"
                            p.op("tensor", lambda e, base=base, pi=pi, qb=qb, kgp=kgp, ps_t=ps_t, gp=gp: e.matmul(
                                ps_t[:, 0:512],
                                qu4[:, gp, pi, qb * 128:(qb + 1) * 128][base:base + 64, :],
                                ku4[:, kgp, pi, :][base:base + 64, :],
                                start=True, stop=True),
                                reads=(f"qu{gp}", f"ku{(g - 1) % 2}", f"ku{gp}", pkey + "E"), writes=(pkey,))
                        off = (3 - qb) * 128
                        mvar = smask[:, off:off + 1024]
                        p.op("vector", lambda e, mvar=mvar: e.tensor_tensor(
                            sc[:, 0:512], ps0[:, 0:512], mvar[:, 0:512], ALU.mult),
                            reads=("ps0", "amask", "scb"), writes=("scb", "ps0E"))
                        p.op("vector", lambda e, mvar=mvar: e.tensor_tensor(
                            sc[:, 512:1024], ps1[:, 0:512], mvar[:, 512:1024], ALU.mult),
                            reads=("ps1", "amask", "scb"), writes=("scb", "ps1E"))
                        if g == 2:
                            p.op("vector", lambda e: e.tensor_tensor(
                                sc[:], sc[:], smask[:, 1408:2432], ALU.mult),
                                reads=("amask", "scb"), writes=("scb",))
                        p.op("vector", lambda e: e.reduce_max(
                            rmax[:, 0:1].unsqueeze(2), sc[:].unsqueeze(1), axis=mybir.AxisListType.X),
                            reads=("scb",), writes=("rmaxb",))
                        p.op("vector", lambda e: e.tensor_scalar(
                            sc[:], sc[:], rmax[:, 0:1], None, ALU.subtract),
                            reads=("scb", "rmaxb"), writes=("scb",))
                        p.op("scalar", lambda e: e.activation(
                            probs[:], sc[:], AF.Exp, scale=f(8.0)),
                            reads=("scb", "prb"), writes=("prb",))
                        p.op("vector", lambda e, mvar=mvar: e.tensor_tensor(
                            probs[:], probs[:], mvar, ALU.mult),
                            reads=("prb", "amask"), writes=("prb",))
                        if g == 2:
                            p.op("vector", lambda e: e.tensor_tensor(
                                probs[:], probs[:], smask[:, 1408:2432], ALU.mult),
                                reads=("prb", "amask"), writes=("prb",))
                        for blk in range(8):
                            p.op("tensor", lambda e, blk=blk: e.transpose(
                                psT[:, 0:128], probs[:, blk * 128:(blk + 1) * 128], eyeb[:]),
                                reads=("prb", "eye", "psTE"), writes=("psT",))
                            p.op("vector", lambda e, blk=blk: e.tensor_copy(
                                probs[:, blk * 128:(blk + 1) * 128], psT[:, 0:128]),
                                reads=("psT", "prb"), writes=("prT", "prb", "psTE"))
                        for blk in range(8):
                            kw = blk // 4
                            kgp = (g - 1 + kw) % 2
                            p.op("tensor", lambda e, blk=blk, kgp=kgp, h=h: e.matmul(
                                psS2[0:65, 0:128], vt4[:, kgp, h, (blk % 4) * 65:(blk % 4 + 1) * 65],
                                probs[:, blk * 128:(blk + 1) * 128],
                                start=(blk == 0), stop=(blk == 7)),
                                reads=(f"vt{kgp}", "prT", "prb", "psS2E"), writes=("psS2",))
                        p.op("vector", lambda e: e.tensor_copy(sums[0:1, 0:128], psS2[64:65, 0:128]),
                             reads=("psS2", "sumsb"), writes=("sums1", "sumsb"))
                        p.op("vector", lambda e: e.tensor_copy(attnf[0:64, 0:128], sc[0:64, 0:128]),
                             reads=("scb", "attnf"), writes=("attnf",))
                        p.op("vector", lambda e: e.reciprocal(sums[0:1, 128:256], sums[0:1, 0:128]),
                             reads=("sums1", "sumsb"), writes=("sums2", "sumsb"))
                        p.op("tensor", lambda e: e.matmul(
                            ps2[0:64, 0:128],
                            onesrow[0:1, 0:64], sums[0:1, 128:256], start=True, stop=True),
                            reads=("ones", "sums2", "ps2E"), writes=("ps2",))
                        p.op("scalar", lambda e: e.copy(attnf[0:64, 0:128], psS2[0:64, 0:128]),
                             reads=("psS2", "attnf"), writes=("attnf", "psS2E"))
                        p.op("vector", lambda e, h=h: e.tensor_tensor(
                            atn[0:64, h * 128:(h + 1) * 128],
                            attnf[0:64, 0:128], ps2[0:64, 0:128], ALU.mult),
                            reads=("attnf", "ps2", "atnb"), writes=("atnb", "ps2E"))
                    for mo in range(KC):
                        for h2 in range(8):
                            p.op("tensor", lambda e, mo=mo, h2=h2: e.matmul(
                                ps2[:, 0:128],
                                wob[0:64, h2 * D + mo * 128:h2 * D + (mo + 1) * 128],
                                atn[0:64, h2 * 128:(h2 + 1) * 128],
                                start=(h2 == 0), stop=(h2 == 7)),
                                reads=("wo", "atnb", "ps2E"), writes=("ps2",))
                        if g >= 2:
                            p.op("vector", lambda e, mo=mo, qb=qb, gp=gp: e.tensor_tensor(
                                x23[:, mo, 1 + qb * 128:1 + (qb + 1) * 128], ps2[:, 0:128],
                                x13[:, mo, 1 + qb * 128:1 + (qb + 1) * 128], ALU.add),
                                reads=("ps2", "x1", "x2b"), writes=("x2b", "ps2E"))
                        else:
                            p.op("vector", lambda e, mo=mo, gp=gp: e.tensor_tensor(
                                x23[:, mo, 0:1], ps2[:, 127:128],
                                x13[:, mo, 512:513], ALU.add),
                                reads=("ps2", "x1", "x2b"), writes=("x2b", "ps2E"))
            p.op("vector", lambda e: e.tensor_copy(sq[:, 0:1], x13[:, 0, 0:1]),
                 reads=("x1", "x1b", "sqb"), writes=("x1F", "x1bF", "sqb"))
            if g >= 2 and not L1_NO_FFN:
                w = g - 2
                for kk in range(KC):
                    sh = 0 if kk < 4 else -1
                    p.op("vector", lambda e, kk=kk, sh=sh: e.tensor_copy(
                        rob3[:, kk, :], x23[:, kk, 1 + sh:513 + sh]),
                        reads=("x2b", "robb"), writes=(f"rob{kk}", "robb"))
                emit_ln(p, rob3, KC, 512, ones, onesrow, sq, stats, bcast, psS, psS2,
                        tag=f"fln{g}", out_keys=[f"rob{kk}" for kk in range(KC)])
                for fs in range(2):
                    fsl = slice(fs * FS, (fs + 1) * FS)
                    for kk in range(KC):
                        p.op("vector", lambda e, kk=kk, fsl=fsl: e.tensor_scalar(
                            lnb3[:, kk, :], rob3[:, kk, fsl], sgb[:, kk:kk + 1],
                            sgb[:, KC + kk:KC + kk + 1], ALU.mult, ALU.add),
                            reads=(f"rob{kk}", "gb", f"lnbF{kk}"), writes=(f"lnb{kk}",))
                    for j in range(NJ):
                        for hh in range(2):
                            p.dma("gpsimd", rearr(wst[:, 0:1024], "p (k c) -> p k c", k=4),
                                  rearr(w1p[512 * hh:512 * (hh + 1), j * 256:(j + 1) * 256],
                                        "(k p) c -> p k c", p=128),
                                  reads=("w1b", "wstage"), writes=("wstage1", "wstage"))
                            p.op("vector", lambda e, hh=hh: e.tensor_copy(
                                w1b[:, hh * 1024:(hh + 1) * 1024], wst[:, 0:1024]),
                                reads=("wstage1",), writes=("w1b",))
                        w1b3 = rearr(w1b[:], "p (k c) -> p k c", k=KC)
                        for k in range(KC):
                            p.op("tensor", lambda e, k=k: e.matmul(
                                psA[:, 0:FS], w1b3[:, k, 0:128], lnb3[:, k, :],
                                start=(k == 0), stop=(k == KC - 1)),
                                reads=("w1b",) + tuple(f"lnb{kk}" for kk in range(KC)),
                                writes=("psA",))
                        for k in range(KC):
                            p.op("tensor", lambda e, k=k: e.matmul(
                                psG[:, 0:FS], w1b3[:, k, 128:256], lnb3[:, k, :],
                                start=(k == 0), stop=(k == KC - 1)),
                                reads=("w1b",) + tuple(f"lnb{kk}" for kk in range(KC)),
                                writes=("psG",))
                        p.op("vector", lambda e, j=j: e.tensor_scalar(
                            gel[:], psG[:, 0:FS], sb1[:, NJ + j:NJ + j + 1], None, ALU.add),
                            reads=("psG", "gb", "gelb"), writes=("gelb",))
                        p.op("scalar", lambda e: e.activation(gel[:], gel[:], AF.Gelu),
                             reads=("gelb",), writes=("gelb",))
                        p.op("vector", lambda e, j=j: e.tensor_scalar(
                            sq[:, 0:FS], psA[:, 0:FS], sb1[:, j:j + 1], None, ALU.add),
                            reads=("psA", "gb", "sqb"), writes=("sq2", "sqb"))
                        p.op("vector", lambda e, j=j: e.tensor_tensor(
                            hb3[:, j, :], sq[:, 0:FS], gel[:], ALU.mult),
                            reads=("sq2", "gelb", "hbb"), writes=(f"hb{j}", "hbb"))
                    for mg in range(2):
                        banks = (ps0, ps1, ps2, psA)
                        bkeys = ("ps0", "ps1", "ps2", "psA")
                        for j in range(NJ):
                            p.dma("sync", w2b_stage(wst), w2p[j * 128:(j + 1) * 128, :],
                                  reads=("w2b", "wstage"), writes=("wstage2", "wstage"))
                            p.op("vector", lambda e: e.tensor_copy(w2b[:, 0:D], wst[:, 0:1024]),
                                 reads=("wstage2",), writes=("w2b",))
                            for mi in range(4):
                                mo = mg * 4 + mi
                                p.op("tensor", lambda e, mi=mi, mo=mo, j=j, banks=banks: e.matmul(
                                    banks[mi][:, 0:FS], w2b[:, mo * 128:(mo + 1) * 128],
                                    hb3[:, j, :], start=(j == 0), stop=(j == NJ - 1)),
                                    reads=("w2b", f"hb{j}", bkeys[mi] + "E"), writes=(bkeys[mi],))
                        for mi in range(4):
                            mo = mg * 4 + mi
                            p.op("vector", lambda e, mi=mi, mo=mo, banks=banks, fsl=fsl: e.tensor_scalar(
                                sq[:, 0:FS], banks[mi][:, 0:FS], sb2[:, mo:mo + 1], None, ALU.add),
                                reads=(bkeys[mi], "gb", "sqb"), writes=("sq2", "sqb", bkeys[mi] + "E"))
                            p.op("vector", lambda e, mi=mi, mo=mo, fsl=fsl, fs=fs: e.tensor_tensor(
                                rob3[:, mo, fsl], sq[:, 0:FS], x23[:, mo, 1 + fs * FS:1 + (fs + 1) * FS], ALU.add),
                                reads=("sq2", "x2b", f"rob{mo}", "robb2"),
                                writes=(f"robo{mo}{fs}", f"rob{mo}", "robb2"))
                    p.op("vector", lambda e: e.tensor_copy(gel[:, 0:1], lnb3[:, 0, 0:1]),
                         reads=tuple(f"lnb{kk}" for kk in range(KC)) + ("gelb",),
                         writes=tuple(f"lnbF{kk}" for kk in range(KC)) + ("gelb",))
                if final:
                    p.op("vector", lambda e: e.tensor_copy(x23[:, :, 0:1], x23[:, :, 512:513]),
                         reads=("x2b",), writes=("x2b",))
                    p.dma("sync", x23[:, :, 1:513],
                          rearr(r3c[:, w * 512:(w + 1) * 512], "(k p) t -> p k t", p=128),
                          reads=("x2b",), writes=("x2b", "r3cb"))
                    for kk in range(KC):
                        p.op("vector", lambda e, kk=kk: e.tensor_tensor(
                            rob3[:, kk, :], rob3[:, kk, :], x23[:, kk, 1:513], ALU.add),
                            reads=("r3cb", "x2b", f"robo{kk}0", f"robo{kk}1"), writes=(f"rob{kk}",))
                    emit_ln(p, rob3, KC, 512, ones, onesrow, sq, stats, bcast, psS, psS2,
                            tag=f"oln{g}", out_keys=[f"rob{kk}" for kk in range(KC)])
                    p.dma("sync", rearr(rout[:, w * 512:(w + 1) * 512], "(k p) t -> p k t", p=128),
                          rob3, reads=tuple(f"rob{kk}" for kk in range(KC)) + ("robb",),
                          writes=("robb",))
                else:
                    p.dma("sync", rearr(rout[:, w * 512:(w + 1) * 512], "(k p) t -> p k t", p=128),
                          rob3,
                          reads=tuple(f"robo{kk}{ff}" for kk in range(KC) for ff in range(2)) + ("robb",),
                          writes=("robb",))
                if not final:
                    p.op("vector", lambda e: e.tensor_copy(x23[:, :, 0:1], x23[:, :, 512:513]),
                         reads=("x2b",), writes=("x2b",))
        if L1_NO_FFN:
            for w2_ in range(4):
                p.dma("sync", rearr(rout[:, w2_ * 512:(w2_ + 1) * 512], "(k p) t -> p k t", p=128),
                      x23[:, :, 0:512], reads=("x2b",), writes=())
        p.emit(nc)
    st.close()
    return nc


def VOnesAP(vt, gp, h):
    from concourse.ap import AP as _AP
    rowlen = vt[:].shape[1]
    off = gp * 8 * 4 * 65 + h * 4 * 65 + 64
    return _AP(vt[:].tensor, off, [[rowlen, 128], [65, 4]])


def w2b_stage(wst):
    return wst[:, 0:1024]



def _l1_hosts():
    if "l1h" in _cache:
        return _cache["l1h"]
    DIM_HEAD, WINDOW, n = 64, 512, SEQ
    half = np.arange(0, DIM_HEAD, 2, dtype=np.float64)
    inv = 1.0 / 10000.0 ** (half / DIM_HEAD)
    t = np.arange(n, dtype=np.float64)
    freqs = t[:, None] * inv[None, :]
    freqs = np.concatenate([freqs, freqs], axis=-1)
    base = (half + 0.4 * DIM_HEAD) / (1.4 * DIM_HEAD)
    power = (t - n // 2) / float(WINDOW // 2)
    scale = base[None, :] ** power[:, None]
    scale = np.concatenate([scale, scale], axis=-1)
    cos, sin = np.cos(freqs), np.sin(freqs)
    tabAq = (cos * scale).T.astype(np.float32)
    tabBq = (sin * scale).T.astype(np.float32)
    tabAk = (cos / scale).T.astype(np.float32)
    tabBk = (sin / scale).T.astype(np.float32)
    am = np.zeros((128, 1408 + 1024), np.float32)
    kk_ = np.arange(128)[:, None]
    cc_ = np.arange(1408)[None, :]
    am[:, 0:1408] = ((cc_ >= 384 + kk_) & (cc_ <= 384 + kk_ + 512)).astype(np.float32)
    am[:, 1408:2432] = (np.arange(1024)[None, :] >= 512).astype(np.float32)
    _cache["l1h"] = (tabAq, tabBq, tabAk, tabBk, am)
    return _cache["l1h"]


def _perm_w1(w1l, b1l, w2l):
    NJ, HID = 22, 2730
    w1a, w1g = w1l[:, :HID], w1l[:, HID:]
    b1a, b1g = b1l[:HID], b1l[HID:]
    w1p = np.zeros((D, NJ * 256), np.float32)
    b1p = np.zeros((128, 2 * NJ), np.float32)
    w2p = np.zeros((NJ * 128, D), np.float32)
    for j in range(NJ):
        lo, hi = j * 128, min((j + 1) * 128, HID)
        nn = hi - lo
        w1p[:, j * 256:j * 256 + nn] = w1a[:, lo:hi]
        w1p[:, j * 256 + 128:j * 256 + 128 + nn] = w1g[:, lo:hi]
        b1p[0:nn, j] = b1a[lo:hi]
        b1p[0:nn, NJ + j] = b1g[lo:hi]
        w2p[j * 128:j * 128 + nn, :] = w2l[lo:hi, :]
    return w1p, b1p, w2p


def run_l1ffn(rinT, inp, layer, r3T=None):
    key = f"l1ffn{int(layer == 1)}"
    if key not in _cache:
        _cache[key] = mk_l1ffn(final=(layer == 1))
    nc = _cache[key]
    tabAq, tabBq, tabAk, tabBk, am = _l1_hosts()
    l = layer
    w1p, b1p, w2p = _perm_w1(inp["L1_ffn_w1"][l], inp["L1_ffn_b1"][l], inp["L1_ffn_w2"][l])
    qs = inp["L1_ffn_q_scale"][l]
    ks = inp["L1_ffn_k_scale"][l]
    qsc = np.stack([np.tile(qs, 2), np.tile(np.roll(qs, 32), 2)], axis=1).astype(np.float32)
    kscv = np.stack([np.tile(ks, 2), np.tile(np.roll(ks, 32), 2)], axis=1).astype(np.float32)
    halo = np.concatenate([np.zeros((D, 1025), np.float32), rinT], axis=1)
    eye = np.eye(128, dtype=np.float32)
    in_maps = []
    for c in range(NCORE):
        t0 = c * TPC
        s0 = (c % 4) * TPC
        chunk = np.ascontiguousarray(halo[:, t0:t0 + 1025 + TPC])
        if c % 4 == 0:
            chunk[:, 0:1025] = 0.0
        idx = np.arange(s0 - 1024, s0 + TPC)
        idx = np.clip(idx, 0, SEQ - 1)
        tabs = np.concatenate([np.tile(tt[:, idx], (2, 1))[None] for tt in
                               (tabAq, tabBq, tabAk, tabBk)], axis=0)
        tabs = np.ascontiguousarray(tabs.transpose(1, 0, 2).reshape(128, 4 * 3072))
        amc = am.copy()
        if c % 4 != 0:
            amc[:, 1408:2432] = 1.0
        d = dict(rin=chunk, wqkv=np.ascontiguousarray(inp["L1_ffn_wqkv"][l]),
                 qsc=qsc, ksc=kscv,
                 wo=np.ascontiguousarray(inp["L1_ffn_wo"][l]),
                 lng=np.ascontiguousarray(inp["L1_ffn_ln_g"][l].reshape(D, 1)),
                 lnb=np.ascontiguousarray(inp["L1_ffn_ln_b"][l].reshape(D, 1)),
                 w1p=w1p, b1p=b1p, w2p=w2p,
                 b2=np.ascontiguousarray(inp["L1_ffn_b2"][l].reshape(D, 1)),
                 tabs=tabs, amask=amc, eye=eye,
                 bonesd=np.concatenate([np.concatenate([np.ones((1, 64), np.float32), np.zeros((1, 64), np.float32)], 1),
                                        np.concatenate([np.zeros((1, 64), np.float32), np.ones((1, 64), np.float32)], 1)], 0),
                 r3c=np.ascontiguousarray((r3T if r3T is not None else rinT)[:, t0:t0 + TPC]))
        in_maps.append(d)
    res = run_bass_kernel_spmd(nc, in_maps, core_ids=list(range(NCORE)))
    return np.concatenate([r["rout"] for r in res.results], axis=1)



def mk_l0post(T=TPC, TSUB=512):
    KC = D // 128
    nsub = T // TSUB
    nc = bass.Bass("TRN2", target_bir_lowering=False, debug=False)
    xc_d = nc.dram_tensor("xc", [D, T], F32, kind="ExternalInput").ap()
    aoc = nc.dram_tensor("aoc", [D, T], BF16, kind="ExternalInput").ap()
    WO = nc.dram_tensor("WO", [D, D], F32, kind="ExternalInput").ap()
    r1 = nc.dram_tensor("r1", [D, T], F32, kind="ExternalOutput").ap()

    st = ExitStack()
    xb = st.enter_context(nc.sbuf_tensor("xb", [128, KC * TSUB], F32))
    ab = st.enter_context(nc.sbuf_tensor("ab", [128, KC * TSUB], BF16))
    wst = st.enter_context(nc.sbuf_tensor("wst", [128, 4096], F32))
    wo = st.enter_context(nc.sbuf_tensor("wo", [128, KC * D], BF16))
    rout = st.enter_context(nc.sbuf_tensor("rout", [128, KC * TSUB], F32))
    ones = st.enter_context(nc.sbuf_tensor("ones", [128, 1], F32))
    onesrow = st.enter_context(nc.sbuf_tensor("onesrow", [1, 128], F32))
    sq = st.enter_context(nc.sbuf_tensor("sq", [128, TSUB], F32))
    stats = st.enter_context(nc.sbuf_tensor("stats", [1, 4 * TSUB], F32))
    bcast = st.enter_context(nc.sbuf_tensor("bcast", [128, 2 * TSUB], F32))
    psA = st.enter_context(nc.psum_tensor("psA", [128, 512], F32))
    psS = st.enter_context(nc.psum_tensor("psS", [128, 512], F32))
    psS2 = st.enter_context(nc.psum_tensor("psS2", [128, 512], F32))
    if True:
        p = Prog()
        xb3 = rearr(xb[:], "p (k t) -> p k t", k=KC)
        ab3 = rearr(ab[:], "p (k t) -> p k t", k=KC)
        rout3 = rearr(rout[:], "p (k t) -> p k t", k=KC)
        p.op("gpsimd", lambda e: e.memset(ones[:], 1.0), writes=("ones",))
        p.op("gpsimd", lambda e: e.memset(onesrow[:], 1.0), writes=("ones",))
        emit_weight_load(p, WO, wst, wo, KC, D, "wo")
        for s in range(nsub):
            sl = slice(s * TSUB, (s + 1) * TSUB)
            p.dma("sync", xb3, rearr(xc_d[:, sl], "(k p) t -> p k t", p=128),
                  reads=("xbf",), writes=("xb",))
            p.dma("sync", ab3, rearr(aoc[:, sl], "(k p) t -> p k t", p=128),
                  reads=("abf",), writes=("ab",))
            for mo in range(KC):
                for k in range(KC):
                    p.op("tensor", lambda e, mo=mo, k=k: e.matmul(
                        psA[:, 0:TSUB], wo[:, k * D + mo * 128:k * D + (mo + 1) * 128],
                        ab3[:, k, :], start=(k == 0), stop=(k == KC - 1)),
                        reads=("wo", "ab"), writes=("psA",))
                p.op("vector", lambda e, mo=mo: e.tensor_tensor(
                    rout3[:, mo, :], psA[:, 0:TSUB], xb3[:, mo, :], ALU.add),
                    reads=("psA", "xb"), writes=(f"rout{mo}",))
            p.op("vector", lambda e: e.tensor_copy(sq[:, 0:1], xb3[:, 0, 0:1]),
                 reads=("xb", "ab", "sqbuf") + tuple(f"rout{mo}" for mo in range(KC)),
                 writes=("xbf", "abf", "sqbuf"))
            emit_ln(p, rout3, KC, TSUB, ones, onesrow, sq, stats, bcast, psS, psS2,
                    tag=f"ln{s}", out_keys=[f"rout{mo}" for mo in range(KC)])
            p.dma("sync", rearr(r1[:, sl], "(k p) t -> p k t", p=128), rout3,
                  reads=tuple(f"rout{mo}" for mo in range(KC)), writes=())
        p.emit(nc)
    st.close()
    return nc


def run_l0post(xT, ao, W_O):
    if "l0post" not in _cache:
        _cache["l0post"] = mk_l0post()
    nc = _cache["l0post"]
    in_maps = []
    for c in range(NCORE):
        sl = slice(c * TPC, (c + 1) * TPC)
        in_maps.append(dict(xc=np.ascontiguousarray(xT[:, sl]),
                            aoc=np.ascontiguousarray(ao[:, sl]),
                            WO=W_O))
    res = run_bass_kernel_spmd(nc, in_maps, core_ids=list(range(NCORE)))
    return np.concatenate([r["r1"] for r in res.results], axis=1)
